# version 12
# baseline (speedup 1.0000x reference)
import numpy as np
import concourse.bass as bass
import concourse.mybir as mybir
from contextlib import ExitStack

F32 = mybir.dt.float32
BF16 = mybir.dt.bfloat16
F8 = mybir.dt.float8e4
AF = mybir.ActivationFunctionType
ALU = mybir.AluOpType
AX = mybir.AxisListType

D, N, NC, NW = 1024, 2048, 8, 512
NH, NKV, HD = 16, 4, 64
EPS = 1e-5
HEAD_PAIRS = [(0, 4), (1, 5), (2, 6), (3, 7), (8, 12), (9, 13), (10, 14), (11, 15)]


class RecInstr:
    __slots__ = ("slot",)

    def __init__(self, slot):
        self.slot = slot

    def then_inc(self, sem, k):
        self.slot["inc"] = (sem, k)
        return self


class Rec:
    def __init__(self, name):
        self.name = name
        self.ops = []

    def __getattr__(self, meth):
        if meth.startswith("__"):
            raise AttributeError(meth)

        def call(*a, **k):
            slot = {}
            self.ops.append((meth, a, k, slot))
            return RecInstr(slot)
        return call


def _replay(rec):
    def f(eng):
        for meth, a, k, slot in rec.ops:
            r = getattr(eng, meth)(*a, **k)
            if "inc" in slot:
                r.then_inc(*slot["inc"])
    return f


class Prog:
    def __init__(self, nc):
        self.nc = nc
        self.es = ExitStack()
        self.counts = {}
        self.sems = {}
        self.sy, self.v, self.sc, self.t, self.g = (
            Rec("sync"), Rec("vector"), Rec("scalar"), Rec("tensor"), Rec("gpsimd"))

    def sb(self, name, shape, dt=F32):
        return self.es.enter_context(self.nc.sbuf_tensor(name, shape, dt))

    def push(self, name, shape, dt=F32):
        self._pushn = getattr(self, "_pushn", 0) + 1
        ctx = self.nc.sbuf_tensor(f"{name}_{self._pushn}", shape, dt)
        h = ctx.__enter__()
        self._stack = getattr(self, "_stack", [])
        self._stack.append((name, ctx))
        return h

    def pop(self, name):
        n, ctx = self._stack.pop()
        assert n == name, (n, name)
        ctx.__exit__(None, None, None)

    def ps(self, name, shape):
        return self.es.enter_context(self.nc.psum_tensor(name, shape, F32))

    def sem(self, name):
        s = self.es.enter_context(self.nc.semaphore(name))
        self.sems[name] = s
        self.counts[name] = 0

    def inc(self, instr, name, k=1):
        self.counts[name] += k
        instr.then_inc(self.sems[name], k)
        return self.counts[name]

    def wait(self, rec, name, val=None):
        rec.wait_ge(self.sems[name], self.counts[name] if val is None else val)

    def finish(self):
        with self.nc.Block() as block:
            block.sync(_replay(self.sy))
            block.vector(_replay(self.v))
            block.scalar(_replay(self.sc))
            block.tensor(_replay(self.t))
            block.gpsimd(_replay(self.g))
        self.es.close()


def build_launch(which, n_cores=8, l2_stop="full"):
    nc = bass.Bass("TRN2", target_bir_lowering=False)
    p = Prog(nc)
    sy, v, sc_, t, g = p.sy, p.v, p.sc, p.t, p.g

    xin = nc.dram_tensor("xT", [2, D, N], F32, kind="ExternalInput").ap()
    if which in (2, 3):
        wf = nc.dram_tensor("WfT", [D, D], F32, kind="ExternalInput").ap()
        fbias = nc.dram_tensor("fbias", [128, NC], F32, kind="ExternalInput").ap()
        amaxv = nc.dram_tensor("amaxv", [1, n_cores], F32, kind="ExternalInput").ap()
    if which == 2:
        wq = nc.dram_tensor("Wq", [D, D], F32, kind="ExternalInput").ap()
        wk = nc.dram_tensor("Wk", [D, NKV * HD], F32, kind="ExternalInput").ap()
        wv = nc.dram_tensor("Wv", [D, NKV * HD], F32, kind="ExternalInput").ap()
        wo = nc.dram_tensor("Wo", [D, D], F32, kind="ExternalInput").ap()
        bq = nc.dram_tensor("bq", [128, NC], F32, kind="ExternalInput").ap()
        bkv = nc.dram_tensor("bkv", [128, 4], F32, kind="ExternalInput").ap()
        bo = nc.dram_tensor("bo", [128, NC], F32, kind="ExternalInput").ap()
        eye = nc.dram_tensor("eye", [128, 128], F32, kind="ExternalInput").ap()
    if which in (1, 2):
        amax_out = nc.dram_tensor("amax_out", [1, 1], F32, kind="ExternalOutput").ap()
    if which == 2:
        xout = nc.dram_tensor("x2T", [2, D, N], F32, kind="ExternalOutput").ap()
    if which == 3:
        xout = nc.dram_tensor("xoutT", [2, D, N], F32, kind="ExternalOutput").ap()
    scr = nc.dram_tensor("scr", [1, 128], F32, kind="Internal").ap()
    scr3 = nc.dram_tensor("scr3", [3, 128], F32, kind="Internal").ap()
    if which == 2:
        xspill = nc.dram_tensor("xspill", [D, N], F32, kind="Internal").ap()

    onesf = p.sb("onesf", [128, 128], F32)
    onesb = p.sb("onesb", [128, 32], BF16)
    row128 = p.sb("row128", [1, 128], F32)
    row3 = p.sb("row3", [1, 3, 128], F32)
    srow = p.sb("srow", [1, 16], F32)
    sc0 = p.sb("sc0", [128, 8], F32)
    stat = p.sb("stat", [128, 3, NW], F32)
    amx = p.sb("amx", [128, 40], F32)
    cbias = p.sb("cbias", [128, 2], F32)
    scratch = p.sb("scratch", [128, N], F32)
    scratchB = p.sb("scratchB", [128, N], F32)

    nxbuf = 1 if which == 2 else 2
    if which != 2:
        X = [p.sb(f"X{i}", [128, NC, N], F32) for i in range(nxbuf)]

    if which in (2, 3):
        wf8 = p.sb("wf8", [128, NC, NC, 128], F8)
        fb = p.sb("fbs", [128, NC], F32)
        fbh = p.sb("fbh", [128, NC], F32)
        ybh2 = p.sb("ybh", [128, 2, 4, NW], BF16)
    if which == 2:
        vtm = p.sb("vtm", [128, 16, 256], BF16)
        eyeb = p.sb("eyeb", [128, 128], F32)
        rz4 = p.sb("rz4", [128, NW], F32)
        rzb = p.sb("rzb", [128, 2, NW], F32)
        bqs = p.sb("bqs", [128, NC], F32)
        bkvs = p.sb("bkvs", [128, 4], F32)
        bos = p.sb("bos", [128, NC], F32)
    if which == 2:
        wfs = p.push("wfs", [128, NC, NC, 128], F32)
    if which == 3:
        wfsc = p.sb("wfsc", [128, 2, NC, 128], F32)

    pbS = p.ps("pbS", [128, 4, NW])
    pbQ = p.ps("pbQ", [128, 4, NW])
    pbPV = pbQ[:, 0:2]
    pbZR = pbQ[:, 2:4]

    for s_ in ["sl", "sr", "so", "gl", "cv", "ca", "cm", "cq", "ce", "cz", "sp", "cf", "cn", "w0", "w1"]:
        p.sem(s_)

    def dma(rec, out, in_, semname):
        p.counts[semname] += 16
        rec.dma_start(out=out, in_=in_).then_inc(p.sems[semname], 16)

    def roundtrip_col_to_row(col_ap):
        p.wait(sy, "cv")
        dma(sy, scr[0:1, :], col_ap, "sr")
        p.wait(sy, "sr")
        dma(sy, row128[0:1, :], scr[0:1, :], "sr")

    def row_to_allparts(src11, dst_col):
        p.wait(v, "sr")
        v.tensor_scalar(row128[0:1, :], onesf[0:1, 0:128], src11, None, ALU.mult)
        p.inc(v.drain(), "cv")
        p.wait(sy, "cv")
        dma(sy, scr[0:1, :], row128[0:1, :], "sr")
        p.wait(sy, "sr")
        dma(sy, dst_col, scr[0:1, :], "sr")

    def stats_pass(Xs):
        cm0 = p.counts["cm"]
        cv0 = p.counts["cv"]
        bufs = [scratch, scratchB]
        for c in range(NC):
            if c >= 2:
                p.wait(v, "cm", cm0 + c - 1)
            v.tensor_mul(bufs[c % 2][:, 0:N], Xs[:, c, :], Xs[:, c, :])
            p.inc(v.drain(), "cv")
            p.wait(t, "cv", cv0 + c + 1)
            for w in range(4):
                t.matmul(pbS[32 * w:32 * w + 32, 0, :], onesf[:, 0:32],
                         Xs[:, c, w * NW:(w + 1) * NW], start=(c == 0),
                         stop=(c == NC - 1), tile_position=(0, 32 * w),
                         skip_group_check=True)
            for w in range(4):
                i = t.matmul(pbS[32 * w:32 * w + 32, 1, :], onesf[:, 0:32],
                             bufs[c % 2][:, w * NW:(w + 1) * NW], start=(c == 0),
                             stop=(c == NC - 1), tile_position=(0, 32 * w),
                             skip_group_check=True)
            p.inc(i, "cm")
        p.wait(v, "cm")
        v.tensor_scalar(stat[:, 0, :], pbS[:, 0, :], 1.0 / D, None, ALU.mult)
        v.drain()
        v.tensor_mul(scratch[:, 0:NW], stat[:, 0, :], stat[:, 0, :])
        v.drain()
        v.tensor_scalar(stat[:, 2, :], pbS[:, 1, :], 1.0 / D, None, ALU.mult)
        v.drain()
        v.tensor_sub(stat[:, 2, :], stat[:, 2, :], scratch[:, 0:NW])
        p.inc(v.drain(), "cv")
        p.wait(sc_, "cv")
        sc_.activation(stat[:, 2, :], stat[:, 2, :], AF.Ln, bias=cbias[:, 0:1])
        sc_.drain()
        p.inc(sc_.activation(stat[:, 1, :], stat[:, 2, :], AF.Exp, scale=-0.5), "ca")
        p.wait(v, "ca")
        p.wait(t, "ca")

    def apply_pass(Xs, dst_fn, row1, absmax_base=None):
        for w in range(4):
            t.matmul(pbS[:, w, :], onesf[32 * w:32 * w + 1, :],
                     stat[32 * w:32 * w + 1, 0, :], start=True, stop=True,
                     tile_position=(32 * w, 0), skip_group_check=True)
            i = t.matmul(pbQ[:, w, :], onesf[32 * w:32 * w + 1, :],
                         stat[32 * w:32 * w + 1, row1, :], start=True, stop=True,
                         tile_position=(32 * w, 0), skip_group_check=True)
        p.inc(i, "cm")
        p.wait(v, "cm")
        chunk_cv = []
        bufs = [scratch, scratchB]
        for c in range(NC):
            b_ = bufs[c % 2]
            if c >= 2:
                v.drain()
            v.tensor_sub(b_[:, 0:N], Xs[:, c, :],
                         pbS[:, :, :].rearrange("p a b -> p (a b)"))
            v.drain()
            if absmax_base is None:
                i2 = v.tensor_mul(dst_fn(c), b_[:, 0:N],
                                  pbQ[:, :, :].rearrange("p a b -> p (a b)"))
                chunk_cv.append(p.inc(i2, "cv"))
            else:
                v.tensor_mul(b_[:, 0:N], b_[:, 0:N],
                             pbQ[:, :, :].rearrange("p a b -> p (a b)"))
                v.drain()
                k = absmax_base + c
                v.tensor_reduce(amx[:, k:k + 1], b_[:, 0:N], AX.X,
                                ALU.max, apply_absolute_value=True)
        if absmax_base is not None:
            p.inc(v.nop(), "cv")
        p.wait(t, "cv")
        return chunk_cv

    def ffn_matmuls(xq_s, Xs):
        banksA = [(pbS, w) for w in range(4)]
        banksB = [(pbQ, w) for w in range(4)]
        for oc in range(NC):
            bs = banksA if oc % 2 == 0 else banksB
            if oc >= 2:
                p.wait(t, "cv")
            for kc in range(NC):
                for w in range(4):
                    i = t.matmul(bs[w][0][:, bs[w][1], :], wf8[:, kc, oc, :],
                                 xq_s[:, kc, w * NW:(w + 1) * NW],
                                 start=(kc == 0), stop=(kc == NC - 1))
            p.inc(i, "cm")
            p.wait(sc_, "cm")
            if oc >= 2:
                p.wait(sc_, "cv")
            i2 = sc_.activation(ybh2[:, oc % 2, :, :], bs[0][0][:, :, :], AF.Identity,
                                bias=fbh[:, oc:oc + 1], scale=sc0[:, 1:2])
            p.inc(i2, "ca")
            p.wait(v, "ca")
            v.tensor_add(Xs[:, oc, :], Xs[:, oc, :],
                         ybh2[:, oc % 2, :, :].rearrange("p a b -> p (a b)"))
            v.drain()
            p.inc(v.nop(), "cv")
        p.wait(t, "cv")

    if which == 2:
        dma(sy, eyeb[:], eye[:, :], "sl")
    if which != 2:
        for i_ in range(nxbuf):
            for c in range(NC):
                rec_ = sy if (i_ * NC + c) % 2 == 0 else g
                dma(rec_, X[i_][:, c, :], xin[i_, c * 128:(c + 1) * 128, :],
                    "sl" if rec_ is sy else "gl")
    if which in (2, 3):
        dma(sy, fb[:, :], fbias[:, :], "sl")
        if which == 2:
            for c in range(NC):
                dma(sy, wfs[:, c, :, :].rearrange("p a b -> p (a b)"),
                    wf[c * 128:(c + 1) * 128, :], "sl")
        dma(sy, amx[0:1, 24:24 + n_cores], amaxv[0:1, :], "sl")
    if which == 2:
        dma(sy, bqs[:, :], bq[:, :], "sl")
        dma(sy, bkvs[:, :], bkv[:, :], "sl")
        dma(sy, bos[:, :], bo[:, :], "sl")

    p.wait(v, "sl")
    if which != 2:
        p.wait(v, "gl")
        p.wait(t, "gl")
    v.memset(onesf[:], 1.0)
    v.memset(onesb[:], 1.0)
    v.memset(cbias[:, 0:1], EPS)
    p.inc(v.drain(), "cv")
    p.wait(t, "cv")
    p.wait(sc_, "cv")
    p.wait(sy, "cv")

    if which in (2, 3):
        v.tensor_scalar(fbh[:, :], fb[:, :], 0.5, None, ALU.mult)
        v.drain()
        if which == 2:
            v.tensor_reduce(amx[:, 17:18],
                            wfs[:, :, :, :].rearrange("p a b c -> p (a b c)"),
                            AX.X, ALU.max, apply_absolute_value=True)
            p.inc(v.drain(), "cv")
        else:
            cv0w = p.counts["cv"]
            for c in range(NC):
                if c >= 2:
                    p.wait(sy, "cv", cv0w + c - 1)
                sem_c = "w0" if c % 2 == 0 else "w1"
                dma(sy, wfsc[:, c % 2, :, :].rearrange("p a b -> p (a b)"),
                    wf[c * 128:(c + 1) * 128, :], sem_c)
                p.wait(v, sem_c)
                v.tensor_reduce(amx[:, 8 + c:9 + c],
                                wfsc[:, c % 2, :, :].rearrange("p a b -> p (a b)"),
                                AX.X, ALU.max, apply_absolute_value=True)
                p.inc(v.drain(), "cv")
            v.tensor_reduce(amx[:, 17:18], amx[:, 8:16], AX.X, ALU.max)
            p.inc(v.drain(), "cv")
        p.wait(sy, "cv")
        roundtrip_col_to_row(amx[:, 17:18])
        p.wait(v, "sr")
        v.tensor_reduce(srow[0:1, 0:1], row128[0:1, :], AX.X, ALU.max)
        v.drain()
        v.tensor_scalar(srow[0:1, 1:2], srow[0:1, 0:1], 1e-12, None, ALU.max)
        v.drain()
        v.reciprocal(srow[0:1, 2:3], srow[0:1, 1:2])
        v.drain()
        v.tensor_scalar(srow[0:1, 3:4], srow[0:1, 2:3], 448.0, 448.0, ALU.mult, ALU.min)
        v.drain()
        v.reciprocal(srow[0:1, 4:5], srow[0:1, 3:4])
        v.drain()
        v.tensor_scalar(srow[0:1, 5:6], srow[0:1, 3:4], 0.5, None, ALU.mult)
        v.drain()
        v.tensor_reduce(srow[0:1, 6:7], amx[0:1, 24:24 + n_cores], AX.X, ALU.max)
        v.drain()
        v.tensor_scalar(srow[0:1, 7:8], srow[0:1, 6:7], 1e-12, None, ALU.max)
        v.drain()
        v.reciprocal(srow[0:1, 8:9], srow[0:1, 7:8])
        v.drain()
        v.tensor_scalar(srow[0:1, 9:10], srow[0:1, 8:9], 448.0, 448.0, ALU.mult, ALU.min)
        v.drain()
        v.reciprocal(srow[0:1, 10:11], srow[0:1, 9:10])
        v.drain()
        v.tensor_scalar(srow[0:1, 11:12], srow[0:1, 9:10], 0.5, None, ALU.mult)
        v.drain()
        v.tensor_mul(srow[0:1, 12:13], srow[0:1, 10:11], srow[0:1, 4:5])
        v.drain()
        v.tensor_scalar(srow[0:1, 13:14], srow[0:1, 12:13], 2.0, None, ALU.mult)
        v.drain()
        p.wait(v, "sr")
        v.tensor_scalar(row3[0:1, 0, :], onesf[0:1, 0:128], srow[0:1, 5:6], None, ALU.mult)
        v.tensor_scalar(row3[0:1, 1, :], onesf[0:1, 0:128], srow[0:1, 11:12], None, ALU.mult)
        v.tensor_scalar(row3[0:1, 2, :], onesf[0:1, 0:128], srow[0:1, 13:14], None, ALU.mult)
        p.inc(v.drain(), "cv")
        p.wait(sy, "cv")
        dma(sy, scr3[:, :], row3[0:1, :, :], "sr")
        p.wait(sy, "sr")
        dma(sy, sc0[:, 4:5], scr3[0:1, :], "sr")
        dma(sy, sc0[:, 0:1], scr3[1:2, :], "sr")
        dma(sy, sc0[:, 1:2], scr3[2:3, :], "sr")
        p.wait(v, "sr")
        if which == 2:
            v.tensor_scalar(wf8[:, :, :, :].rearrange("p a b c -> p (a b c)"),
                            wfs[:, :, :, :].rearrange("p a b c -> p (a b c)"),
                            sc0[:, 4:5], None, ALU.mult)
            p.inc(v.drain(), "cq")
            p.wait(t, "cq")
            p.pop("wfs")
        else:
            cq0w = p.counts["cq"]
            for c in range(NC):
                if c >= 2:
                    p.wait(sy, "cq", cq0w + c - 1)
                sem_c = "w0" if c % 2 == 0 else "w1"
                dma(sy, wfsc[:, c % 2, :, :].rearrange("p a b -> p (a b)"),
                    wf[c * 128:(c + 1) * 128, :], sem_c)
                p.wait(v, sem_c)
                v.tensor_scalar(wf8[:, c, :, :].rearrange("p a b -> p (a b)"),
                                wfsc[:, c % 2, :, :].rearrange("p a b -> p (a b)"),
                                sc0[:, 4:5], None, ALU.mult)
                p.inc(v.drain(), "cq")
            p.wait(t, "cq")

    if which == 1:
        for s in range(2):
            stats_pass(X[s])
            apply_pass(X[s], None, 1, absmax_base=0)
            v.drain()
            v.tensor_reduce(amx[:, 20 + s:21 + s], amx[:, 0:8], AX.X, ALU.max)
            p.inc(v.drain(), "cv")
        v.tensor_reduce(amx[:, 22:23], amx[:, 20:22], AX.X, ALU.max)
        p.inc(v.drain(), "cv")
        p.wait(sy, "cv")
        roundtrip_col_to_row(amx[:, 22:23])
        p.wait(v, "sr")
        v.tensor_reduce(srow[0:1, 0:1], row128[0:1, :], AX.X, ALU.max)
        p.inc(v.drain(), "cv")
        p.wait(sy, "cv")
        dma(sy, amax_out[0:1, :], srow[0:1, 0:1], "so")
        p.wait(g, "so")
        g.nop()
        p.finish()
        return nc

    if which == 3:
        for s in range(2):
            xq_s = p.push("xq", [128, NC, N], F8)
            stats_pass(X[s])
            v.tensor_scalar(stat[:, 2, :], stat[:, 1, :], sc0[:, 0:1], None, ALU.mult)
            p.inc(v.drain(), "ca")
            p.wait(t, "ca")
            apply_pass(X[s], lambda c, xq_s=xq_s: xq_s[:, c, :], 2)
            ffn_matmuls(xq_s, X[s])
            p.pop("xq")
            p.wait(sy, "cv")
            for c in range(NC):
                dma(sy, xout[s, c * 128:(c + 1) * 128, :], X[s][:, c, :], "so")
        p.wait(g, "so")
        g.nop()
        p.finish()
        return nc

    for s in range(2):
        qT = p.push("qT", [128, NC, N], BF16)
        kT = p.push("kT", [128, 2, N], BF16)
        xnb = p.push("xnb", [128, NC, N], BF16)
        Xs = p.push("X", [128, NC, N], F32)
        if s == 1:
            p.wait(sy, "so")
            p.wait(sy, "cv")
        for c in range(NC):
            dma(sy, Xs[:, c, :], xin[s, c * 128:(c + 1) * 128, :], "sl")
        p.wait(v, "sl")
        p.wait(t, "sl")
        if s == 1:
            p.wait(v, "so")
            p.wait(t, "so")
        xq_s = p.push("xq", [128, NC, N], F8)
        stats_pass(Xs)
        v.tensor_scalar(stat[:, 2, :], stat[:, 1, :], sc0[:, 0:1], None, ALU.mult)
        p.inc(v.drain(), "ca")
        p.wait(t, "ca")
        apply_pass(Xs, lambda c: xq_s[:, c, :], 2)
        if l2_stop != "ln1":
            ffn_matmuls(xq_s, Xs)
        p.pop("xq")
        if l2_stop in ("ln1", "ffn1"):
            p.wait(sy, "cv")
            for c in range(NC):
                dma(sy, xout[s, c * 128:(c + 1) * 128, :], Xs[:, c, :], "so")
            p.pop("X")
            p.pop("xnb")
            p.pop("kT")
            p.pop("qT")
            continue
        stats_pass(Xs)
        ch_cv = apply_pass(Xs, lambda c: xnb[:, c, :], 1)
        for c in range(NC):
            rec_ = sy if c % 2 == 0 else sc_
            rec_.wait_ge(p.sems["cv"], ch_cv[c])
            dma(rec_, xspill[c * 128:(c + 1) * 128, :], Xs[:, c, :], "sp")
        p.pop("X")
        wbuf = p.push("wbuf", [128, NC, NC, 128], BF16)
        wkvb = p.push("wkvb", [128, NC, 4, 128], BF16)
        p.wait(g, "sp")
        if s == 1:
            p.wait(g, "cm")
        else:
            p.wait(g, "cq")
        for kc in range(NC):
            for oc in range(NC):
                dma(g, wbuf[:, kc, oc, :],
                    wq[kc * 128:(kc + 1) * 128, oc * 128:(oc + 1) * 128], "gl")
        for kc in range(NC):
            dma(g, wkvb[:, kc, 0:2, :].rearrange("p a b -> p (a b)"),
                wk[kc * 128:(kc + 1) * 128, :], "gl")
            dma(g, wkvb[:, kc, 2:4, :].rearrange("p a b -> p (a b)"),
                wv[kc * 128:(kc + 1) * 128, :], "gl")
        p.wait(t, "gl")

        def proj(n_oc, wsl, dstT, bias_fn, oc_base=0):
            banksA = [(pbS, w) for w in range(4)]
            banksB = [(pbQ, w) for w in range(4)]
            for oc in range(n_oc):
                bs = banksA if oc % 2 == 0 else banksB
                if oc >= 2:
                    p.wait(t, "cv")
                for kc in range(NC):
                    for w in range(4):
                        i = t.matmul(bs[w][0][:, bs[w][1], :], wsl(kc, oc),
                                     xnb[:, kc, w * NW:(w + 1) * NW],
                                     start=(kc == 0), stop=(kc == NC - 1))
                p.inc(i, "cm")
                p.wait(sc_, "cm")
                i2 = sc_.activation(dstT[:, oc_base + oc, :], bs[0][0][:, :, :],
                                    AF.Identity, bias=bias_fn(oc))
                p.inc(i2, "cv")
            p.wait(t, "cv")

        proj(NC, lambda kc, oc: wbuf[:, kc, oc, :], qT, lambda oc: bqs[:, oc:oc + 1])
        proj(2, lambda kc, oc: wkvb[:, kc, oc, :], kT, lambda oc: bkvs[:, oc:oc + 1])
        vT = p.push("vT", [128, 2, N], F32)
        proj(2, lambda kc, oc: wkvb[:, kc, 2 + oc, :], vT,
             lambda oc: bkvs[:, 2 + oc:3 + oc])

        for kvc in range(2):
            for scn in range(16):
                bk_ = scn % 2
                if kvc * 16 + scn >= 2:
                    p.wait(t, "cv")
                i = t.transpose(pbZR[0:128, bk_, 0:128],
                                vT[:, kvc, scn * 128:(scn + 1) * 128], eyeb[:, :])
                p.inc(i, "cm")
                p.wait(v, "cm")
                v.tensor_copy(vtm[:, scn, kvc * 128:(kvc + 1) * 128],
                              pbZR[:, bk_, 0:128])
                v.drain()
                p.inc(v.nop(), "cv")
        p.wait(t, "cv")
        p.pop("vT")

        for c in range(NC):
            if c > 0:
                p.wait(v, "cm")
            v.tensor_mul(scratch[:, 0:N], qT[:, c, :], qT[:, c, :])
            p.inc(v.drain(), "cv")
            p.wait(t, "cv")
            slot = c % 4
            tgt = [(pbS, w) for w in range(4)]
            for w in range(4):
                i = t.matmul(tgt[w][0][32 * slot:32 * slot + 32, tgt[w][1], :],
                             onesf[:, 0:32], scratch[:, w * NW:(w + 1) * NW],
                             start=(c < 4), stop=(c >= 4),
                             tile_position=(0, 32 * slot), skip_group_check=True)
            p.inc(i, "cm")
        p.wait(v, "cm")
        v.tensor_reduce(amx[:, 0:1], pbS[:, :, :].rearrange("p a b -> p (a b)"),
                        AX.X, ALU.max)
        p.inc(v.drain(), "cv")
        p.wait(t, "cv")
        for c in range(2):
            if c > 0:
                p.wait(v, "cm")
            v.tensor_mul(scratch[:, 0:N], kT[:, c, :], kT[:, c, :])
            p.inc(v.drain(), "cv")
            p.wait(t, "cv")
            for w in range(4):
                i = t.matmul(pbPV[32 * (c * 2 + w // 2):32 * (c * 2 + w // 2) + 32,
                                  w % 2, :],
                             onesf[:, 0:32], scratch[:, w * NW:(w + 1) * NW],
                             start=True, stop=True,
                             tile_position=(0, 32 * (c * 2 + w // 2)),
                             skip_group_check=True)
            p.inc(i, "cm")
        p.wait(v, "cm")
        v.tensor_reduce(amx[:, 1:2], pbPV[:, :, :].rearrange("p a b -> p (a b)"),
                        AX.X, ALU.max)
        p.inc(v.drain(), "cv")
        p.wait(sy, "cv")
        roundtrip_col_to_row(amx[:, 0:1])
        p.wait(v, "sr")
        v.tensor_reduce(srow[0:1, 14:15], row128[0:1, :], AX.X, ALU.max)
        p.inc(v.drain(), "cv")
        p.wait(sy, "cv")
        roundtrip_col_to_row(amx[:, 1:2])
        p.wait(v, "sr")
        v.tensor_reduce(srow[0:1, 15:16], row128[0:1, :], AX.X, ALU.max)
        v.drain()
        v.tensor_mul(srow[0:1, 14:15], srow[0:1, 14:15], srow[0:1, 15:16])
        p.inc(v.drain(), "cv")
        p.wait(sc_, "cv")
        sc_.activation(srow[0:1, 14:15], srow[0:1, 14:15], AF.Ln)
        sc_.drain()
        p.inc(sc_.activation(srow[0:1, 14:15], srow[0:1, 14:15], AF.Exp, scale=0.5),
              "ca")
        p.wait(v, "ca")
        v.tensor_scalar(srow[0:1, 14:15], srow[0:1, 14:15], -0.125, None, ALU.mult)
        p.inc(v.drain(), "cv")
        row_to_allparts(srow[0:1, 14:15], sc0[:, 2 + s:3 + s])
        p.wait(sc_, "sr")
        p.wait(t, "sr")

        p.pop("wkvb")
        p.pop("wbuf")
        p.pop("xnb")
        aoT = p.push("aoT", [128, NC, N], BF16)
        expb = p.push("expb", [128, 4, 16, NW], BF16)
        p.wait(sc_, "sp")
        p.wait(v, "sp")
        p.wait(t, "sp")

        for b in range(4):
            qc0, qc1 = 2 * b, 2 * b + 1
            ktc = qc0 // 4
            kvs = [HEAD_PAIRS[qc0][0] // 4, HEAD_PAIRS[qc0][1] // 4,
                   HEAD_PAIRS[qc1][0] // 4, HEAD_PAIRS[qc1][1] // 4]
            for w in range(4):
                nwin = slice(w * NW, (w + 1) * NW)
                ce0 = p.counts["ce"]
                cf0 = p.counts["cf"]

                def qk_group(scn, js, sem, base):
                    s0 = scn * 128
                    if scn >= 1:
                        p.wait(t, sem, base + scn)
                    for j, qc in js:
                        lohalf = (j % 2) * 64
                        i = t.matmul(pbS[0:128, j, :],
                                     kT[lohalf:lohalf + 64, ktc, s0:s0 + 128],
                                     qT[lohalf:lohalf + 64, qc, nwin],
                                     start=True, stop=True,
                                     tile_position=(lohalf, 0),
                                     skip_group_check=True)
                    p.inc(i, "cm")

                def exp_group(scn, j0, sem):
                    p.wait(sc_, "cm")
                    i = sc_.activation(expb[:, j0:j0 + 2, scn, :],
                                       pbS[:, j0:j0 + 2, :], AF.Exp,
                                       bias=sc0[:, 2 + s:3 + s], scale=0.125)
                    p.inc(i, sem)

                def pv_z_group(scn, jp, last):
                    for j, pvb in jp:
                        col = (j % 2) * 64
                        i = t.matmul(pbQ[col:col + 64, pvb, :],
                                     vtm[:, scn, kvs[j] * 64:kvs[j] * 64 + 64],
                                     expb[:, j, scn, :], start=(scn == 0),
                                     stop=(scn == 15),
                                     tile_position=(0, col), skip_group_check=True)
                    for j, _ in jp:
                        i2 = t.matmul(pbQ[32 * j:32 * j + 32, 2, :], onesb[:, :],
                                      expb[:, j, scn, :], start=(scn == 0),
                                      stop=(scn == 15),
                                      tile_position=(0, 32 * j),
                                      skip_group_check=True)
                    if last:
                        p.inc(i, "cm")
                        p.inc(i2, "cz")

                for scn in range(16):
                    qk_group(scn, ((0, qc0), (1, qc0)), "ce", ce0)
                    exp_group(scn, 0, "ce")
                    qk_group(scn, ((2, qc1), (3, qc1)), "cf", cf0)
                    exp_group(scn, 2, "cf")
                    if scn >= 1:
                        pv_z_group(scn - 1, ((0, 0), (1, 0)), False)
                        pv_z_group(scn - 1, ((2, 1), (3, 1)), False)
                p.wait(t, "ce", ce0 + 16)
                pv_z_group(15, ((0, 0), (1, 0)), False)
                p.wait(t, "cf", cf0 + 16)
                pv_z_group(15, ((2, 1), (3, 1)), True)
                p.wait(v, "cz")
                v.reciprocal(rz4[:, :], pbZR[:, 0, :])
                p.inc(v.drain(), "cz")
                p.wait(t, "cz")
                t.matmul(pbZR[0:64, 1, :], onesf[0:1, 0:64], rz4[0:1, :],
                         start=True, stop=True, tile_position=(0, 0),
                         skip_group_check=True)
                t.matmul(pbZR[64:128, 1, :], onesf[32:33, 0:64], rz4[32:33, :],
                         start=True, stop=True, tile_position=(32, 64),
                         skip_group_check=True)
                t.matmul(pbZR[0:64, 0, :], onesf[64:65, 0:64], rz4[64:65, :],
                         start=True, stop=True, tile_position=(64, 0),
                         skip_group_check=True)
                i = t.matmul(pbZR[64:128, 0, :], onesf[96:97, 0:64], rz4[96:97, :],
                             start=True, stop=True, tile_position=(96, 64),
                             skip_group_check=True)
                p.inc(i, "cm")
                p.wait(v, "cm")
                v.tensor_copy(rzb[:, 0, :], pbZR[:, 1, :])
                v.drain()
                v.tensor_copy(rzb[:, 1, :], pbZR[:, 0, :])
                v.drain()
                v.tensor_mul(aoT[:, qc0, nwin], pbPV[:, 0, :], rzb[:, 0, :])
                v.drain()
                v.tensor_mul(aoT[:, qc1, nwin], pbPV[:, 1, :], rzb[:, 1, :])
                v.drain()
                p.inc(v.nop(), "ce")

        p.pop("expb")
        Xs = p.push("X", [128, NC, N], F32)
        p.wait(sy, "ce")
        for c in range(NC):
            dma(sy, Xs[:, c, :], xspill[c * 128:(c + 1) * 128, :], "sl")
        wbuf = p.push("wbuf", [128, NC, NC, 128], BF16)
        p.wait(g, "ce")
        for kc in range(NC):
            for oc in range(NC):
                dma(g, wbuf[:, kc, oc, :],
                    wo[kc * 128:(kc + 1) * 128, oc * 128:(oc + 1) * 128], "gl")
        p.wait(t, "gl")
        banksA = [(pbS, w) for w in range(4)]
        banksB = [(pbQ, w) for w in range(4)]
        for oc in range(NC):
            bs = banksA if oc % 2 == 0 else banksB
            if oc >= 2:
                p.wait(t, "cv")
            for kc in range(NC):
                for w in range(4):
                    i = t.matmul(bs[w][0][:, bs[w][1], :], wbuf[:, kc, oc, :],
                                 aoT[:, kc, w * NW:(w + 1) * NW],
                                 start=(kc == 0), stop=(kc == NC - 1))
            p.inc(i, "cm")
            p.wait(v, "cm")
            if oc == 0:
                p.wait(v, "sl")
            v.scalar_tensor_tensor(Xs[:, oc, :], bs[0][0][:, :, :],
                                   bos[:, oc:oc + 1], Xs[:, oc, :],
                                   ALU.add, ALU.add)
            v.drain()
            p.inc(v.nop(), "cv")
        p.wait(t, "cv")
        p.pop("wbuf")

        p.wait(sy, "cv")
        for c in range(NC):
            dma(sy, xout[s, c * 128:(c + 1) * 128, :], Xs[:, c, :], "so")
        stats_pass(Xs)
        apply_pass(Xs, None, 1, absmax_base=0)
        v.drain()
        v.tensor_reduce(amx[:, 20 + s:21 + s], amx[:, 0:8], AX.X, ALU.max)
        p.inc(v.drain(), "cv")
        p.pop("X")
        p.pop("aoT")
        p.pop("kT")
        p.pop("qT")

    if l2_stop == "full":
        v.tensor_reduce(amx[:, 22:23], amx[:, 20:22], AX.X, ALU.max)
        p.inc(v.drain(), "cv")
        p.wait(sy, "cv")
        roundtrip_col_to_row(amx[:, 22:23])
        p.wait(v, "sr")
        v.tensor_reduce(srow[0:1, 0:1], row128[0:1, :], AX.X, ALU.max)
        p.inc(v.drain(), "cv")
        p.wait(sy, "cv")
        dma(sy, amax_out[0:1, :], srow[0:1, 0:1], "so")
    p.wait(g, "so")
    g.nop()
    p.finish()
    return nc


def host_prepare(inputs, n_cores=8):
    import ml_dtypes
    x = np.asarray(inputs["L0_attn_x"])
    Wq = np.asarray(inputs["L0_attn_Wq"])
    bq = np.asarray(inputs["L0_attn_bq"])
    Wk = np.asarray(inputs["L0_attn_Wk"])
    bk = np.asarray(inputs["L0_attn_bk"])
    Wv = np.asarray(inputs["L0_attn_Wv"])
    bv = np.asarray(inputs["L0_attn_bv"])
    Wo = np.asarray(inputs["L0_attn_Wo"])
    bo = np.asarray(inputs["L0_attn_bo"])
    Wf = np.asarray(inputs["L0_ffn_weight"])
    bf_ = np.asarray(inputs["L0_ffn_bias"])

    headperm = [h for pr in HEAD_PAIRS for h in pr]
    colperm = np.concatenate([np.arange(h * HD, (h + 1) * HD) for h in headperm])
    Wq_p = np.ascontiguousarray(Wq[:, colperm])
    bq_p = np.ascontiguousarray(bq[colperm]).reshape(NC, 128).T.copy()
    myrows = []
    for h in headperm:
        g, kv = h % 4, h // 4
        base = (g * 4 + kv) * HD
        myrows.extend(range(base, base + HD))
    Wo_p = np.ascontiguousarray(Wo[myrows, :])
    bkv_host = np.stack([bk.reshape(2, 128).T, bv.reshape(2, 128).T],
                        axis=1).reshape(128, 4)
    bkv_host = np.concatenate([bk.reshape(2, 128).T, bv.reshape(2, 128).T], axis=1)
    bo_host = bo.reshape(NC, 128).T.copy()
    fb_host = bf_.reshape(NC, 128).T.copy()
    WfT = np.ascontiguousarray(Wf.T)
    eye = np.eye(128, dtype=np.float32)

    per_core = []
    for c in range(n_cores):
        xc = x[2 * c:2 * c + 2]
        xT = np.ascontiguousarray(xc.transpose(0, 2, 1))
        per_core.append({
            "xT": xT, "Wq": Wq_p, "Wk": Wk, "Wv": Wv, "Wo": Wo_p,
            "bq": np.ascontiguousarray(bq_p), "bkv": np.ascontiguousarray(bkv_host),
            "bo": np.ascontiguousarray(bo_host), "WfT": WfT,
            "fbias": np.ascontiguousarray(fb_host), "eye": eye,
        })
    return per_core


def host_finish(results):
    outs = []
    for r in results:
        xT = r["xoutT"]
        outs.append(np.ascontiguousarray(xT.transpose(0, 2, 1)))
    return np.concatenate(outs, axis=0).astype(np.float32)

from concourse.bass_utils import run_bass_kernel_spmd

def kernel(**inputs):
    n_cores = 8
    percore = host_prepare(inputs, n_cores)
    core_ids = list(range(n_cores))

    nc1 = build_launch(1, n_cores)
    res1 = run_bass_kernel_spmd(nc1, [{"xT": pc["xT"]} for pc in percore],
                                core_ids=core_ids)
    amax1 = np.concatenate([res1.results[i]["amax_out"] for i in range(n_cores)],
                           axis=1)

    nc2 = build_launch(2, n_cores)
    in2 = [dict(pc, amaxv=amax1) for pc in percore]
    res2 = run_bass_kernel_spmd(nc2, in2, core_ids=core_ids)
    amax3 = np.concatenate([res2.results[i]["amax_out"] for i in range(n_cores)],
                           axis=1)

    nc3 = build_launch(3, n_cores)
    in3 = [{"xT": res2.results[i]["x2T"], "WfT": pc["WfT"],
            "fbias": pc["fbias"], "amaxv": amax3}
           for i, pc in enumerate(percore)]
    res3 = run_bass_kernel_spmd(nc3, in3, core_ids=core_ids)
    return host_finish([res3.results[i] for i in range(n_cores)])


# version 13
# speedup vs baseline: 1.0279x; 1.0279x over previous
import numpy as np
import concourse.bass as bass
import concourse.mybir as mybir
from contextlib import ExitStack

F32 = mybir.dt.float32
BF16 = mybir.dt.bfloat16
F8 = mybir.dt.float8e4
AF = mybir.ActivationFunctionType
ALU = mybir.AluOpType
AX = mybir.AxisListType

D, N, NC, NW = 1024, 2048, 8, 512
NH, NKV, HD = 16, 4, 64
EPS = 1e-5
HEAD_PAIRS = [(0, 4), (1, 5), (2, 6), (3, 7), (8, 12), (9, 13), (10, 14), (11, 15)]


class RecInstr:
    __slots__ = ("slot",)

    def __init__(self, slot):
        self.slot = slot

    def then_inc(self, sem, k):
        self.slot["inc"] = (sem, k)
        return self


class Rec:
    def __init__(self, name):
        self.name = name
        self.ops = []

    def __getattr__(self, meth):
        if meth.startswith("__"):
            raise AttributeError(meth)

        def call(*a, **k):
            slot = {}
            self.ops.append((meth, a, k, slot))
            return RecInstr(slot)
        return call


def _replay(rec):
    def f(eng):
        for meth, a, k, slot in rec.ops:
            r = getattr(eng, meth)(*a, **k)
            if "inc" in slot:
                r.then_inc(*slot["inc"])
    return f


class Prog:
    def __init__(self, nc):
        self.nc = nc
        self.es = ExitStack()
        self.counts = {}
        self.sems = {}
        self.sy, self.v, self.sc, self.t, self.g = (
            Rec("sync"), Rec("vector"), Rec("scalar"), Rec("tensor"), Rec("gpsimd"))

    def sb(self, name, shape, dt=F32):
        return self.es.enter_context(self.nc.sbuf_tensor(name, shape, dt))

    def push(self, name, shape, dt=F32):
        self._pushn = getattr(self, "_pushn", 0) + 1
        ctx = self.nc.sbuf_tensor(f"{name}_{self._pushn}", shape, dt)
        h = ctx.__enter__()
        self._stack = getattr(self, "_stack", [])
        self._stack.append((name, ctx))
        return h

    def pop(self, name):
        n, ctx = self._stack.pop()
        assert n == name, (n, name)
        ctx.__exit__(None, None, None)

    def ps(self, name, shape):
        return self.es.enter_context(self.nc.psum_tensor(name, shape, F32))

    def sem(self, name):
        s = self.es.enter_context(self.nc.semaphore(name))
        self.sems[name] = s
        self.counts[name] = 0

    def inc(self, instr, name, k=1):
        self.counts[name] += k
        instr.then_inc(self.sems[name], k)
        return self.counts[name]

    def wait(self, rec, name, val=None):
        rec.wait_ge(self.sems[name], self.counts[name] if val is None else val)

    def finish(self):
        with self.nc.Block() as block:
            block.sync(_replay(self.sy))
            block.vector(_replay(self.v))
            block.scalar(_replay(self.sc))
            block.tensor(_replay(self.t))
            block.gpsimd(_replay(self.g))
        self.es.close()


def build_launch(which, n_cores=8, l2_stop="full"):
    nc = bass.Bass("TRN2", target_bir_lowering=False)
    p = Prog(nc)
    sy, v, sc_, t, g = p.sy, p.v, p.sc, p.t, p.g

    xin = nc.dram_tensor("xT", [2, D, N], F32, kind="ExternalInput").ap()
    if which in (2, 3):
        wf = nc.dram_tensor("WfT", [D, D], F32, kind="ExternalInput").ap()
        fbias = nc.dram_tensor("fbias", [128, NC], F32, kind="ExternalInput").ap()
        amaxv = nc.dram_tensor("amaxv", [1, n_cores], F32, kind="ExternalInput").ap()
    if which == 2:
        wq = nc.dram_tensor("Wq", [D, D], F32, kind="ExternalInput").ap()
        wk = nc.dram_tensor("Wk", [D, NKV * HD], F32, kind="ExternalInput").ap()
        wv = nc.dram_tensor("Wv", [D, NKV * HD], F32, kind="ExternalInput").ap()
        wo = nc.dram_tensor("Wo", [D, D], F32, kind="ExternalInput").ap()
        bq = nc.dram_tensor("bq", [128, NC], F32, kind="ExternalInput").ap()
        bkv = nc.dram_tensor("bkv", [128, 4], F32, kind="ExternalInput").ap()
        bo = nc.dram_tensor("bo", [128, NC], F32, kind="ExternalInput").ap()
        eye = nc.dram_tensor("eye", [128, 128], F32, kind="ExternalInput").ap()
    if which in (1, 2):
        amax_out = nc.dram_tensor("amax_out", [1, 1], F32, kind="ExternalOutput").ap()
    if which == 1:
        stats1o = nc.dram_tensor("stats1", [2, 128, 2 * NW], F32,
                                 kind="ExternalOutput").ap()
    if which == 2:
        stats1i = nc.dram_tensor("stats1", [2, 128, 2 * NW], F32,
                                 kind="ExternalInput").ap()
        stats3o = nc.dram_tensor("stats3", [2, 128, 2 * NW], F32,
                                 kind="ExternalOutput").ap()
    if which == 3:
        stats3i = nc.dram_tensor("stats3", [2, 128, 2 * NW], F32,
                                 kind="ExternalInput").ap()
    if which == 2:
        xout = nc.dram_tensor("x2T", [2, D, N], F32, kind="ExternalOutput").ap()
    if which == 3:
        xout = nc.dram_tensor("xoutT", [2, D, N], F32, kind="ExternalOutput").ap()
    scr = nc.dram_tensor("scr", [1, 128], F32, kind="Internal").ap()
    scr3 = nc.dram_tensor("scr3", [3, 128], F32, kind="Internal").ap()
    if which == 2:
        xspill = nc.dram_tensor("xspill", [D, N], F32, kind="Internal").ap()

    onesf = p.sb("onesf", [128, 128], F32)
    onesb = p.sb("onesb", [128, 32], BF16)
    row128 = p.sb("row128", [1, 128], F32)
    row3 = p.sb("row3", [1, 3, 128], F32)
    srow = p.sb("srow", [1, 16], F32)
    sc0 = p.sb("sc0", [128, 8], F32)
    stat = p.sb("stat", [128, 3, NW], F32)
    amx = p.sb("amx", [128, 40], F32)
    cbias = p.sb("cbias", [128, 2], F32)
    scratch = p.sb("scratch", [128, N], F32)
    scratchB = p.sb("scratchB", [128, N], F32)

    nxbuf = 1 if which == 2 else 2
    if which != 2:
        X = [p.sb(f"X{i}", [128, NC, N], F32) for i in range(nxbuf)]

    if which in (2, 3):
        wf8 = p.sb("wf8", [128, NC, NC, 128], F8)
        fb = p.sb("fbs", [128, NC], F32)
        fbh = p.sb("fbh", [128, NC], F32)
        ybh2 = p.sb("ybh", [128, 2, 4, NW], BF16)
    if which == 2:
        vtm = p.sb("vtm", [128, 16, 256], BF16)
        eyeb = p.sb("eyeb", [128, 128], F32)
        rz4 = p.sb("rz4", [128, NW], F32)
        rzb = p.sb("rzb", [128, 2, NW], F32)
        bqs = p.sb("bqs", [128, NC], F32)
        bkvs = p.sb("bkvs", [128, 4], F32)
        bos = p.sb("bos", [128, NC], F32)
    if which == 2:
        wfs = p.push("wfs", [128, NC, NC, 128], F32)
    if which == 3:
        wfsc = p.sb("wfsc", [128, 2, NC, 128], F32)

    pbS = p.ps("pbS", [128, 4, NW])
    pbQ = p.ps("pbQ", [128, 4, NW])
    pbPV = pbQ[:, 0:2]
    pbZR = pbQ[:, 2:4]

    for s_ in ["sl", "sr", "so", "gl", "cv", "ca", "cm", "cq", "ce", "cz", "sp", "cf", "cn", "w0", "w1"]:
        p.sem(s_)

    def dma(rec, out, in_, semname):
        p.counts[semname] += 16
        rec.dma_start(out=out, in_=in_).then_inc(p.sems[semname], 16)

    def roundtrip_col_to_row(col_ap):
        p.wait(sy, "cv")
        dma(sy, scr[0:1, :], col_ap, "sr")
        p.wait(sy, "sr")
        dma(sy, row128[0:1, :], scr[0:1, :], "sr")

    def row_to_allparts(src11, dst_col):
        p.wait(v, "sr")
        v.tensor_scalar(row128[0:1, :], onesf[0:1, 0:128], src11, None, ALU.mult)
        p.inc(v.drain(), "cv")
        p.wait(sy, "cv")
        dma(sy, scr[0:1, :], row128[0:1, :], "sr")
        p.wait(sy, "sr")
        dma(sy, dst_col, scr[0:1, :], "sr")

    def stats_pass(Xs):
        cm0 = p.counts["cm"]
        cv0 = p.counts["cv"]
        bufs = [scratch, scratchB]
        for c in range(NC):
            if c >= 2:
                p.wait(v, "cm", cm0 + c - 1)
            v.tensor_mul(bufs[c % 2][:, 0:N], Xs[:, c, :], Xs[:, c, :])
            p.inc(v.drain(), "cv")
            p.wait(t, "cv", cv0 + c + 1)
            for w in range(4):
                t.matmul(pbS[32 * w:32 * w + 32, 0, :], onesf[:, 0:32],
                         Xs[:, c, w * NW:(w + 1) * NW], start=(c == 0),
                         stop=(c == NC - 1), tile_position=(0, 32 * w),
                         skip_group_check=True)
            for w in range(4):
                i = t.matmul(pbS[32 * w:32 * w + 32, 1, :], onesf[:, 0:32],
                             bufs[c % 2][:, w * NW:(w + 1) * NW], start=(c == 0),
                             stop=(c == NC - 1), tile_position=(0, 32 * w),
                             skip_group_check=True)
            p.inc(i, "cm")
        p.wait(v, "cm")
        v.tensor_scalar(stat[:, 0, :], pbS[:, 0, :], 1.0 / D, None, ALU.mult)
        v.drain()
        v.tensor_mul(scratch[:, 0:NW], stat[:, 0, :], stat[:, 0, :])
        v.drain()
        v.tensor_scalar(stat[:, 2, :], pbS[:, 1, :], 1.0 / D, None, ALU.mult)
        v.drain()
        v.tensor_sub(stat[:, 2, :], stat[:, 2, :], scratch[:, 0:NW])
        p.inc(v.drain(), "cv")
        p.wait(sc_, "cv")
        sc_.activation(stat[:, 2, :], stat[:, 2, :], AF.Ln, bias=cbias[:, 0:1])
        sc_.drain()
        p.inc(sc_.activation(stat[:, 1, :], stat[:, 2, :], AF.Exp, scale=-0.5), "ca")
        p.wait(v, "ca")
        p.wait(t, "ca")

    def apply_pass(Xs, dst_fn, row1, absmax_base=None):
        for w in range(4):
            t.matmul(pbS[:, w, :], onesf[32 * w:32 * w + 1, :],
                     stat[32 * w:32 * w + 1, 0, :], start=True, stop=True,
                     tile_position=(32 * w, 0), skip_group_check=True)
            i = t.matmul(pbQ[:, w, :], onesf[32 * w:32 * w + 1, :],
                         stat[32 * w:32 * w + 1, row1, :], start=True, stop=True,
                         tile_position=(32 * w, 0), skip_group_check=True)
        p.inc(i, "cm")
        p.wait(v, "cm")
        chunk_cv = []
        bufs = [scratch, scratchB]
        for c in range(NC):
            b_ = bufs[c % 2]
            if c >= 2:
                v.drain()
            v.tensor_sub(b_[:, 0:N], Xs[:, c, :],
                         pbS[:, :, :].rearrange("p a b -> p (a b)"))
            v.drain()
            if absmax_base is None:
                i2 = v.tensor_mul(dst_fn(c), b_[:, 0:N],
                                  pbQ[:, :, :].rearrange("p a b -> p (a b)"))
                chunk_cv.append(p.inc(i2, "cv"))
            else:
                v.tensor_mul(b_[:, 0:N], b_[:, 0:N],
                             pbQ[:, :, :].rearrange("p a b -> p (a b)"))
                v.drain()
                k = absmax_base + c
                v.tensor_reduce(amx[:, k:k + 1], b_[:, 0:N], AX.X,
                                ALU.max, apply_absolute_value=True)
        if absmax_base is not None:
            p.inc(v.nop(), "cv")
        p.wait(t, "cv")
        return chunk_cv

    def ffn_matmuls(xq_s, Xs):
        banksA = [(pbS, w) for w in range(4)]
        banksB = [(pbQ, w) for w in range(4)]
        for oc in range(NC):
            bs = banksA if oc % 2 == 0 else banksB
            if oc >= 2:
                p.wait(t, "cv")
            for kc in range(NC):
                for w in range(4):
                    i = t.matmul(bs[w][0][:, bs[w][1], :], wf8[:, kc, oc, :],
                                 xq_s[:, kc, w * NW:(w + 1) * NW],
                                 start=(kc == 0), stop=(kc == NC - 1))
            p.inc(i, "cm")
            p.wait(sc_, "cm")
            if oc >= 2:
                p.wait(sc_, "cv")
            i2 = sc_.activation(ybh2[:, oc % 2, :, :], bs[0][0][:, :, :], AF.Identity,
                                bias=fbh[:, oc:oc + 1], scale=sc0[:, 1:2])
            p.inc(i2, "ca")
            p.wait(v, "ca")
            v.tensor_add(Xs[:, oc, :], Xs[:, oc, :],
                         ybh2[:, oc % 2, :, :].rearrange("p a b -> p (a b)"))
            v.drain()
            p.inc(v.nop(), "cv")
        p.wait(t, "cv")

    if which == 2:
        dma(sy, eyeb[:], eye[:, :], "sl")
    if which != 2:
        for i_ in range(nxbuf):
            for c in range(NC):
                rec_ = sy if (i_ * NC + c) % 2 == 0 else g
                dma(rec_, X[i_][:, c, :], xin[i_, c * 128:(c + 1) * 128, :],
                    "sl" if rec_ is sy else "gl")
    if which in (2, 3):
        dma(sy, fb[:, :], fbias[:, :], "sl")
        if which == 2:
            for c in range(NC):
                dma(sy, wfs[:, c, :, :].rearrange("p a b -> p (a b)"),
                    wf[c * 128:(c + 1) * 128, :], "sl")
        dma(sy, amx[0:1, 24:24 + n_cores], amaxv[0:1, :], "sl")
    if which == 2:
        dma(sy, bqs[:, :], bq[:, :], "sl")
        dma(sy, bkvs[:, :], bkv[:, :], "sl")
        dma(sy, bos[:, :], bo[:, :], "sl")

    p.wait(v, "sl")
    if which != 2:
        p.wait(v, "gl")
        p.wait(t, "gl")
    v.memset(onesf[:], 1.0)
    v.memset(onesb[:], 1.0)
    v.memset(cbias[:, 0:1], EPS)
    p.inc(v.drain(), "cv")
    p.wait(t, "cv")
    p.wait(sc_, "cv")
    p.wait(sy, "cv")

    if which in (2, 3):
        v.tensor_scalar(fbh[:, :], fb[:, :], 0.5, None, ALU.mult)
        v.drain()
        if which == 2:
            v.tensor_reduce(amx[:, 17:18],
                            wfs[:, :, :, :].rearrange("p a b c -> p (a b c)"),
                            AX.X, ALU.max, apply_absolute_value=True)
            p.inc(v.drain(), "cv")
        else:
            cv0w = p.counts["cv"]
            for c in range(NC):
                if c >= 2:
                    p.wait(sy, "cv", cv0w + c - 1)
                sem_c = "w0" if c % 2 == 0 else "w1"
                dma(sy, wfsc[:, c % 2, :, :].rearrange("p a b -> p (a b)"),
                    wf[c * 128:(c + 1) * 128, :], sem_c)
                p.wait(v, sem_c)
                v.tensor_reduce(amx[:, 8 + c:9 + c],
                                wfsc[:, c % 2, :, :].rearrange("p a b -> p (a b)"),
                                AX.X, ALU.max, apply_absolute_value=True)
                p.inc(v.drain(), "cv")
            v.tensor_reduce(amx[:, 17:18], amx[:, 8:16], AX.X, ALU.max)
            p.inc(v.drain(), "cv")
        p.wait(sy, "cv")
        roundtrip_col_to_row(amx[:, 17:18])
        p.wait(v, "sr")
        v.tensor_reduce(srow[0:1, 0:1], row128[0:1, :], AX.X, ALU.max)
        v.drain()
        v.tensor_scalar(srow[0:1, 1:2], srow[0:1, 0:1], 1e-12, None, ALU.max)
        v.drain()
        v.reciprocal(srow[0:1, 2:3], srow[0:1, 1:2])
        v.drain()
        v.tensor_scalar(srow[0:1, 3:4], srow[0:1, 2:3], 448.0, 448.0, ALU.mult, ALU.min)
        v.drain()
        v.reciprocal(srow[0:1, 4:5], srow[0:1, 3:4])
        v.drain()
        v.tensor_scalar(srow[0:1, 5:6], srow[0:1, 3:4], 0.5, None, ALU.mult)
        v.drain()
        v.tensor_reduce(srow[0:1, 6:7], amx[0:1, 24:24 + n_cores], AX.X, ALU.max)
        v.drain()
        v.tensor_scalar(srow[0:1, 7:8], srow[0:1, 6:7], 1e-12, None, ALU.max)
        v.drain()
        v.reciprocal(srow[0:1, 8:9], srow[0:1, 7:8])
        v.drain()
        v.tensor_scalar(srow[0:1, 9:10], srow[0:1, 8:9], 448.0, 448.0, ALU.mult, ALU.min)
        v.drain()
        v.reciprocal(srow[0:1, 10:11], srow[0:1, 9:10])
        v.drain()
        v.tensor_scalar(srow[0:1, 11:12], srow[0:1, 9:10], 0.5, None, ALU.mult)
        v.drain()
        v.tensor_mul(srow[0:1, 12:13], srow[0:1, 10:11], srow[0:1, 4:5])
        v.drain()
        v.tensor_scalar(srow[0:1, 13:14], srow[0:1, 12:13], 2.0, None, ALU.mult)
        v.drain()
        p.wait(v, "sr")
        v.tensor_scalar(row3[0:1, 0, :], onesf[0:1, 0:128], srow[0:1, 5:6], None, ALU.mult)
        v.tensor_scalar(row3[0:1, 1, :], onesf[0:1, 0:128], srow[0:1, 11:12], None, ALU.mult)
        v.tensor_scalar(row3[0:1, 2, :], onesf[0:1, 0:128], srow[0:1, 13:14], None, ALU.mult)
        p.inc(v.drain(), "cv")
        p.wait(sy, "cv")
        dma(sy, scr3[:, :], row3[0:1, :, :], "sr")
        p.wait(sy, "sr")
        dma(sy, sc0[:, 4:5], scr3[0:1, :], "sr")
        dma(sy, sc0[:, 0:1], scr3[1:2, :], "sr")
        dma(sy, sc0[:, 1:2], scr3[2:3, :], "sr")
        p.wait(v, "sr")
        if which == 2:
            v.tensor_scalar(wf8[:, :, :, :].rearrange("p a b c -> p (a b c)"),
                            wfs[:, :, :, :].rearrange("p a b c -> p (a b c)"),
                            sc0[:, 4:5], None, ALU.mult)
            p.inc(v.drain(), "cq")
            p.wait(t, "cq")
            p.pop("wfs")
        else:
            cq0w = p.counts["cq"]
            for c in range(NC):
                if c >= 2:
                    p.wait(sy, "cq", cq0w + c - 1)
                sem_c = "w0" if c % 2 == 0 else "w1"
                dma(sy, wfsc[:, c % 2, :, :].rearrange("p a b -> p (a b)"),
                    wf[c * 128:(c + 1) * 128, :], sem_c)
                p.wait(v, sem_c)
                v.tensor_scalar(wf8[:, c, :, :].rearrange("p a b -> p (a b)"),
                                wfsc[:, c % 2, :, :].rearrange("p a b -> p (a b)"),
                                sc0[:, 4:5], None, ALU.mult)
                p.inc(v.drain(), "cq")
            p.wait(t, "cq")

    if which == 1:
        for s in range(2):
            if s == 1:
                p.wait(v, "so")
                p.wait(sc_, "so")
            stats_pass(X[s])
            p.wait(sy, "ca")
            dma(sy, stats1o[s, :, :],
                stat[:, 0:2, :].rearrange("p a b -> p (a b)"), "so")
            apply_pass(X[s], None, 1, absmax_base=0)
            v.drain()
            v.tensor_reduce(amx[:, 20 + s:21 + s], amx[:, 0:8], AX.X, ALU.max)
            p.inc(v.drain(), "cv")
        v.tensor_reduce(amx[:, 22:23], amx[:, 20:22], AX.X, ALU.max)
        p.inc(v.drain(), "cv")
        p.wait(sy, "cv")
        roundtrip_col_to_row(amx[:, 22:23])
        p.wait(v, "sr")
        v.tensor_reduce(srow[0:1, 0:1], row128[0:1, :], AX.X, ALU.max)
        p.inc(v.drain(), "cv")
        p.wait(sy, "cv")
        dma(sy, amax_out[0:1, :], srow[0:1, 0:1], "so")
        p.wait(g, "so")
        g.nop()
        p.finish()
        return nc

    if which == 3:
        for s in range(2):
            xq_s = p.push("xq", [128, NC, N], F8)
            if p.counts["cm"]:
                p.wait(sy, "cm")
            dma(sy, stat[:, 0:2, :].rearrange("p a b -> p (a b)"),
                stats3i[s, :, :], "sl")
            p.wait(v, "sl")
            v.tensor_scalar(stat[:, 2, :], stat[:, 1, :], sc0[:, 0:1], None, ALU.mult)
            p.inc(v.drain(), "ca")
            p.wait(t, "ca")
            p.wait(t, "sl")
            apply_pass(X[s], lambda c, xq_s=xq_s: xq_s[:, c, :], 2)
            ffn_matmuls(xq_s, X[s])
            p.pop("xq")
            p.wait(sy, "cv")
            for c in range(NC):
                dma(sy, xout[s, c * 128:(c + 1) * 128, :], X[s][:, c, :], "so")
        p.wait(g, "so")
        g.nop()
        p.finish()
        return nc

    for s in range(2):
        qT = p.push("qT", [128, NC, N], BF16)
        kT = p.push("kT", [128, 2, N], BF16)
        xnb = p.push("xnb", [128, NC, N], BF16)
        Xs = p.push("X", [128, NC, N], F32)
        if s == 1:
            p.wait(sy, "so")
            p.wait(sy, "cv")
            p.wait(g, "so")
            p.wait(g, "cv")
        for c in range(NC):
            rec_ = sy if c % 2 == 0 else g
            dma(rec_, Xs[:, c, :], xin[s, c * 128:(c + 1) * 128, :],
                "sl" if rec_ is sy else "gl")
        if p.counts["cm"]:
            p.wait(sy, "cm")
        dma(sy, stat[:, 0:2, :].rearrange("p a b -> p (a b)"),
            stats1i[s, :, :], "sl")
        p.wait(v, "sl")
        p.wait(t, "sl")
        p.wait(v, "gl")
        p.wait(t, "gl")
        if s == 1:
            p.wait(v, "so")
            p.wait(t, "so")
        xq_s = p.push("xq", [128, NC, N], F8)
        v.tensor_scalar(stat[:, 2, :], stat[:, 1, :], sc0[:, 0:1], None, ALU.mult)
        p.inc(v.drain(), "ca")
        p.wait(t, "ca")
        apply_pass(Xs, lambda c: xq_s[:, c, :], 2)
        if l2_stop != "ln1":
            ffn_matmuls(xq_s, Xs)
        p.pop("xq")
        if l2_stop in ("ln1", "ffn1"):
            p.wait(sy, "cv")
            for c in range(NC):
                dma(sy, xout[s, c * 128:(c + 1) * 128, :], Xs[:, c, :], "so")
            p.pop("X")
            p.pop("xnb")
            p.pop("kT")
            p.pop("qT")
            continue
        stats_pass(Xs)
        ch_cv = apply_pass(Xs, lambda c: xnb[:, c, :], 1)
        for c in range(NC):
            rec_ = sy if c % 2 == 0 else sc_
            rec_.wait_ge(p.sems["cv"], ch_cv[c])
            dma(rec_, xspill[c * 128:(c + 1) * 128, :], Xs[:, c, :], "sp")
        p.pop("X")
        wbuf = p.push("wbuf", [128, NC, NC, 128], BF16)
        wkvb = p.push("wkvb", [128, NC, 4, 128], BF16)
        p.wait(g, "sp")
        if s == 1:
            p.wait(g, "cm")
        else:
            p.wait(g, "cq")
        for kc in range(NC):
            for oc in range(NC):
                dma(g, wbuf[:, kc, oc, :],
                    wq[kc * 128:(kc + 1) * 128, oc * 128:(oc + 1) * 128], "gl")
        for kc in range(NC):
            dma(g, wkvb[:, kc, 0:2, :].rearrange("p a b -> p (a b)"),
                wk[kc * 128:(kc + 1) * 128, :], "gl")
            dma(g, wkvb[:, kc, 2:4, :].rearrange("p a b -> p (a b)"),
                wv[kc * 128:(kc + 1) * 128, :], "gl")
        p.wait(t, "gl")

        def proj(n_oc, wsl, dstT, bias_fn, oc_base=0):
            banksA = [(pbS, w) for w in range(4)]
            banksB = [(pbQ, w) for w in range(4)]
            for oc in range(n_oc):
                bs = banksA if oc % 2 == 0 else banksB
                if oc >= 2:
                    p.wait(t, "cv")
                for kc in range(NC):
                    for w in range(4):
                        i = t.matmul(bs[w][0][:, bs[w][1], :], wsl(kc, oc),
                                     xnb[:, kc, w * NW:(w + 1) * NW],
                                     start=(kc == 0), stop=(kc == NC - 1))
                p.inc(i, "cm")
                p.wait(sc_, "cm")
                i2 = sc_.activation(dstT[:, oc_base + oc, :], bs[0][0][:, :, :],
                                    AF.Identity, bias=bias_fn(oc))
                p.inc(i2, "cv")
            p.wait(t, "cv")

        proj(NC, lambda kc, oc: wbuf[:, kc, oc, :], qT, lambda oc: bqs[:, oc:oc + 1])
        proj(2, lambda kc, oc: wkvb[:, kc, oc, :], kT, lambda oc: bkvs[:, oc:oc + 1])
        vT = p.push("vT", [128, 2, N], F32)
        proj(2, lambda kc, oc: wkvb[:, kc, 2 + oc, :], vT,
             lambda oc: bkvs[:, 2 + oc:3 + oc])

        for kvc in range(2):
            for scn in range(16):
                bk_ = scn % 2
                if kvc * 16 + scn >= 2:
                    p.wait(t, "cv")
                i = t.transpose(pbZR[0:128, bk_, 0:128],
                                vT[:, kvc, scn * 128:(scn + 1) * 128], eyeb[:, :])
                p.inc(i, "cm")
                p.wait(v, "cm")
                v.tensor_copy(vtm[:, scn, kvc * 128:(kvc + 1) * 128],
                              pbZR[:, bk_, 0:128])
                v.drain()
                p.inc(v.nop(), "cv")
        p.wait(t, "cv")
        p.pop("vT")

        for c in range(NC):
            if c > 0:
                p.wait(v, "cm")
            v.tensor_mul(scratch[:, 0:N], qT[:, c, :], qT[:, c, :])
            p.inc(v.drain(), "cv")
            p.wait(t, "cv")
            slot = c % 4
            tgt = [(pbS, w) for w in range(4)]
            for w in range(4):
                i = t.matmul(tgt[w][0][32 * slot:32 * slot + 32, tgt[w][1], :],
                             onesf[:, 0:32], scratch[:, w * NW:(w + 1) * NW],
                             start=(c < 4), stop=(c >= 4),
                             tile_position=(0, 32 * slot), skip_group_check=True)
            p.inc(i, "cm")
        p.wait(v, "cm")
        v.tensor_reduce(amx[:, 0:1], pbS[:, :, :].rearrange("p a b -> p (a b)"),
                        AX.X, ALU.max)
        p.inc(v.drain(), "cv")
        p.wait(t, "cv")
        for c in range(2):
            if c > 0:
                p.wait(v, "cm")
            v.tensor_mul(scratch[:, 0:N], kT[:, c, :], kT[:, c, :])
            p.inc(v.drain(), "cv")
            p.wait(t, "cv")
            for w in range(4):
                i = t.matmul(pbPV[32 * (c * 2 + w // 2):32 * (c * 2 + w // 2) + 32,
                                  w % 2, :],
                             onesf[:, 0:32], scratch[:, w * NW:(w + 1) * NW],
                             start=True, stop=True,
                             tile_position=(0, 32 * (c * 2 + w // 2)),
                             skip_group_check=True)
            p.inc(i, "cm")
        p.wait(v, "cm")
        v.tensor_reduce(amx[:, 1:2], pbPV[:, :, :].rearrange("p a b -> p (a b)"),
                        AX.X, ALU.max)
        p.inc(v.drain(), "cv")
        p.wait(sy, "cv")
        roundtrip_col_to_row(amx[:, 0:1])
        p.wait(v, "sr")
        v.tensor_reduce(srow[0:1, 14:15], row128[0:1, :], AX.X, ALU.max)
        p.inc(v.drain(), "cv")
        p.wait(sy, "cv")
        roundtrip_col_to_row(amx[:, 1:2])
        p.wait(v, "sr")
        v.tensor_reduce(srow[0:1, 15:16], row128[0:1, :], AX.X, ALU.max)
        v.drain()
        v.tensor_mul(srow[0:1, 14:15], srow[0:1, 14:15], srow[0:1, 15:16])
        p.inc(v.drain(), "cv")
        p.wait(sc_, "cv")
        sc_.activation(srow[0:1, 14:15], srow[0:1, 14:15], AF.Ln)
        sc_.drain()
        p.inc(sc_.activation(srow[0:1, 14:15], srow[0:1, 14:15], AF.Exp, scale=0.5),
              "ca")
        p.wait(v, "ca")
        v.tensor_scalar(srow[0:1, 14:15], srow[0:1, 14:15], -0.125, None, ALU.mult)
        p.inc(v.drain(), "cv")
        row_to_allparts(srow[0:1, 14:15], sc0[:, 2 + s:3 + s])
        p.wait(sc_, "sr")
        p.wait(t, "sr")

        p.pop("wkvb")
        p.pop("wbuf")
        p.pop("xnb")
        aoT = p.push("aoT", [128, NC, N], BF16)
        expb = p.push("expb", [128, 4, 16, NW], BF16)
        p.wait(sc_, "sp")
        p.wait(v, "sp")
        p.wait(t, "sp")

        for b in range(4):
            qc0, qc1 = 2 * b, 2 * b + 1
            ktc = qc0 // 4
            kvs = [HEAD_PAIRS[qc0][0] // 4, HEAD_PAIRS[qc0][1] // 4,
                   HEAD_PAIRS[qc1][0] // 4, HEAD_PAIRS[qc1][1] // 4]
            for w in range(4):
                nwin = slice(w * NW, (w + 1) * NW)
                ce0 = p.counts["ce"]
                cf0 = p.counts["cf"]

                def qk_group(scn, js, sem, base):
                    s0 = scn * 128
                    if scn >= 1:
                        p.wait(t, sem, base + scn)
                    for j, qc in js:
                        lohalf = (j % 2) * 64
                        i = t.matmul(pbS[0:128, j, :],
                                     kT[lohalf:lohalf + 64, ktc, s0:s0 + 128],
                                     qT[lohalf:lohalf + 64, qc, nwin],
                                     start=True, stop=True,
                                     tile_position=(lohalf, 0),
                                     skip_group_check=True)
                    p.inc(i, "cm")

                def exp_group(scn, j0, sem):
                    p.wait(sc_, "cm")
                    i = sc_.activation(expb[:, j0:j0 + 2, scn, :],
                                       pbS[:, j0:j0 + 2, :], AF.Exp,
                                       bias=sc0[:, 2 + s:3 + s], scale=0.125)
                    p.inc(i, sem)

                def pv_z_group(scn, jp, last):
                    for j, pvb in jp:
                        col = (j % 2) * 64
                        i = t.matmul(pbQ[col:col + 64, pvb, :],
                                     vtm[:, scn, kvs[j] * 64:kvs[j] * 64 + 64],
                                     expb[:, j, scn, :], start=(scn == 0),
                                     stop=(scn == 15),
                                     tile_position=(0, col), skip_group_check=True)
                    for j, _ in jp:
                        i2 = t.matmul(pbQ[32 * j:32 * j + 32, 2, :], onesb[:, :],
                                      expb[:, j, scn, :], start=(scn == 0),
                                      stop=(scn == 15),
                                      tile_position=(0, 32 * j),
                                      skip_group_check=True)
                    if last:
                        p.inc(i, "cm")
                        p.inc(i2, "cz")

                for scn in range(16):
                    qk_group(scn, ((0, qc0), (1, qc0)), "ce", ce0)
                    exp_group(scn, 0, "ce")
                    qk_group(scn, ((2, qc1), (3, qc1)), "cf", cf0)
                    exp_group(scn, 2, "cf")
                    if scn >= 1:
                        pv_z_group(scn - 1, ((0, 0), (1, 0)), False)
                        pv_z_group(scn - 1, ((2, 1), (3, 1)), False)
                p.wait(t, "ce", ce0 + 16)
                pv_z_group(15, ((0, 0), (1, 0)), False)
                p.wait(t, "cf", cf0 + 16)
                pv_z_group(15, ((2, 1), (3, 1)), True)
                p.wait(v, "cz")
                v.reciprocal(rz4[:, :], pbZR[:, 0, :])
                p.inc(v.drain(), "cz")
                p.wait(t, "cz")
                t.matmul(pbZR[0:64, 1, :], onesf[0:1, 0:64], rz4[0:1, :],
                         start=True, stop=True, tile_position=(0, 0),
                         skip_group_check=True)
                t.matmul(pbZR[64:128, 1, :], onesf[32:33, 0:64], rz4[32:33, :],
                         start=True, stop=True, tile_position=(32, 64),
                         skip_group_check=True)
                t.matmul(pbZR[0:64, 0, :], onesf[64:65, 0:64], rz4[64:65, :],
                         start=True, stop=True, tile_position=(64, 0),
                         skip_group_check=True)
                i = t.matmul(pbZR[64:128, 0, :], onesf[96:97, 0:64], rz4[96:97, :],
                             start=True, stop=True, tile_position=(96, 64),
                             skip_group_check=True)
                p.inc(i, "cm")
                p.wait(v, "cm")
                v.tensor_copy(rzb[:, 0, :], pbZR[:, 1, :])
                v.drain()
                v.tensor_copy(rzb[:, 1, :], pbZR[:, 0, :])
                v.drain()
                v.tensor_mul(aoT[:, qc0, nwin], pbPV[:, 0, :], rzb[:, 0, :])
                v.drain()
                v.tensor_mul(aoT[:, qc1, nwin], pbPV[:, 1, :], rzb[:, 1, :])
                v.drain()
                p.inc(v.nop(), "ce")

        p.pop("expb")
        Xs = p.push("X", [128, NC, N], F32)
        p.wait(sy, "ce")
        for c in range(NC):
            dma(sy, Xs[:, c, :], xspill[c * 128:(c + 1) * 128, :], "sl")
        wbuf = p.push("wbuf", [128, NC, NC, 128], BF16)
        p.wait(g, "ce")
        for kc in range(NC):
            for oc in range(NC):
                dma(g, wbuf[:, kc, oc, :],
                    wo[kc * 128:(kc + 1) * 128, oc * 128:(oc + 1) * 128], "gl")
        p.wait(t, "gl")
        banksA = [(pbS, w) for w in range(4)]
        banksB = [(pbQ, w) for w in range(4)]
        for oc in range(NC):
            bs = banksA if oc % 2 == 0 else banksB
            if oc >= 2:
                p.wait(t, "cv")
            for kc in range(NC):
                for w in range(4):
                    i = t.matmul(bs[w][0][:, bs[w][1], :], wbuf[:, kc, oc, :],
                                 aoT[:, kc, w * NW:(w + 1) * NW],
                                 start=(kc == 0), stop=(kc == NC - 1))
            p.inc(i, "cm")
            p.wait(v, "cm")
            if oc == 0:
                p.wait(v, "sl")
            v.scalar_tensor_tensor(Xs[:, oc, :], bs[0][0][:, :, :],
                                   bos[:, oc:oc + 1], Xs[:, oc, :],
                                   ALU.add, ALU.add)
            v.drain()
            p.inc(v.nop(), "cv")
        p.wait(t, "cv")
        p.pop("wbuf")

        p.wait(sy, "cv")
        for c in range(NC):
            dma(sy, xout[s, c * 128:(c + 1) * 128, :], Xs[:, c, :], "so")
        stats_pass(Xs)
        p.wait(sy, "ca")
        dma(sy, stats3o[s, :, :],
            stat[:, 0:2, :].rearrange("p a b -> p (a b)"), "so")
        apply_pass(Xs, None, 1, absmax_base=0)
        v.drain()
        v.tensor_reduce(amx[:, 20 + s:21 + s], amx[:, 0:8], AX.X, ALU.max)
        p.inc(v.drain(), "cv")
        p.pop("X")
        p.pop("aoT")
        p.pop("kT")
        p.pop("qT")

    if l2_stop == "full":
        v.tensor_reduce(amx[:, 22:23], amx[:, 20:22], AX.X, ALU.max)
        p.inc(v.drain(), "cv")
        p.wait(sy, "cv")
        roundtrip_col_to_row(amx[:, 22:23])
        p.wait(v, "sr")
        v.tensor_reduce(srow[0:1, 0:1], row128[0:1, :], AX.X, ALU.max)
        p.inc(v.drain(), "cv")
        p.wait(sy, "cv")
        dma(sy, amax_out[0:1, :], srow[0:1, 0:1], "so")
    p.wait(g, "so")
    g.nop()
    p.finish()
    return nc


def host_prepare(inputs, n_cores=8):
    import ml_dtypes
    x = np.asarray(inputs["L0_attn_x"])
    Wq = np.asarray(inputs["L0_attn_Wq"])
    bq = np.asarray(inputs["L0_attn_bq"])
    Wk = np.asarray(inputs["L0_attn_Wk"])
    bk = np.asarray(inputs["L0_attn_bk"])
    Wv = np.asarray(inputs["L0_attn_Wv"])
    bv = np.asarray(inputs["L0_attn_bv"])
    Wo = np.asarray(inputs["L0_attn_Wo"])
    bo = np.asarray(inputs["L0_attn_bo"])
    Wf = np.asarray(inputs["L0_ffn_weight"])
    bf_ = np.asarray(inputs["L0_ffn_bias"])

    headperm = [h for pr in HEAD_PAIRS for h in pr]
    colperm = np.concatenate([np.arange(h * HD, (h + 1) * HD) for h in headperm])
    Wq_p = np.ascontiguousarray(Wq[:, colperm])
    bq_p = np.ascontiguousarray(bq[colperm]).reshape(NC, 128).T.copy()
    myrows = []
    for h in headperm:
        g, kv = h % 4, h // 4
        base = (g * 4 + kv) * HD
        myrows.extend(range(base, base + HD))
    Wo_p = np.ascontiguousarray(Wo[myrows, :])
    bkv_host = np.stack([bk.reshape(2, 128).T, bv.reshape(2, 128).T],
                        axis=1).reshape(128, 4)
    bkv_host = np.concatenate([bk.reshape(2, 128).T, bv.reshape(2, 128).T], axis=1)
    bo_host = bo.reshape(NC, 128).T.copy()
    fb_host = bf_.reshape(NC, 128).T.copy()
    WfT = np.ascontiguousarray(Wf.T)
    eye = np.eye(128, dtype=np.float32)

    per_core = []
    for c in range(n_cores):
        xc = x[2 * c:2 * c + 2]
        xT = np.ascontiguousarray(xc.transpose(0, 2, 1))
        per_core.append({
            "xT": xT, "Wq": Wq_p, "Wk": Wk, "Wv": Wv, "Wo": Wo_p,
            "bq": np.ascontiguousarray(bq_p), "bkv": np.ascontiguousarray(bkv_host),
            "bo": np.ascontiguousarray(bo_host), "WfT": WfT,
            "fbias": np.ascontiguousarray(fb_host), "eye": eye,
        })
    return per_core


def host_finish(results):
    outs = []
    for r in results:
        xT = r["xoutT"]
        outs.append(np.ascontiguousarray(xT.transpose(0, 2, 1)))
    return np.concatenate(outs, axis=0).astype(np.float32)

from concourse.bass_utils import run_bass_kernel_spmd

def kernel(**inputs):
    n_cores = 8
    percore = host_prepare(inputs, n_cores)
    core_ids = list(range(n_cores))

    nc1 = build_launch(1, n_cores)
    res1 = run_bass_kernel_spmd(nc1, [{"xT": pc["xT"]} for pc in percore],
                                core_ids=core_ids)
    amax1 = np.concatenate([res1.results[i]["amax_out"] for i in range(n_cores)],
                           axis=1)

    nc2 = build_launch(2, n_cores)
    in2 = [dict(pc, amaxv=amax1, stats1=res1.results[i]["stats1"])
           for i, pc in enumerate(percore)]
    res2 = run_bass_kernel_spmd(nc2, in2, core_ids=core_ids)
    amax3 = np.concatenate([res2.results[i]["amax_out"] for i in range(n_cores)],
                           axis=1)

    nc3 = build_launch(3, n_cores)
    in3 = [{"xT": res2.results[i]["x2T"], "WfT": pc["WfT"],
            "fbias": pc["fbias"], "amaxv": amax3,
            "stats3": res2.results[i]["stats3"]}
           for i, pc in enumerate(percore)]
    res3 = run_bass_kernel_spmd(nc3, in3, core_ids=core_ids)
    return host_finish([res3.results[i] for i in range(n_cores)])


# version 14
# speedup vs baseline: 1.0364x; 1.0083x over previous
import numpy as np
import concourse.bass as bass
import concourse.mybir as mybir
from contextlib import ExitStack

F32 = mybir.dt.float32
BF16 = mybir.dt.bfloat16
F8 = mybir.dt.float8e4
AF = mybir.ActivationFunctionType
ALU = mybir.AluOpType
AX = mybir.AxisListType

D, N, NC, NW = 1024, 2048, 8, 512
NH, NKV, HD = 16, 4, 64
EPS = 1e-5
HEAD_PAIRS = [(0, 4), (1, 5), (2, 6), (3, 7), (8, 12), (9, 13), (10, 14), (11, 15)]


class RecInstr:
    __slots__ = ("slot",)

    def __init__(self, slot):
        self.slot = slot

    def then_inc(self, sem, k):
        self.slot["inc"] = (sem, k)
        return self


class Rec:
    def __init__(self, name):
        self.name = name
        self.ops = []

    def __getattr__(self, meth):
        if meth.startswith("__"):
            raise AttributeError(meth)

        def call(*a, **k):
            slot = {}
            self.ops.append((meth, a, k, slot))
            return RecInstr(slot)
        return call


def _replay(rec):
    def f(eng):
        for meth, a, k, slot in rec.ops:
            r = getattr(eng, meth)(*a, **k)
            if "inc" in slot:
                r.then_inc(*slot["inc"])
    return f


class Prog:
    def __init__(self, nc):
        self.nc = nc
        self.es = ExitStack()
        self.counts = {}
        self.sems = {}
        self.sy, self.v, self.sc, self.t, self.g = (
            Rec("sync"), Rec("vector"), Rec("scalar"), Rec("tensor"), Rec("gpsimd"))

    def sb(self, name, shape, dt=F32):
        return self.es.enter_context(self.nc.sbuf_tensor(name, shape, dt))

    def push(self, name, shape, dt=F32):
        self._pushn = getattr(self, "_pushn", 0) + 1
        ctx = self.nc.sbuf_tensor(f"{name}_{self._pushn}", shape, dt)
        h = ctx.__enter__()
        self._stack = getattr(self, "_stack", [])
        self._stack.append((name, ctx))
        return h

    def pop(self, name):
        n, ctx = self._stack.pop()
        assert n == name, (n, name)
        ctx.__exit__(None, None, None)

    def ps(self, name, shape):
        return self.es.enter_context(self.nc.psum_tensor(name, shape, F32))

    def sem(self, name):
        s = self.es.enter_context(self.nc.semaphore(name))
        self.sems[name] = s
        self.counts[name] = 0

    def inc(self, instr, name, k=1):
        self.counts[name] += k
        instr.then_inc(self.sems[name], k)
        return self.counts[name]

    def wait(self, rec, name, val=None):
        rec.wait_ge(self.sems[name], self.counts[name] if val is None else val)

    def finish(self):
        with self.nc.Block() as block:
            block.sync(_replay(self.sy))
            block.vector(_replay(self.v))
            block.scalar(_replay(self.sc))
            block.tensor(_replay(self.t))
            block.gpsimd(_replay(self.g))
        self.es.close()


def build_launch(which, n_cores=8, l2_stop="full"):
    nc = bass.Bass("TRN2", target_bir_lowering=False)
    p = Prog(nc)
    sy, v, sc_, t, g = p.sy, p.v, p.sc, p.t, p.g

    xin = nc.dram_tensor("xT", [2, D, N], F32, kind="ExternalInput").ap()
    if which in (2, 3):
        wf = nc.dram_tensor("WfT", [D, D], F32, kind="ExternalInput").ap()
        fbias = nc.dram_tensor("fbias", [128, NC], F32, kind="ExternalInput").ap()
        amaxv = nc.dram_tensor("amaxv", [1, n_cores], F32, kind="ExternalInput").ap()
    if which == 2:
        wq = nc.dram_tensor("Wq", [D, D], F32, kind="ExternalInput").ap()
        wk = nc.dram_tensor("Wk", [D, NKV * HD], F32, kind="ExternalInput").ap()
        wv = nc.dram_tensor("Wv", [D, NKV * HD], F32, kind="ExternalInput").ap()
        wo = nc.dram_tensor("Wo", [D, D], F32, kind="ExternalInput").ap()
        bq = nc.dram_tensor("bq", [128, NC], F32, kind="ExternalInput").ap()
        bkv = nc.dram_tensor("bkv", [128, 4], F32, kind="ExternalInput").ap()
        bo = nc.dram_tensor("bo", [128, NC], F32, kind="ExternalInput").ap()
        eye = nc.dram_tensor("eye", [128, 128], F32, kind="ExternalInput").ap()
    if which in (1, 2):
        amax_out = nc.dram_tensor("amax_out", [1, 1], F32, kind="ExternalOutput").ap()
    if which == 1:
        stats1o = nc.dram_tensor("stats1", [2, 128, 2 * NW], F32,
                                 kind="ExternalOutput").ap()
    if which == 2:
        stats1i = nc.dram_tensor("stats1", [2, 128, 2 * NW], F32,
                                 kind="ExternalInput").ap()
        stats3o = nc.dram_tensor("stats3", [2, 128, 2 * NW], F32,
                                 kind="ExternalOutput").ap()
    if which == 3:
        stats3i = nc.dram_tensor("stats3", [2, 128, 2 * NW], F32,
                                 kind="ExternalInput").ap()
    if which == 2:
        xout = nc.dram_tensor("x2T", [2, D, N], F32, kind="ExternalOutput").ap()
    if which == 3:
        xout = nc.dram_tensor("xoutT", [2, D, N], F32, kind="ExternalOutput").ap()
    scr = nc.dram_tensor("scr", [1, 128], F32, kind="Internal").ap()
    scr3 = nc.dram_tensor("scr3", [3, 128], F32, kind="Internal").ap()
    if which == 2:
        xspill = nc.dram_tensor("xspill", [D, N], F32, kind="Internal").ap()

    onesf = p.sb("onesf", [128, 128], F32)
    onesb = p.sb("onesb", [128, 32], BF16)
    row128 = p.sb("row128", [1, 128], F32)
    row3 = p.sb("row3", [1, 3, 128], F32)
    srow = p.sb("srow", [1, 16], F32)
    sc0 = p.sb("sc0", [128, 8], F32)
    stat = p.sb("stat", [128, 3, NW], F32)
    amx = p.sb("amx", [128, 40], F32)
    cbias = p.sb("cbias", [128, 2], F32)
    scratch = p.sb("scratch", [128, N], F32)
    scratchB = p.sb("scratchB", [128, N], F32)

    nxbuf = 1 if which == 2 else 2
    if which != 2:
        X = [p.sb(f"X{i}", [128, NC, N], F32) for i in range(nxbuf)]

    if which in (2, 3):
        wf8 = p.sb("wf8", [128, NC, NC, 128], F8)
        fb = p.sb("fbs", [128, NC], F32)
        fbh = p.sb("fbh", [128, NC], F32)
        ybh2 = p.sb("ybh", [128, 2, 4, NW], BF16)
    if which == 2:
        vtm = p.sb("vtm", [128, 16, 256], BF16)
        eyeb = p.sb("eyeb", [128, 128], F32)
        rz4 = p.sb("rz4", [128, NW], F32)
        rzb = p.sb("rzb", [128, 2, NW], F32)
        bqs = p.sb("bqs", [128, NC], F32)
        bkvs = p.sb("bkvs", [128, 4], F32)
        bos = p.sb("bos", [128, NC], F32)
    if which == 2:
        wfs = p.push("wfs", [128, NC, NC, 128], F32)
    if which == 3:
        wfsc = p.sb("wfsc", [128, 2, NC, 128], F32)

    pbS = p.ps("pbS", [128, 4, NW])
    pbQ = p.ps("pbQ", [128, 4, NW])
    pbPV = pbQ[:, 0:2]
    pbZR = pbQ[:, 2:4]

    for s_ in ["sl", "sr", "so", "gl", "cv", "ca", "cm", "cq", "ce", "cz", "sp", "cf", "cn", "w0", "w1", "go", "sq"]:
        p.sem(s_)

    def dma(rec, out, in_, semname):
        p.counts[semname] += 16
        rec.dma_start(out=out, in_=in_).then_inc(p.sems[semname], 16)

    def roundtrip_col_to_row(col_ap):
        p.wait(sy, "cv")
        dma(sy, scr[0:1, :], col_ap, "sr")
        p.wait(sy, "sr")
        dma(sy, row128[0:1, :], scr[0:1, :], "sr")

    def row_to_allparts(src11, dst_col):
        p.wait(v, "sr")
        v.tensor_scalar(row128[0:1, :], onesf[0:1, 0:128], src11, None, ALU.mult)
        p.inc(v.drain(), "cv")
        p.wait(sy, "cv")
        dma(sy, scr[0:1, :], row128[0:1, :], "sr")
        p.wait(sy, "sr")
        dma(sy, dst_col, scr[0:1, :], "sr")

    def stats_pass(Xs):
        cm0 = p.counts["cm"]
        cv0 = p.counts["cv"]
        bufs = [scratch, scratchB]
        for c in range(NC):
            if c >= 2:
                p.wait(v, "cm", cm0 + c - 1)
            v.tensor_mul(bufs[c % 2][:, 0:N], Xs[:, c, :], Xs[:, c, :])
            p.inc(v.drain(), "cv")
            p.wait(t, "cv", cv0 + c + 1)
            for w in range(4):
                t.matmul(pbS[32 * w:32 * w + 32, 0, :], onesf[:, 0:32],
                         Xs[:, c, w * NW:(w + 1) * NW], start=(c == 0),
                         stop=(c == NC - 1), tile_position=(0, 32 * w),
                         skip_group_check=True)
            for w in range(4):
                i = t.matmul(pbS[32 * w:32 * w + 32, 1, :], onesf[:, 0:32],
                             bufs[c % 2][:, w * NW:(w + 1) * NW], start=(c == 0),
                             stop=(c == NC - 1), tile_position=(0, 32 * w),
                             skip_group_check=True)
            p.inc(i, "cm")
        p.wait(v, "cm")
        v.tensor_scalar(stat[:, 0, :], pbS[:, 0, :], 1.0 / D, None, ALU.mult)
        v.drain()
        v.tensor_mul(scratch[:, 0:NW], stat[:, 0, :], stat[:, 0, :])
        v.drain()
        v.tensor_scalar(stat[:, 2, :], pbS[:, 1, :], 1.0 / D, None, ALU.mult)
        v.drain()
        v.tensor_sub(stat[:, 2, :], stat[:, 2, :], scratch[:, 0:NW])
        p.inc(v.drain(), "cv")
        p.wait(sc_, "cv")
        sc_.activation(stat[:, 2, :], stat[:, 2, :], AF.Ln, bias=cbias[:, 0:1])
        sc_.drain()
        p.inc(sc_.activation(stat[:, 1, :], stat[:, 2, :], AF.Exp, scale=-0.5), "ca")
        p.wait(v, "ca")
        p.wait(t, "ca")

    def apply_pass(Xs, dst_fn, row1, absmax_base=None):
        for w in range(4):
            t.matmul(pbS[:, w, :], onesf[32 * w:32 * w + 1, :],
                     stat[32 * w:32 * w + 1, 0, :], start=True, stop=True,
                     tile_position=(32 * w, 0), skip_group_check=True)
            i = t.matmul(pbQ[:, w, :], onesf[32 * w:32 * w + 1, :],
                         stat[32 * w:32 * w + 1, row1, :], start=True, stop=True,
                         tile_position=(32 * w, 0), skip_group_check=True)
        p.inc(i, "cm")
        p.wait(v, "cm")
        chunk_cv = []
        bufs = [scratch, scratchB]
        for c in range(NC):
            b_ = bufs[c % 2]
            if c >= 2:
                v.drain()
            v.tensor_sub(b_[:, 0:N], Xs[:, c, :],
                         pbS[:, :, :].rearrange("p a b -> p (a b)"))
            v.drain()
            if absmax_base is None:
                i2 = v.tensor_mul(dst_fn(c), b_[:, 0:N],
                                  pbQ[:, :, :].rearrange("p a b -> p (a b)"))
                chunk_cv.append(p.inc(i2, "cv"))
            else:
                v.tensor_mul(b_[:, 0:N], b_[:, 0:N],
                             pbQ[:, :, :].rearrange("p a b -> p (a b)"))
                v.drain()
                k = absmax_base + c
                v.tensor_reduce(amx[:, k:k + 1], b_[:, 0:N], AX.X,
                                ALU.max, apply_absolute_value=True)
        if absmax_base is not None:
            p.inc(v.nop(), "cv")
        p.wait(t, "cv")
        return chunk_cv

    def ffn_matmuls(xq_s, Xs):
        oc_cv = []
        banksA = [(pbS, w) for w in range(4)]
        banksB = [(pbQ, w) for w in range(4)]
        for oc in range(NC):
            bs = banksA if oc % 2 == 0 else banksB
            if oc >= 2:
                p.wait(t, "cv")
            for kc in range(NC):
                for w in range(4):
                    i = t.matmul(bs[w][0][:, bs[w][1], :], wf8[:, kc, oc, :],
                                 xq_s[:, kc, w * NW:(w + 1) * NW],
                                 start=(kc == 0), stop=(kc == NC - 1))
            p.inc(i, "cm")
            p.wait(sc_, "cm")
            if oc >= 2:
                p.wait(sc_, "cv")
            i2 = sc_.activation(ybh2[:, oc % 2, :, :], bs[0][0][:, :, :], AF.Identity,
                                bias=fbh[:, oc:oc + 1], scale=sc0[:, 1:2])
            p.inc(i2, "ca")
            p.wait(v, "ca")
            v.tensor_add(Xs[:, oc, :], Xs[:, oc, :],
                         ybh2[:, oc % 2, :, :].rearrange("p a b -> p (a b)"))
            v.drain()
            oc_cv.append(p.inc(v.nop(), "cv"))
        p.wait(t, "cv")
        return oc_cv

    if which == 2:
        dma(sy, eyeb[:], eye[:, :], "sl")
    if which != 2:
        for i_ in range(nxbuf):
            for c in range(NC):
                rec_ = sy if (i_ * NC + c) % 2 == 0 else g
                dma(rec_, X[i_][:, c, :], xin[i_, c * 128:(c + 1) * 128, :],
                    "sl" if rec_ is sy else "gl")
    if which in (2, 3):
        dma(sy, fb[:, :], fbias[:, :], "sl")
        if which == 2:
            for c in range(NC):
                dma(sy, wfs[:, c, :, :].rearrange("p a b -> p (a b)"),
                    wf[c * 128:(c + 1) * 128, :], "sl")
        dma(sy, amx[0:1, 24:24 + n_cores], amaxv[0:1, :], "sl")
    if which == 2:
        dma(sy, bqs[:, :], bq[:, :], "sl")
        dma(sy, bkvs[:, :], bkv[:, :], "sl")
        dma(sy, bos[:, :], bo[:, :], "sl")

    p.wait(v, "sl")
    if which != 2:
        p.wait(v, "gl")
        p.wait(t, "gl")
    v.memset(onesf[:], 1.0)
    v.memset(onesb[:], 1.0)
    v.memset(cbias[:, 0:1], EPS)
    p.inc(v.drain(), "cv")
    p.wait(t, "cv")
    p.wait(sc_, "cv")
    p.wait(sy, "cv")

    if which in (2, 3):
        v.tensor_scalar(fbh[:, :], fb[:, :], 0.5, None, ALU.mult)
        v.drain()
        if which == 2:
            v.tensor_reduce(amx[:, 17:18],
                            wfs[:, :, :, :].rearrange("p a b c -> p (a b c)"),
                            AX.X, ALU.max, apply_absolute_value=True)
            p.inc(v.drain(), "cv")
        else:
            cv0w = p.counts["cv"]
            for c in range(NC):
                if c >= 2:
                    p.wait(sy, "cv", cv0w + c - 1)
                sem_c = "w0" if c % 2 == 0 else "w1"
                dma(sy, wfsc[:, c % 2, :, :].rearrange("p a b -> p (a b)"),
                    wf[c * 128:(c + 1) * 128, :], sem_c)
                p.wait(v, sem_c)
                v.tensor_reduce(amx[:, 8 + c:9 + c],
                                wfsc[:, c % 2, :, :].rearrange("p a b -> p (a b)"),
                                AX.X, ALU.max, apply_absolute_value=True)
                p.inc(v.drain(), "cv")
            v.tensor_reduce(amx[:, 17:18], amx[:, 8:16], AX.X, ALU.max)
            p.inc(v.drain(), "cv")
        p.wait(sy, "cv")
        roundtrip_col_to_row(amx[:, 17:18])
        p.wait(v, "sr")
        v.tensor_reduce(srow[0:1, 0:1], row128[0:1, :], AX.X, ALU.max)
        v.drain()
        v.tensor_scalar(srow[0:1, 1:2], srow[0:1, 0:1], 1e-12, None, ALU.max)
        v.drain()
        v.reciprocal(srow[0:1, 2:3], srow[0:1, 1:2])
        v.drain()
        v.tensor_scalar(srow[0:1, 3:4], srow[0:1, 2:3], 448.0, 448.0, ALU.mult, ALU.min)
        v.drain()
        v.reciprocal(srow[0:1, 4:5], srow[0:1, 3:4])
        v.drain()
        v.tensor_scalar(srow[0:1, 5:6], srow[0:1, 3:4], 0.5, None, ALU.mult)
        v.drain()
        v.tensor_reduce(srow[0:1, 6:7], amx[0:1, 24:24 + n_cores], AX.X, ALU.max)
        v.drain()
        v.tensor_scalar(srow[0:1, 7:8], srow[0:1, 6:7], 1e-12, None, ALU.max)
        v.drain()
        v.reciprocal(srow[0:1, 8:9], srow[0:1, 7:8])
        v.drain()
        v.tensor_scalar(srow[0:1, 9:10], srow[0:1, 8:9], 448.0, 448.0, ALU.mult, ALU.min)
        v.drain()
        v.reciprocal(srow[0:1, 10:11], srow[0:1, 9:10])
        v.drain()
        v.tensor_scalar(srow[0:1, 11:12], srow[0:1, 9:10], 0.5, None, ALU.mult)
        v.drain()
        v.tensor_mul(srow[0:1, 12:13], srow[0:1, 10:11], srow[0:1, 4:5])
        v.drain()
        v.tensor_scalar(srow[0:1, 13:14], srow[0:1, 12:13], 2.0, None, ALU.mult)
        v.drain()
        p.wait(v, "sr")
        v.tensor_scalar(row3[0:1, 0, :], onesf[0:1, 0:128], srow[0:1, 5:6], None, ALU.mult)
        v.tensor_scalar(row3[0:1, 1, :], onesf[0:1, 0:128], srow[0:1, 11:12], None, ALU.mult)
        v.tensor_scalar(row3[0:1, 2, :], onesf[0:1, 0:128], srow[0:1, 13:14], None, ALU.mult)
        p.inc(v.drain(), "cv")
        p.wait(sy, "cv")
        dma(sy, scr3[:, :], row3[0:1, :, :], "sr")
        p.wait(sy, "sr")
        dma(sy, sc0[:, 4:5], scr3[0:1, :], "sr")
        dma(sy, sc0[:, 0:1], scr3[1:2, :], "sr")
        dma(sy, sc0[:, 1:2], scr3[2:3, :], "sr")
        p.wait(v, "sr")
        if which == 2:
            v.tensor_scalar(wf8[:, :, :, :].rearrange("p a b c -> p (a b c)"),
                            wfs[:, :, :, :].rearrange("p a b c -> p (a b c)"),
                            sc0[:, 4:5], None, ALU.mult)
            p.inc(v.drain(), "cq")
            p.wait(t, "cq")
            p.pop("wfs")
        else:
            cq0w = p.counts["cq"]
            for c in range(NC):
                if c >= 2:
                    p.wait(sy, "cq", cq0w + c - 1)
                sem_c = "w0" if c % 2 == 0 else "w1"
                dma(sy, wfsc[:, c % 2, :, :].rearrange("p a b -> p (a b)"),
                    wf[c * 128:(c + 1) * 128, :], sem_c)
                p.wait(v, sem_c)
                v.tensor_scalar(wf8[:, c, :, :].rearrange("p a b -> p (a b)"),
                                wfsc[:, c % 2, :, :].rearrange("p a b -> p (a b)"),
                                sc0[:, 4:5], None, ALU.mult)
                p.inc(v.drain(), "cq")
            p.wait(t, "cq")

    if which == 1:
        for s in range(2):
            if s == 1:
                p.wait(v, "so")
                p.wait(sc_, "so")
            stats_pass(X[s])
            p.wait(sy, "ca")
            dma(sy, stats1o[s, :, :],
                stat[:, 0:2, :].rearrange("p a b -> p (a b)"), "so")
            apply_pass(X[s], None, 1, absmax_base=0)
            v.drain()
            v.tensor_reduce(amx[:, 20 + s:21 + s], amx[:, 0:8], AX.X, ALU.max)
            p.inc(v.drain(), "cv")
        v.tensor_reduce(amx[:, 22:23], amx[:, 20:22], AX.X, ALU.max)
        p.inc(v.drain(), "cv")
        p.wait(sy, "cv")
        roundtrip_col_to_row(amx[:, 22:23])
        p.wait(v, "sr")
        v.tensor_reduce(srow[0:1, 0:1], row128[0:1, :], AX.X, ALU.max)
        p.inc(v.drain(), "cv")
        p.wait(sy, "cv")
        dma(sy, amax_out[0:1, :], srow[0:1, 0:1], "so")
        p.wait(g, "so")
        g.nop()
        p.finish()
        return nc

    if which == 3:
        for s in range(2):
            xq_s = p.push("xq", [128, NC, N], F8)
            if p.counts["cm"]:
                p.wait(sc_, "cm")
            dma(sc_, stat[:, 0:2, :].rearrange("p a b -> p (a b)"),
                stats3i[s, :, :], "sq")
            p.wait(v, "sq")
            p.wait(v, "sl")
            v.tensor_scalar(stat[:, 2, :], stat[:, 1, :], sc0[:, 0:1], None, ALU.mult)
            p.inc(v.drain(), "ca")
            p.wait(t, "ca")
            p.wait(t, "sq")
            apply_pass(X[s], lambda c, xq_s=xq_s: xq_s[:, c, :], 2)
            oc_cv = ffn_matmuls(xq_s, X[s])
            p.pop("xq")
            for c in range(NC):
                rec_ = sy if c % 2 == 0 else g
                rec_.wait_ge(p.sems["cv"], oc_cv[c])
                dma(rec_, xout[s, c * 128:(c + 1) * 128, :], X[s][:, c, :],
                    "so" if c % 2 == 0 else "go")
        p.wait(g, "so")
        p.wait(g, "go")
        g.nop()
        p.finish()
        return nc

    for s in range(2):
        qT = p.push("qT", [128, NC, N], BF16)
        kT = p.push("kT", [128, 2, N], BF16)
        xnb = p.push("xnb", [128, NC, N], BF16)
        Xs = p.push("X", [128, NC, N], F32)
        if s == 1:
            p.wait(sy, "so")
            p.wait(sy, "go")
            p.wait(sy, "cv")
            p.wait(g, "so")
            p.wait(g, "go")
            p.wait(g, "cv")
        for c in range(NC):
            rec_ = sy if c % 2 == 0 else g
            dma(rec_, Xs[:, c, :], xin[s, c * 128:(c + 1) * 128, :],
                "sl" if rec_ is sy else "gl")
        if p.counts["cm"]:
            p.wait(sy, "cm")
        dma(sy, stat[:, 0:2, :].rearrange("p a b -> p (a b)"),
            stats1i[s, :, :], "sl")
        p.wait(v, "sl")
        p.wait(t, "sl")
        p.wait(v, "gl")
        p.wait(t, "gl")
        if s == 1:
            p.wait(v, "so")
            p.wait(t, "so")
            p.wait(v, "go")
            p.wait(t, "go")
        xq_s = p.push("xq", [128, NC, N], F8)
        v.tensor_scalar(stat[:, 2, :], stat[:, 1, :], sc0[:, 0:1], None, ALU.mult)
        p.inc(v.drain(), "ca")
        p.wait(t, "ca")
        apply_pass(Xs, lambda c: xq_s[:, c, :], 2)
        if l2_stop != "ln1":
            ffn_matmuls(xq_s, Xs)
        p.pop("xq")
        if l2_stop in ("ln1", "ffn1"):
            p.wait(sy, "cv")
            for c in range(NC):
                dma(sy, xout[s, c * 128:(c + 1) * 128, :], Xs[:, c, :], "so")
            p.pop("X")
            p.pop("xnb")
            p.pop("kT")
            p.pop("qT")
            continue
        stats_pass(Xs)
        ch_cv = apply_pass(Xs, lambda c: xnb[:, c, :], 1)
        for c in range(NC):
            rec_ = sy if c % 2 == 0 else sc_
            rec_.wait_ge(p.sems["cv"], ch_cv[c])
            dma(rec_, xspill[c * 128:(c + 1) * 128, :], Xs[:, c, :], "sp")
        p.pop("X")
        wbuf = p.push("wbuf", [128, NC, NC, 128], BF16)
        wkvb = p.push("wkvb", [128, NC, 4, 128], BF16)
        p.wait(g, "sp")
        if s == 1:
            p.wait(g, "cm")
        else:
            p.wait(g, "cq")
        for kc in range(NC):
            for oc in range(NC):
                dma(g, wbuf[:, kc, oc, :],
                    wq[kc * 128:(kc + 1) * 128, oc * 128:(oc + 1) * 128], "gl")
        for kc in range(NC):
            dma(g, wkvb[:, kc, 0:2, :].rearrange("p a b -> p (a b)"),
                wk[kc * 128:(kc + 1) * 128, :], "gl")
            dma(g, wkvb[:, kc, 2:4, :].rearrange("p a b -> p (a b)"),
                wv[kc * 128:(kc + 1) * 128, :], "gl")
        p.wait(t, "gl")

        def proj(n_oc, wsl, dstT, bias_fn, oc_base=0):
            banksA = [(pbS, w) for w in range(4)]
            banksB = [(pbQ, w) for w in range(4)]
            for oc in range(n_oc):
                bs = banksA if oc % 2 == 0 else banksB
                if oc >= 2:
                    p.wait(t, "cv")
                for kc in range(NC):
                    for w in range(4):
                        i = t.matmul(bs[w][0][:, bs[w][1], :], wsl(kc, oc),
                                     xnb[:, kc, w * NW:(w + 1) * NW],
                                     start=(kc == 0), stop=(kc == NC - 1))
                p.inc(i, "cm")
                p.wait(sc_, "cm")
                i2 = sc_.activation(dstT[:, oc_base + oc, :], bs[0][0][:, :, :],
                                    AF.Identity, bias=bias_fn(oc))
                p.inc(i2, "cv")
            p.wait(t, "cv")

        proj(NC, lambda kc, oc: wbuf[:, kc, oc, :], qT, lambda oc: bqs[:, oc:oc + 1])
        proj(2, lambda kc, oc: wkvb[:, kc, oc, :], kT, lambda oc: bkvs[:, oc:oc + 1])
        vT = p.push("vT", [128, 2, N], F32)
        proj(2, lambda kc, oc: wkvb[:, kc, 2 + oc, :], vT,
             lambda oc: bkvs[:, 2 + oc:3 + oc])

        for kvc in range(2):
            for scn in range(16):
                bk_ = scn % 2
                if kvc * 16 + scn >= 2:
                    p.wait(t, "cv")
                i = t.transpose(pbZR[0:128, bk_, 0:128],
                                vT[:, kvc, scn * 128:(scn + 1) * 128], eyeb[:, :])
                p.inc(i, "cm")
                p.wait(v, "cm")
                v.tensor_copy(vtm[:, scn, kvc * 128:(kvc + 1) * 128],
                              pbZR[:, bk_, 0:128])
                v.drain()
                p.inc(v.nop(), "cv")
        p.wait(t, "cv")
        p.pop("vT")

        for c in range(NC):
            if c > 0:
                p.wait(v, "cm")
            v.tensor_mul(scratch[:, 0:N], qT[:, c, :], qT[:, c, :])
            p.inc(v.drain(), "cv")
            p.wait(t, "cv")
            slot = c % 4
            tgt = [(pbS, w) for w in range(4)]
            for w in range(4):
                i = t.matmul(tgt[w][0][32 * slot:32 * slot + 32, tgt[w][1], :],
                             onesf[:, 0:32], scratch[:, w * NW:(w + 1) * NW],
                             start=(c < 4), stop=(c >= 4),
                             tile_position=(0, 32 * slot), skip_group_check=True)
            p.inc(i, "cm")
        p.wait(v, "cm")
        v.tensor_reduce(amx[:, 0:1], pbS[:, :, :].rearrange("p a b -> p (a b)"),
                        AX.X, ALU.max)
        p.inc(v.drain(), "cv")
        p.wait(t, "cv")
        for c in range(2):
            if c > 0:
                p.wait(v, "cm")
            v.tensor_mul(scratch[:, 0:N], kT[:, c, :], kT[:, c, :])
            p.inc(v.drain(), "cv")
            p.wait(t, "cv")
            for w in range(4):
                i = t.matmul(pbPV[32 * (c * 2 + w // 2):32 * (c * 2 + w // 2) + 32,
                                  w % 2, :],
                             onesf[:, 0:32], scratch[:, w * NW:(w + 1) * NW],
                             start=True, stop=True,
                             tile_position=(0, 32 * (c * 2 + w // 2)),
                             skip_group_check=True)
            p.inc(i, "cm")
        p.wait(v, "cm")
        v.tensor_reduce(amx[:, 1:2], pbPV[:, :, :].rearrange("p a b -> p (a b)"),
                        AX.X, ALU.max)
        p.inc(v.drain(), "cv")
        p.wait(sy, "cv")
        roundtrip_col_to_row(amx[:, 0:1])
        p.wait(v, "sr")
        v.tensor_reduce(srow[0:1, 14:15], row128[0:1, :], AX.X, ALU.max)
        p.inc(v.drain(), "cv")
        p.wait(sy, "cv")
        roundtrip_col_to_row(amx[:, 1:2])
        p.wait(v, "sr")
        v.tensor_reduce(srow[0:1, 15:16], row128[0:1, :], AX.X, ALU.max)
        v.drain()
        v.tensor_mul(srow[0:1, 14:15], srow[0:1, 14:15], srow[0:1, 15:16])
        p.inc(v.drain(), "cv")
        p.wait(sc_, "cv")
        sc_.activation(srow[0:1, 14:15], srow[0:1, 14:15], AF.Ln)
        sc_.drain()
        p.inc(sc_.activation(srow[0:1, 14:15], srow[0:1, 14:15], AF.Exp, scale=0.5),
              "ca")
        p.wait(v, "ca")
        v.tensor_scalar(srow[0:1, 14:15], srow[0:1, 14:15], -0.125, None, ALU.mult)
        p.inc(v.drain(), "cv")
        row_to_allparts(srow[0:1, 14:15], sc0[:, 2 + s:3 + s])
        p.wait(sc_, "sr")
        p.wait(t, "sr")

        p.pop("wkvb")
        p.pop("wbuf")
        p.pop("xnb")
        aoT = p.push("aoT", [128, NC, N], BF16)
        expb = p.push("expb", [128, 4, 16, NW], BF16)
        p.wait(sc_, "sp")
        p.wait(v, "sp")
        p.wait(t, "sp")

        for b in range(4):
            qc0, qc1 = 2 * b, 2 * b + 1
            ktc = qc0 // 4
            kvs = [HEAD_PAIRS[qc0][0] // 4, HEAD_PAIRS[qc0][1] // 4,
                   HEAD_PAIRS[qc1][0] // 4, HEAD_PAIRS[qc1][1] // 4]
            for w in range(4):
                nwin = slice(w * NW, (w + 1) * NW)
                ce0 = p.counts["ce"]
                cf0 = p.counts["cf"]

                def qk_group(scn, js, sem, base):
                    s0 = scn * 128
                    if scn >= 1:
                        p.wait(t, sem, base + scn)
                    for j, qc in js:
                        lohalf = (j % 2) * 64
                        i = t.matmul(pbS[0:128, j, :],
                                     kT[lohalf:lohalf + 64, ktc, s0:s0 + 128],
                                     qT[lohalf:lohalf + 64, qc, nwin],
                                     start=True, stop=True,
                                     tile_position=(lohalf, 0),
                                     skip_group_check=True)
                    p.inc(i, "cm")

                def exp_group(scn, j0, sem):
                    p.wait(sc_, "cm")
                    i = sc_.activation(expb[:, j0:j0 + 2, scn, :],
                                       pbS[:, j0:j0 + 2, :], AF.Exp,
                                       bias=sc0[:, 2 + s:3 + s], scale=0.125)
                    p.inc(i, sem)

                def pv_z_group(scn, jp, last):
                    for j, pvb in jp:
                        col = (j % 2) * 64
                        i = t.matmul(pbQ[col:col + 64, pvb, :],
                                     vtm[:, scn, kvs[j] * 64:kvs[j] * 64 + 64],
                                     expb[:, j, scn, :], start=(scn == 0),
                                     stop=(scn == 15),
                                     tile_position=(0, col), skip_group_check=True)
                    for j, _ in jp:
                        i2 = t.matmul(pbQ[32 * j:32 * j + 32, 2, :], onesb[:, :],
                                      expb[:, j, scn, :], start=(scn == 0),
                                      stop=(scn == 15),
                                      tile_position=(0, 32 * j),
                                      skip_group_check=True)
                    if last:
                        p.inc(i, "cm")
                        p.inc(i2, "cz")

                for scn in range(16):
                    qk_group(scn, ((0, qc0), (1, qc0)), "ce", ce0)
                    exp_group(scn, 0, "ce")
                    qk_group(scn, ((2, qc1), (3, qc1)), "cf", cf0)
                    exp_group(scn, 2, "cf")
                    if scn >= 1:
                        pv_z_group(scn - 1, ((0, 0), (1, 0)), False)
                        pv_z_group(scn - 1, ((2, 1), (3, 1)), False)
                p.wait(t, "ce", ce0 + 16)
                pv_z_group(15, ((0, 0), (1, 0)), False)
                p.wait(t, "cf", cf0 + 16)
                pv_z_group(15, ((2, 1), (3, 1)), True)
                p.wait(v, "cz")
                v.reciprocal(rz4[:, :], pbZR[:, 0, :])
                p.inc(v.drain(), "cz")
                p.wait(t, "cz")
                t.matmul(pbZR[0:64, 1, :], onesf[0:1, 0:64], rz4[0:1, :],
                         start=True, stop=True, tile_position=(0, 0),
                         skip_group_check=True)
                t.matmul(pbZR[64:128, 1, :], onesf[32:33, 0:64], rz4[32:33, :],
                         start=True, stop=True, tile_position=(32, 64),
                         skip_group_check=True)
                t.matmul(pbZR[0:64, 0, :], onesf[64:65, 0:64], rz4[64:65, :],
                         start=True, stop=True, tile_position=(64, 0),
                         skip_group_check=True)
                i = t.matmul(pbZR[64:128, 0, :], onesf[96:97, 0:64], rz4[96:97, :],
                             start=True, stop=True, tile_position=(96, 64),
                             skip_group_check=True)
                p.inc(i, "cm")
                p.wait(v, "cm")
                v.tensor_copy(rzb[:, 0, :], pbZR[:, 1, :])
                v.drain()
                v.tensor_copy(rzb[:, 1, :], pbZR[:, 0, :])
                v.drain()
                v.tensor_mul(aoT[:, qc0, nwin], pbPV[:, 0, :], rzb[:, 0, :])
                v.drain()
                v.tensor_mul(aoT[:, qc1, nwin], pbPV[:, 1, :], rzb[:, 1, :])
                v.drain()
                p.inc(v.nop(), "ce")

        p.pop("expb")
        Xs = p.push("X", [128, NC, N], F32)
        p.wait(sy, "ce")
        for c in range(NC):
            dma(sy, Xs[:, c, :], xspill[c * 128:(c + 1) * 128, :], "sl")
        wbuf = p.push("wbuf", [128, NC, NC, 128], BF16)
        p.wait(g, "ce")
        for kc in range(NC):
            for oc in range(NC):
                dma(g, wbuf[:, kc, oc, :],
                    wo[kc * 128:(kc + 1) * 128, oc * 128:(oc + 1) * 128], "gl")
        p.wait(t, "gl")
        banksA = [(pbS, w) for w in range(4)]
        banksB = [(pbQ, w) for w in range(4)]
        oc_cv2 = []
        for oc in range(NC):
            bs = banksA if oc % 2 == 0 else banksB
            if oc >= 2:
                p.wait(t, "cv")
            for kc in range(NC):
                for w in range(4):
                    i = t.matmul(bs[w][0][:, bs[w][1], :], wbuf[:, kc, oc, :],
                                 aoT[:, kc, w * NW:(w + 1) * NW],
                                 start=(kc == 0), stop=(kc == NC - 1))
            p.inc(i, "cm")
            p.wait(v, "cm")
            if oc == 0:
                p.wait(v, "sl")
            v.scalar_tensor_tensor(Xs[:, oc, :], bs[0][0][:, :, :],
                                   bos[:, oc:oc + 1], Xs[:, oc, :],
                                   ALU.add, ALU.add)
            v.drain()
            oc_cv2.append(p.inc(v.nop(), "cv"))
        p.wait(t, "cv")
        p.pop("wbuf")

        for c in range(NC):
            rec_ = sy if c % 2 == 0 else g
            rec_.wait_ge(p.sems["cv"], oc_cv2[c])
            dma(rec_, xout[s, c * 128:(c + 1) * 128, :], Xs[:, c, :],
                "so" if c % 2 == 0 else "go")
        stats_pass(Xs)
        p.wait(sy, "ca")
        dma(sy, stats3o[s, :, :],
            stat[:, 0:2, :].rearrange("p a b -> p (a b)"), "so")
        apply_pass(Xs, None, 1, absmax_base=0)
        v.drain()
        v.tensor_reduce(amx[:, 20 + s:21 + s], amx[:, 0:8], AX.X, ALU.max)
        p.inc(v.drain(), "cv")
        p.pop("X")
        p.pop("aoT")
        p.pop("kT")
        p.pop("qT")

    if l2_stop == "full":
        v.tensor_reduce(amx[:, 22:23], amx[:, 20:22], AX.X, ALU.max)
        p.inc(v.drain(), "cv")
        p.wait(sy, "cv")
        roundtrip_col_to_row(amx[:, 22:23])
        p.wait(v, "sr")
        v.tensor_reduce(srow[0:1, 0:1], row128[0:1, :], AX.X, ALU.max)
        p.inc(v.drain(), "cv")
        p.wait(sy, "cv")
        dma(sy, amax_out[0:1, :], srow[0:1, 0:1], "so")
    p.wait(g, "so")
    p.wait(g, "go")
    g.nop()
    p.finish()
    return nc


def host_prepare(inputs, n_cores=8):
    import ml_dtypes
    x = np.asarray(inputs["L0_attn_x"])
    Wq = np.asarray(inputs["L0_attn_Wq"])
    bq = np.asarray(inputs["L0_attn_bq"])
    Wk = np.asarray(inputs["L0_attn_Wk"])
    bk = np.asarray(inputs["L0_attn_bk"])
    Wv = np.asarray(inputs["L0_attn_Wv"])
    bv = np.asarray(inputs["L0_attn_bv"])
    Wo = np.asarray(inputs["L0_attn_Wo"])
    bo = np.asarray(inputs["L0_attn_bo"])
    Wf = np.asarray(inputs["L0_ffn_weight"])
    bf_ = np.asarray(inputs["L0_ffn_bias"])

    headperm = [h for pr in HEAD_PAIRS for h in pr]
    colperm = np.concatenate([np.arange(h * HD, (h + 1) * HD) for h in headperm])
    Wq_p = np.ascontiguousarray(Wq[:, colperm])
    bq_p = np.ascontiguousarray(bq[colperm]).reshape(NC, 128).T.copy()
    myrows = []
    for h in headperm:
        g, kv = h % 4, h // 4
        base = (g * 4 + kv) * HD
        myrows.extend(range(base, base + HD))
    Wo_p = np.ascontiguousarray(Wo[myrows, :])
    bkv_host = np.stack([bk.reshape(2, 128).T, bv.reshape(2, 128).T],
                        axis=1).reshape(128, 4)
    bkv_host = np.concatenate([bk.reshape(2, 128).T, bv.reshape(2, 128).T], axis=1)
    bo_host = bo.reshape(NC, 128).T.copy()
    fb_host = bf_.reshape(NC, 128).T.copy()
    WfT = np.ascontiguousarray(Wf.T)
    eye = np.eye(128, dtype=np.float32)

    per_core = []
    for c in range(n_cores):
        xc = x[2 * c:2 * c + 2]
        xT = np.ascontiguousarray(xc.transpose(0, 2, 1))
        per_core.append({
            "xT": xT, "Wq": Wq_p, "Wk": Wk, "Wv": Wv, "Wo": Wo_p,
            "bq": np.ascontiguousarray(bq_p), "bkv": np.ascontiguousarray(bkv_host),
            "bo": np.ascontiguousarray(bo_host), "WfT": WfT,
            "fbias": np.ascontiguousarray(fb_host), "eye": eye,
        })
    return per_core


def host_finish(results):
    outs = []
    for r in results:
        xT = r["xoutT"]
        outs.append(np.ascontiguousarray(xT.transpose(0, 2, 1)))
    return np.concatenate(outs, axis=0).astype(np.float32)

from concourse.bass_utils import run_bass_kernel_spmd

def kernel(**inputs):
    n_cores = 8
    percore = host_prepare(inputs, n_cores)
    core_ids = list(range(n_cores))

    nc1 = build_launch(1, n_cores)
    res1 = run_bass_kernel_spmd(nc1, [{"xT": pc["xT"]} for pc in percore],
                                core_ids=core_ids)
    amax1 = np.concatenate([res1.results[i]["amax_out"] for i in range(n_cores)],
                           axis=1)

    nc2 = build_launch(2, n_cores)
    in2 = [dict(pc, amaxv=amax1, stats1=res1.results[i]["stats1"])
           for i, pc in enumerate(percore)]
    res2 = run_bass_kernel_spmd(nc2, in2, core_ids=core_ids)
    amax3 = np.concatenate([res2.results[i]["amax_out"] for i in range(n_cores)],
                           axis=1)

    nc3 = build_launch(3, n_cores)
    in3 = [{"xT": res2.results[i]["x2T"], "WfT": pc["WfT"],
            "fbias": pc["fbias"], "amaxv": amax3,
            "stats3": res2.results[i]["stats3"]}
           for i, pc in enumerate(percore)]
    res3 = run_bass_kernel_spmd(nc3, in3, core_ids=core_ids)
    return host_finish([res3.results[i] for i in range(n_cores)])


# version 15 (final)
# speedup vs baseline: 1.0440x; 1.0073x over previous
import numpy as np
import concourse.bass as bass
import concourse.mybir as mybir
from contextlib import ExitStack

F32 = mybir.dt.float32
BF16 = mybir.dt.bfloat16
F8 = mybir.dt.float8e4
AF = mybir.ActivationFunctionType
ALU = mybir.AluOpType
AX = mybir.AxisListType

D, N, NC, NW = 1024, 2048, 8, 512
NH, NKV, HD = 16, 4, 64
EPS = 1e-5
HEAD_PAIRS = [(0, 4), (1, 5), (2, 6), (3, 7), (8, 12), (9, 13), (10, 14), (11, 15)]


class RecInstr:
    __slots__ = ("slot",)

    def __init__(self, slot):
        self.slot = slot

    def then_inc(self, sem, k):
        self.slot["inc"] = (sem, k)
        return self


class Rec:
    def __init__(self, name):
        self.name = name
        self.ops = []

    def __getattr__(self, meth):
        if meth.startswith("__"):
            raise AttributeError(meth)

        def call(*a, **k):
            slot = {}
            self.ops.append((meth, a, k, slot))
            return RecInstr(slot)
        return call


def _replay(rec):
    def f(eng):
        for meth, a, k, slot in rec.ops:
            r = getattr(eng, meth)(*a, **k)
            if "inc" in slot:
                r.then_inc(*slot["inc"])
    return f


class Prog:
    def __init__(self, nc):
        self.nc = nc
        self.es = ExitStack()
        self.counts = {}
        self.sems = {}
        self.sy, self.v, self.sc, self.t, self.g = (
            Rec("sync"), Rec("vector"), Rec("scalar"), Rec("tensor"), Rec("gpsimd"))

    def sb(self, name, shape, dt=F32):
        return self.es.enter_context(self.nc.sbuf_tensor(name, shape, dt))

    def push(self, name, shape, dt=F32):
        self._pushn = getattr(self, "_pushn", 0) + 1
        ctx = self.nc.sbuf_tensor(f"{name}_{self._pushn}", shape, dt)
        h = ctx.__enter__()
        self._stack = getattr(self, "_stack", [])
        self._stack.append((name, ctx))
        return h

    def pop(self, name):
        n, ctx = self._stack.pop()
        assert n == name, (n, name)
        ctx.__exit__(None, None, None)

    def ps(self, name, shape):
        return self.es.enter_context(self.nc.psum_tensor(name, shape, F32))

    def sem(self, name):
        s = self.es.enter_context(self.nc.semaphore(name))
        self.sems[name] = s
        self.counts[name] = 0

    def inc(self, instr, name, k=1):
        self.counts[name] += k
        instr.then_inc(self.sems[name], k)
        return self.counts[name]

    def wait(self, rec, name, val=None):
        rec.wait_ge(self.sems[name], self.counts[name] if val is None else val)

    def finish(self):
        with self.nc.Block() as block:
            block.sync(_replay(self.sy))
            block.vector(_replay(self.v))
            block.scalar(_replay(self.sc))
            block.tensor(_replay(self.t))
            block.gpsimd(_replay(self.g))
        self.es.close()


def build_launch(which, n_cores=8, l2_stop="full"):
    nc = bass.Bass("TRN2", target_bir_lowering=False)
    p = Prog(nc)
    sy, v, sc_, t, g = p.sy, p.v, p.sc, p.t, p.g

    xin = nc.dram_tensor("xT", [2, D, N], F32, kind="ExternalInput").ap()
    if which in (2, 3):
        wf = nc.dram_tensor("WfT", [D, D], F32, kind="ExternalInput").ap()
        fbias = nc.dram_tensor("fbias", [128, NC], F32, kind="ExternalInput").ap()
        amaxv = nc.dram_tensor("amaxv", [1, n_cores], F32, kind="ExternalInput").ap()
    if which == 2:
        wq = nc.dram_tensor("Wq", [D, D], F32, kind="ExternalInput").ap()
        wk = nc.dram_tensor("Wk", [D, NKV * HD], F32, kind="ExternalInput").ap()
        wv = nc.dram_tensor("Wv", [D, NKV * HD], F32, kind="ExternalInput").ap()
        wo = nc.dram_tensor("Wo", [D, D], F32, kind="ExternalInput").ap()
        bq = nc.dram_tensor("bq", [128, NC], F32, kind="ExternalInput").ap()
        bkv = nc.dram_tensor("bkv", [128, 4], F32, kind="ExternalInput").ap()
        bo = nc.dram_tensor("bo", [128, NC], F32, kind="ExternalInput").ap()
        eye = nc.dram_tensor("eye", [128, 128], F32, kind="ExternalInput").ap()
    if which in (1, 2):
        amax_out = nc.dram_tensor("amax_out", [1, 1], F32, kind="ExternalOutput").ap()
    if which == 1:
        stats1o = nc.dram_tensor("stats1", [2, 128, 2 * NW], F32,
                                 kind="ExternalOutput").ap()
    if which == 2:
        stats1i = nc.dram_tensor("stats1", [2, 128, 2 * NW], F32,
                                 kind="ExternalInput").ap()
        stats3o = nc.dram_tensor("stats3", [2, 128, 2 * NW], F32,
                                 kind="ExternalOutput").ap()
    if which == 3:
        stats3i = nc.dram_tensor("stats3", [2, 128, 2 * NW], F32,
                                 kind="ExternalInput").ap()
    if which == 2:
        xout = nc.dram_tensor("x2T", [2, D, N], F32, kind="ExternalOutput").ap()
    if which == 3:
        xout = nc.dram_tensor("xoutT", [2, D, N], F32, kind="ExternalOutput").ap()
    scr = nc.dram_tensor("scr", [1, 128], F32, kind="Internal").ap()
    scr3 = nc.dram_tensor("scr3", [3, 128], F32, kind="Internal").ap()
    if which == 2:
        xspill = nc.dram_tensor("xspill", [D, N], F32, kind="Internal").ap()

    onesf = p.sb("onesf", [128, 128], F32)
    onesb = p.sb("onesb", [128, 32], BF16)
    row128 = p.sb("row128", [1, 128], F32)
    row3 = p.sb("row3", [1, 3, 128], F32)
    srow = p.sb("srow", [1, 16], F32)
    sc0 = p.sb("sc0", [128, 8], F32)
    stat = p.sb("stat", [128, 3, NW], F32)
    amx = p.sb("amx", [128, 40], F32)
    cbias = p.sb("cbias", [128, 2], F32)
    scratch = p.sb("scratch", [128, N], F32)
    scratchB = p.sb("scratchB", [128, N], F32)

    nxbuf = 1 if which == 2 else 2
    if which != 2:
        X = [p.sb(f"X{i}", [128, NC, N], F32) for i in range(nxbuf)]

    if which in (2, 3):
        wf8 = p.sb("wf8", [128, NC, NC, 128], F8)
        fb = p.sb("fbs", [128, NC], F32)
        fbh = p.sb("fbh", [128, NC], F32)
        ybh2 = p.sb("ybh", [128, 2, 4, NW], BF16)
    if which == 2:
        vtm = p.sb("vtm", [128, 16, 256], BF16)
        eyeb = p.sb("eyeb", [128, 128], F32)
        rz4 = p.sb("rz4", [128, NW], F32)
        rzb = p.sb("rzb", [128, 2, NW], F32)
        bqs = p.sb("bqs", [128, NC], F32)
        bkvs = p.sb("bkvs", [128, 4], F32)
        bos = p.sb("bos", [128, NC], F32)
    if which == 2:
        wfs = p.push("wfs", [128, NC, NC, 128], F32)
    if which == 3:
        wfsc = p.sb("wfsc", [128, 2, NC, 128], F32)

    pbS = p.ps("pbS", [128, 4, NW])
    pbQ = p.ps("pbQ", [128, 4, NW])
    pbPV = pbQ[:, 0:2]
    pbZR = pbQ[:, 2:4]

    for s_ in ["sl", "sr", "so", "gl", "cv", "ca", "cm", "cq", "ce", "cz", "sp", "cf", "cn", "w0", "w1", "go", "sq"]:
        p.sem(s_)

    def dma(rec, out, in_, semname):
        p.counts[semname] += 16
        rec.dma_start(out=out, in_=in_).then_inc(p.sems[semname], 16)

    def roundtrip_col_to_row(col_ap):
        p.wait(sy, "cv")
        dma(sy, scr[0:1, :], col_ap, "sr")
        p.wait(sy, "sr")
        dma(sy, row128[0:1, :], scr[0:1, :], "sr")

    def row_to_allparts(src11, dst_col):
        p.wait(v, "sr")
        v.tensor_scalar(row128[0:1, :], onesf[0:1, 0:128], src11, None, ALU.mult)
        p.inc(v.drain(), "cv")
        p.wait(sy, "cv")
        dma(sy, scr[0:1, :], row128[0:1, :], "sr")
        p.wait(sy, "sr")
        dma(sy, dst_col, scr[0:1, :], "sr")

    def stats_pass(Xs):
        cm0 = p.counts["cm"]
        cv0 = p.counts["cv"]
        bufs = [scratch, scratchB]
        for c in range(NC):
            if c >= 2:
                p.wait(v, "cm", cm0 + c - 1)
            v.tensor_mul(bufs[c % 2][:, 0:N], Xs[:, c, :], Xs[:, c, :])
            p.inc(v.drain(), "cv")
            p.wait(t, "cv", cv0 + c + 1)
            for w in range(4):
                t.matmul(pbS[32 * w:32 * w + 32, 0, :], onesf[:, 0:32],
                         Xs[:, c, w * NW:(w + 1) * NW], start=(c == 0),
                         stop=(c == NC - 1), tile_position=(0, 32 * w),
                         skip_group_check=True)
            for w in range(4):
                i = t.matmul(pbS[32 * w:32 * w + 32, 1, :], onesf[:, 0:32],
                             bufs[c % 2][:, w * NW:(w + 1) * NW], start=(c == 0),
                             stop=(c == NC - 1), tile_position=(0, 32 * w),
                             skip_group_check=True)
            p.inc(i, "cm")
        p.wait(v, "cm")
        v.tensor_scalar(stat[:, 0, :], pbS[:, 0, :], 1.0 / D, None, ALU.mult)
        v.drain()
        v.tensor_mul(scratch[:, 0:NW], stat[:, 0, :], stat[:, 0, :])
        v.drain()
        v.tensor_scalar(stat[:, 2, :], pbS[:, 1, :], 1.0 / D, None, ALU.mult)
        v.drain()
        v.tensor_sub(stat[:, 2, :], stat[:, 2, :], scratch[:, 0:NW])
        p.inc(v.drain(), "cv")
        p.wait(sc_, "cv")
        sc_.activation(stat[:, 2, :], stat[:, 2, :], AF.Ln, bias=cbias[:, 0:1])
        sc_.drain()
        p.inc(sc_.activation(stat[:, 1, :], stat[:, 2, :], AF.Exp, scale=-0.5), "ca")
        p.wait(v, "ca")
        p.wait(t, "ca")

    def apply_pass(Xs, dst_fn, row1, absmax_base=None):
        for w in range(4):
            t.matmul(pbS[:, w, :], onesf[32 * w:32 * w + 1, :],
                     stat[32 * w:32 * w + 1, 0, :], start=True, stop=True,
                     tile_position=(32 * w, 0), skip_group_check=True)
            i = t.matmul(pbQ[:, w, :], onesf[32 * w:32 * w + 1, :],
                         stat[32 * w:32 * w + 1, row1, :], start=True, stop=True,
                         tile_position=(32 * w, 0), skip_group_check=True)
        p.inc(i, "cm")
        p.wait(v, "cm")
        chunk_cv = []
        bufs = [scratch, scratchB]
        for c in range(NC):
            b_ = bufs[c % 2]
            if c >= 2:
                v.drain()
            v.tensor_sub(b_[:, 0:N], Xs[:, c, :],
                         pbS[:, :, :].rearrange("p a b -> p (a b)"))
            v.drain()
            if absmax_base is None:
                i2 = v.tensor_mul(dst_fn(c), b_[:, 0:N],
                                  pbQ[:, :, :].rearrange("p a b -> p (a b)"))
                chunk_cv.append(p.inc(i2, "cv"))
            else:
                v.tensor_mul(b_[:, 0:N], b_[:, 0:N],
                             pbQ[:, :, :].rearrange("p a b -> p (a b)"))
                v.drain()
                k = absmax_base + c
                v.tensor_reduce(amx[:, k:k + 1], b_[:, 0:N], AX.X,
                                ALU.max, apply_absolute_value=True)
        if absmax_base is not None:
            p.inc(v.nop(), "cv")
        p.wait(t, "cv")
        return chunk_cv

    def ffn_matmuls(xq_s, Xs):
        oc_cv = []
        banksA = [(pbS, w) for w in range(4)]
        banksB = [(pbQ, w) for w in range(4)]
        for oc in range(NC):
            bs = banksA if oc % 2 == 0 else banksB
            if oc >= 2:
                p.wait(t, "cv")
            for kc in range(NC):
                for w in range(4):
                    i = t.matmul(bs[w][0][:, bs[w][1], :], wf8[:, kc, oc, :],
                                 xq_s[:, kc, w * NW:(w + 1) * NW],
                                 start=(kc == 0), stop=(kc == NC - 1))
            p.inc(i, "cm")
            p.wait(sc_, "cm")
            if oc >= 2:
                p.wait(sc_, "cv")
            i2 = sc_.activation(ybh2[:, oc % 2, :, :], bs[0][0][:, :, :], AF.Identity,
                                bias=fbh[:, oc:oc + 1], scale=sc0[:, 1:2])
            p.inc(i2, "ca")
            p.wait(v, "ca")
            v.tensor_add(Xs[:, oc, :], Xs[:, oc, :],
                         ybh2[:, oc % 2, :, :].rearrange("p a b -> p (a b)"))
            v.drain()
            oc_cv.append(p.inc(v.nop(), "cv"))
        p.wait(t, "cv")
        return oc_cv

    if which == 2:
        dma(sy, eyeb[:], eye[:, :], "sl")
    if which == 3:
        dma(sy, fb[:, :], fbias[:, :], "sp")
        dma(sy, amx[0:1, 24:24 + n_cores], amaxv[0:1, :], "sp")
    if which != 2:
        for i_ in range(nxbuf):
            for c in range(NC):
                rec_ = sy if (i_ * NC + c) % 2 == 0 else g
                dma(rec_, X[i_][:, c, :], xin[i_, c * 128:(c + 1) * 128, :],
                    "sl" if rec_ is sy else "gl")
    if which == 2:
        dma(sy, fb[:, :], fbias[:, :], "sl")
        for c in range(NC):
            dma(sy, wfs[:, c, :, :].rearrange("p a b -> p (a b)"),
                wf[c * 128:(c + 1) * 128, :], "sl")
        dma(sy, amx[0:1, 24:24 + n_cores], amaxv[0:1, :], "sl")
        dma(sy, bqs[:, :], bq[:, :], "sl")
        dma(sy, bkvs[:, :], bkv[:, :], "sl")
        dma(sy, bos[:, :], bo[:, :], "sl")

    if which != 3:
        p.wait(v, "sl")
    if which == 1:
        p.wait(v, "gl")
        p.wait(t, "gl")
    v.memset(onesf[:], 1.0)
    v.memset(onesb[:], 1.0)
    v.memset(cbias[:, 0:1], EPS)
    p.inc(v.drain(), "cv")
    p.wait(t, "cv")
    p.wait(sc_, "cv")
    p.wait(sy, "cv")

    if which in (2, 3):
        if which == 3:
            p.wait(v, "sp")
        v.tensor_scalar(fbh[:, :], fb[:, :], 0.5, None, ALU.mult)
        v.drain()
        if which == 2:
            v.tensor_reduce(amx[:, 17:18],
                            wfs[:, :, :, :].rearrange("p a b c -> p (a b c)"),
                            AX.X, ALU.max, apply_absolute_value=True)
            p.inc(v.drain(), "cv")
        else:
            cv0w = p.counts["cv"]
            for c in range(NC):
                if c >= 2:
                    p.wait(sc_, "cv", cv0w + c - 1)
                sem_c = "w0" if c % 2 == 0 else "w1"
                dma(sc_, wfsc[:, c % 2, :, :].rearrange("p a b -> p (a b)"),
                    wf[c * 128:(c + 1) * 128, :], sem_c)
                p.wait(v, sem_c)
                v.tensor_reduce(amx[:, 8 + c:9 + c],
                                wfsc[:, c % 2, :, :].rearrange("p a b -> p (a b)"),
                                AX.X, ALU.max, apply_absolute_value=True)
                p.inc(v.drain(), "cv")
            v.tensor_reduce(amx[:, 17:18], amx[:, 8:16], AX.X, ALU.max)
            p.inc(v.drain(), "cv")
        p.wait(sy, "cv")
        roundtrip_col_to_row(amx[:, 17:18])
        p.wait(v, "sr")
        v.tensor_reduce(srow[0:1, 0:1], row128[0:1, :], AX.X, ALU.max)
        v.drain()
        v.tensor_scalar(srow[0:1, 1:2], srow[0:1, 0:1], 1e-12, None, ALU.max)
        v.drain()
        v.reciprocal(srow[0:1, 2:3], srow[0:1, 1:2])
        v.drain()
        v.tensor_scalar(srow[0:1, 3:4], srow[0:1, 2:3], 448.0, 448.0, ALU.mult, ALU.min)
        v.drain()
        v.reciprocal(srow[0:1, 4:5], srow[0:1, 3:4])
        v.drain()
        v.tensor_scalar(srow[0:1, 5:6], srow[0:1, 3:4], 0.5, None, ALU.mult)
        v.drain()
        v.tensor_reduce(srow[0:1, 6:7], amx[0:1, 24:24 + n_cores], AX.X, ALU.max)
        v.drain()
        v.tensor_scalar(srow[0:1, 7:8], srow[0:1, 6:7], 1e-12, None, ALU.max)
        v.drain()
        v.reciprocal(srow[0:1, 8:9], srow[0:1, 7:8])
        v.drain()
        v.tensor_scalar(srow[0:1, 9:10], srow[0:1, 8:9], 448.0, 448.0, ALU.mult, ALU.min)
        v.drain()
        v.reciprocal(srow[0:1, 10:11], srow[0:1, 9:10])
        v.drain()
        v.tensor_scalar(srow[0:1, 11:12], srow[0:1, 9:10], 0.5, None, ALU.mult)
        v.drain()
        v.tensor_mul(srow[0:1, 12:13], srow[0:1, 10:11], srow[0:1, 4:5])
        v.drain()
        v.tensor_scalar(srow[0:1, 13:14], srow[0:1, 12:13], 2.0, None, ALU.mult)
        v.drain()
        p.wait(v, "sr")
        v.tensor_scalar(row3[0:1, 0, :], onesf[0:1, 0:128], srow[0:1, 5:6], None, ALU.mult)
        v.tensor_scalar(row3[0:1, 1, :], onesf[0:1, 0:128], srow[0:1, 11:12], None, ALU.mult)
        v.tensor_scalar(row3[0:1, 2, :], onesf[0:1, 0:128], srow[0:1, 13:14], None, ALU.mult)
        p.inc(v.drain(), "cv")
        p.wait(sy, "cv")
        dma(sy, scr3[:, :], row3[0:1, :, :], "sr")
        p.wait(sy, "sr")
        dma(sy, sc0[:, 4:5], scr3[0:1, :], "sr")
        dma(sy, sc0[:, 0:1], scr3[1:2, :], "sr")
        dma(sy, sc0[:, 1:2], scr3[2:3, :], "sr")
        p.wait(v, "sr")
        if which == 2:
            v.tensor_scalar(wf8[:, :, :, :].rearrange("p a b c -> p (a b c)"),
                            wfs[:, :, :, :].rearrange("p a b c -> p (a b c)"),
                            sc0[:, 4:5], None, ALU.mult)
            p.inc(v.drain(), "cq")
            p.wait(t, "cq")
            p.pop("wfs")
        else:
            p.wait(sc_, "cv")
            cq0w = p.counts["cq"]
            for c in range(NC):
                if c >= 2:
                    p.wait(sc_, "cq", cq0w + c - 1)
                sem_c = "w0" if c % 2 == 0 else "w1"
                dma(sc_, wfsc[:, c % 2, :, :].rearrange("p a b -> p (a b)"),
                    wf[c * 128:(c + 1) * 128, :], sem_c)
                p.wait(v, sem_c)
                v.tensor_scalar(wf8[:, c, :, :].rearrange("p a b -> p (a b)"),
                                wfsc[:, c % 2, :, :].rearrange("p a b -> p (a b)"),
                                sc0[:, 4:5], None, ALU.mult)
                p.inc(v.drain(), "cq")
            p.wait(t, "cq")

    if which == 1:
        for s in range(2):
            if s == 1:
                p.wait(v, "so")
                p.wait(sc_, "so")
            stats_pass(X[s])
            p.wait(sy, "ca")
            dma(sy, stats1o[s, :, :],
                stat[:, 0:2, :].rearrange("p a b -> p (a b)"), "so")
            apply_pass(X[s], None, 1, absmax_base=0)
            v.drain()
            v.tensor_reduce(amx[:, 20 + s:21 + s], amx[:, 0:8], AX.X, ALU.max)
            p.inc(v.drain(), "cv")
        v.tensor_reduce(amx[:, 22:23], amx[:, 20:22], AX.X, ALU.max)
        p.inc(v.drain(), "cv")
        p.wait(sy, "cv")
        roundtrip_col_to_row(amx[:, 22:23])
        p.wait(v, "sr")
        v.tensor_reduce(srow[0:1, 0:1], row128[0:1, :], AX.X, ALU.max)
        p.inc(v.drain(), "cv")
        p.wait(sy, "cv")
        dma(sy, amax_out[0:1, :], srow[0:1, 0:1], "so")
        p.wait(g, "so")
        g.nop()
        p.finish()
        return nc

    if which == 3:
        for s in range(2):
            xq_s = p.push("xq", [128, NC, N], F8)
            if p.counts["cm"]:
                p.wait(sc_, "cm")
            dma(sc_, stat[:, 0:2, :].rearrange("p a b -> p (a b)"),
                stats3i[s, :, :], "sq")
            p.wait(v, "sq")
            v.tensor_scalar(stat[:, 2, :], stat[:, 1, :], sc0[:, 0:1], None, ALU.mult)
            p.inc(v.drain(), "ca")
            p.wait(t, "ca")
            p.wait(t, "sq")
            p.wait(v, "sl")
            p.wait(v, "gl")
            apply_pass(X[s], lambda c, xq_s=xq_s: xq_s[:, c, :], 2)
            oc_cv = ffn_matmuls(xq_s, X[s])
            p.pop("xq")
            for c in range(NC):
                rec_ = sy if c % 2 == 0 else g
                rec_.wait_ge(p.sems["cv"], oc_cv[c])
                dma(rec_, xout[s, c * 128:(c + 1) * 128, :], X[s][:, c, :],
                    "so" if c % 2 == 0 else "go")
        p.wait(g, "so")
        p.wait(g, "go")
        g.nop()
        p.finish()
        return nc

    for s in range(2):
        qT = p.push("qT", [128, NC, N], BF16)
        kT = p.push("kT", [128, 2, N], BF16)
        xnb = p.push("xnb", [128, NC, N], BF16)
        Xs = p.push("X", [128, NC, N], F32)
        if s == 1:
            p.wait(sy, "so")
            p.wait(sy, "go")
            p.wait(sy, "cv")
            p.wait(g, "so")
            p.wait(g, "go")
            p.wait(g, "cv")
        for c in range(NC):
            rec_ = sy if c % 2 == 0 else g
            dma(rec_, Xs[:, c, :], xin[s, c * 128:(c + 1) * 128, :],
                "sl" if rec_ is sy else "gl")
        if p.counts["cm"]:
            p.wait(sy, "cm")
        dma(sy, stat[:, 0:2, :].rearrange("p a b -> p (a b)"),
            stats1i[s, :, :], "sl")
        p.wait(v, "sl")
        p.wait(t, "sl")
        p.wait(v, "gl")
        p.wait(t, "gl")
        if s == 1:
            p.wait(v, "so")
            p.wait(t, "so")
            p.wait(v, "go")
            p.wait(t, "go")
        xq_s = p.push("xq", [128, NC, N], F8)
        v.tensor_scalar(stat[:, 2, :], stat[:, 1, :], sc0[:, 0:1], None, ALU.mult)
        p.inc(v.drain(), "ca")
        p.wait(t, "ca")
        apply_pass(Xs, lambda c: xq_s[:, c, :], 2)
        if l2_stop != "ln1":
            ffn_matmuls(xq_s, Xs)
        p.pop("xq")
        if l2_stop in ("ln1", "ffn1"):
            p.wait(sy, "cv")
            for c in range(NC):
                dma(sy, xout[s, c * 128:(c + 1) * 128, :], Xs[:, c, :], "so")
            p.pop("X")
            p.pop("xnb")
            p.pop("kT")
            p.pop("qT")
            continue
        stats_pass(Xs)
        ch_cv = apply_pass(Xs, lambda c: xnb[:, c, :], 1)
        for c in range(NC):
            rec_ = sy if c % 2 == 0 else sc_
            rec_.wait_ge(p.sems["cv"], ch_cv[c])
            dma(rec_, xspill[c * 128:(c + 1) * 128, :], Xs[:, c, :], "sp")
        p.pop("X")
        wbuf = p.push("wbuf", [128, NC, NC, 128], BF16)
        wkvb = p.push("wkvb", [128, NC, 4, 128], BF16)
        p.wait(g, "sp")
        if s == 1:
            p.wait(g, "cm")
        else:
            p.wait(g, "cq")
        for kc in range(NC):
            for oc in range(NC):
                dma(g, wbuf[:, kc, oc, :],
                    wq[kc * 128:(kc + 1) * 128, oc * 128:(oc + 1) * 128], "gl")
        for kc in range(NC):
            dma(g, wkvb[:, kc, 0:2, :].rearrange("p a b -> p (a b)"),
                wk[kc * 128:(kc + 1) * 128, :], "gl")
            dma(g, wkvb[:, kc, 2:4, :].rearrange("p a b -> p (a b)"),
                wv[kc * 128:(kc + 1) * 128, :], "gl")
        p.wait(t, "gl")

        def proj(n_oc, wsl, dstT, bias_fn, oc_base=0):
            banksA = [(pbS, w) for w in range(4)]
            banksB = [(pbQ, w) for w in range(4)]
            for oc in range(n_oc):
                bs = banksA if oc % 2 == 0 else banksB
                if oc >= 2:
                    p.wait(t, "cv")
                for kc in range(NC):
                    for w in range(4):
                        i = t.matmul(bs[w][0][:, bs[w][1], :], wsl(kc, oc),
                                     xnb[:, kc, w * NW:(w + 1) * NW],
                                     start=(kc == 0), stop=(kc == NC - 1))
                p.inc(i, "cm")
                p.wait(sc_, "cm")
                i2 = sc_.activation(dstT[:, oc_base + oc, :], bs[0][0][:, :, :],
                                    AF.Identity, bias=bias_fn(oc))
                p.inc(i2, "cv")
            p.wait(t, "cv")

        proj(NC, lambda kc, oc: wbuf[:, kc, oc, :], qT, lambda oc: bqs[:, oc:oc + 1])
        proj(2, lambda kc, oc: wkvb[:, kc, oc, :], kT, lambda oc: bkvs[:, oc:oc + 1])
        vT = p.push("vT", [128, 2, N], F32)
        proj(2, lambda kc, oc: wkvb[:, kc, 2 + oc, :], vT,
             lambda oc: bkvs[:, 2 + oc:3 + oc])

        for kvc in range(2):
            for scn in range(16):
                bk_ = scn % 2
                if kvc * 16 + scn >= 2:
                    p.wait(t, "cv")
                i = t.transpose(pbZR[0:128, bk_, 0:128],
                                vT[:, kvc, scn * 128:(scn + 1) * 128], eyeb[:, :])
                p.inc(i, "cm")
                p.wait(v, "cm")
                v.tensor_copy(vtm[:, scn, kvc * 128:(kvc + 1) * 128],
                              pbZR[:, bk_, 0:128])
                v.drain()
                p.inc(v.nop(), "cv")
        p.wait(t, "cv")
        p.pop("vT")

        for c in range(NC):
            if c > 0:
                p.wait(v, "cm")
            v.tensor_mul(scratch[:, 0:N], qT[:, c, :], qT[:, c, :])
            p.inc(v.drain(), "cv")
            p.wait(t, "cv")
            slot = c % 4
            tgt = [(pbS, w) for w in range(4)]
            for w in range(4):
                i = t.matmul(tgt[w][0][32 * slot:32 * slot + 32, tgt[w][1], :],
                             onesf[:, 0:32], scratch[:, w * NW:(w + 1) * NW],
                             start=(c < 4), stop=(c >= 4),
                             tile_position=(0, 32 * slot), skip_group_check=True)
            p.inc(i, "cm")
        p.wait(v, "cm")
        v.tensor_reduce(amx[:, 0:1], pbS[:, :, :].rearrange("p a b -> p (a b)"),
                        AX.X, ALU.max)
        p.inc(v.drain(), "cv")
        p.wait(t, "cv")
        for c in range(2):
            if c > 0:
                p.wait(v, "cm")
            v.tensor_mul(scratch[:, 0:N], kT[:, c, :], kT[:, c, :])
            p.inc(v.drain(), "cv")
            p.wait(t, "cv")
            for w in range(4):
                i = t.matmul(pbPV[32 * (c * 2 + w // 2):32 * (c * 2 + w // 2) + 32,
                                  w % 2, :],
                             onesf[:, 0:32], scratch[:, w * NW:(w + 1) * NW],
                             start=True, stop=True,
                             tile_position=(0, 32 * (c * 2 + w // 2)),
                             skip_group_check=True)
            p.inc(i, "cm")
        p.wait(v, "cm")
        v.tensor_reduce(amx[:, 1:2], pbPV[:, :, :].rearrange("p a b -> p (a b)"),
                        AX.X, ALU.max)
        p.inc(v.drain(), "cv")
        p.wait(sy, "cv")
        roundtrip_col_to_row(amx[:, 0:1])
        p.wait(v, "sr")
        v.tensor_reduce(srow[0:1, 14:15], row128[0:1, :], AX.X, ALU.max)
        p.inc(v.drain(), "cv")
        p.wait(sy, "cv")
        roundtrip_col_to_row(amx[:, 1:2])
        p.wait(v, "sr")
        v.tensor_reduce(srow[0:1, 15:16], row128[0:1, :], AX.X, ALU.max)
        v.drain()
        v.tensor_mul(srow[0:1, 14:15], srow[0:1, 14:15], srow[0:1, 15:16])
        p.inc(v.drain(), "cv")
        p.wait(sc_, "cv")
        sc_.activation(srow[0:1, 14:15], srow[0:1, 14:15], AF.Ln)
        sc_.drain()
        p.inc(sc_.activation(srow[0:1, 14:15], srow[0:1, 14:15], AF.Exp, scale=0.5),
              "ca")
        p.wait(v, "ca")
        v.tensor_scalar(srow[0:1, 14:15], srow[0:1, 14:15], -0.125, None, ALU.mult)
        p.inc(v.drain(), "cv")
        row_to_allparts(srow[0:1, 14:15], sc0[:, 2 + s:3 + s])
        p.wait(sc_, "sr")
        p.wait(t, "sr")

        p.pop("wkvb")
        p.pop("wbuf")
        p.pop("xnb")
        aoT = p.push("aoT", [128, NC, N], BF16)
        expb = p.push("expb", [128, 4, 16, NW], BF16)
        p.wait(sc_, "sp")
        p.wait(v, "sp")
        p.wait(t, "sp")

        for b in range(4):
            qc0, qc1 = 2 * b, 2 * b + 1
            ktc = qc0 // 4
            kvs = [HEAD_PAIRS[qc0][0] // 4, HEAD_PAIRS[qc0][1] // 4,
                   HEAD_PAIRS[qc1][0] // 4, HEAD_PAIRS[qc1][1] // 4]
            for w in range(4):
                nwin = slice(w * NW, (w + 1) * NW)
                ce0 = p.counts["ce"]
                cf0 = p.counts["cf"]

                def qk_group(scn, js, sem, base):
                    s0 = scn * 128
                    if scn >= 1:
                        p.wait(t, sem, base + scn)
                    for j, qc in js:
                        lohalf = (j % 2) * 64
                        i = t.matmul(pbS[0:128, j, :],
                                     kT[lohalf:lohalf + 64, ktc, s0:s0 + 128],
                                     qT[lohalf:lohalf + 64, qc, nwin],
                                     start=True, stop=True,
                                     tile_position=(lohalf, 0),
                                     skip_group_check=True)
                    p.inc(i, "cm")

                def exp_group(scn, j0, sem):
                    p.wait(sc_, "cm")
                    i = sc_.activation(expb[:, j0:j0 + 2, scn, :],
                                       pbS[:, j0:j0 + 2, :], AF.Exp,
                                       bias=sc0[:, 2 + s:3 + s], scale=0.125)
                    p.inc(i, sem)

                def pv_z_group(scn, jp, last):
                    for j, pvb in jp:
                        col = (j % 2) * 64
                        i = t.matmul(pbQ[col:col + 64, pvb, :],
                                     vtm[:, scn, kvs[j] * 64:kvs[j] * 64 + 64],
                                     expb[:, j, scn, :], start=(scn == 0),
                                     stop=(scn == 15),
                                     tile_position=(0, col), skip_group_check=True)
                    for j, _ in jp:
                        i2 = t.matmul(pbQ[32 * j:32 * j + 32, 2, :], onesb[:, :],
                                      expb[:, j, scn, :], start=(scn == 0),
                                      stop=(scn == 15),
                                      tile_position=(0, 32 * j),
                                      skip_group_check=True)
                    if last:
                        p.inc(i, "cm")
                        p.inc(i2, "cz")

                for scn in range(16):
                    qk_group(scn, ((0, qc0), (1, qc0)), "ce", ce0)
                    exp_group(scn, 0, "ce")
                    qk_group(scn, ((2, qc1), (3, qc1)), "cf", cf0)
                    exp_group(scn, 2, "cf")
                    if scn >= 1:
                        pv_z_group(scn - 1, ((0, 0), (1, 0)), False)
                        pv_z_group(scn - 1, ((2, 1), (3, 1)), False)
                p.wait(t, "ce", ce0 + 16)
                pv_z_group(15, ((0, 0), (1, 0)), False)
                p.wait(t, "cf", cf0 + 16)
                pv_z_group(15, ((2, 1), (3, 1)), True)
                p.wait(v, "cz")
                v.reciprocal(rz4[:, :], pbZR[:, 0, :])
                p.inc(v.drain(), "cz")
                p.wait(t, "cz")
                t.matmul(pbZR[0:64, 1, :], onesf[0:1, 0:64], rz4[0:1, :],
                         start=True, stop=True, tile_position=(0, 0),
                         skip_group_check=True)
                t.matmul(pbZR[64:128, 1, :], onesf[32:33, 0:64], rz4[32:33, :],
                         start=True, stop=True, tile_position=(32, 64),
                         skip_group_check=True)
                t.matmul(pbZR[0:64, 0, :], onesf[64:65, 0:64], rz4[64:65, :],
                         start=True, stop=True, tile_position=(64, 0),
                         skip_group_check=True)
                i = t.matmul(pbZR[64:128, 0, :], onesf[96:97, 0:64], rz4[96:97, :],
                             start=True, stop=True, tile_position=(96, 64),
                             skip_group_check=True)
                p.inc(i, "cm")
                p.wait(v, "cm")
                v.tensor_copy(rzb[:, 0, :], pbZR[:, 1, :])
                v.drain()
                v.tensor_copy(rzb[:, 1, :], pbZR[:, 0, :])
                v.drain()
                v.tensor_mul(aoT[:, qc0, nwin], pbPV[:, 0, :], rzb[:, 0, :])
                v.drain()
                v.tensor_mul(aoT[:, qc1, nwin], pbPV[:, 1, :], rzb[:, 1, :])
                v.drain()
                p.inc(v.nop(), "ce")

        p.pop("expb")
        Xs = p.push("X", [128, NC, N], F32)
        p.wait(sy, "ce")
        for c in range(NC):
            dma(sy, Xs[:, c, :], xspill[c * 128:(c + 1) * 128, :], "sl")
        wbuf = p.push("wbuf", [128, NC, NC, 128], BF16)
        p.wait(g, "ce")
        for kc in range(NC):
            for oc in range(NC):
                dma(g, wbuf[:, kc, oc, :],
                    wo[kc * 128:(kc + 1) * 128, oc * 128:(oc + 1) * 128], "gl")
        p.wait(t, "gl")
        banksA = [(pbS, w) for w in range(4)]
        banksB = [(pbQ, w) for w in range(4)]
        oc_cv2 = []
        for oc in range(NC):
            bs = banksA if oc % 2 == 0 else banksB
            if oc >= 2:
                p.wait(t, "cv")
            for kc in range(NC):
                for w in range(4):
                    i = t.matmul(bs[w][0][:, bs[w][1], :], wbuf[:, kc, oc, :],
                                 aoT[:, kc, w * NW:(w + 1) * NW],
                                 start=(kc == 0), stop=(kc == NC - 1))
            p.inc(i, "cm")
            p.wait(v, "cm")
            if oc == 0:
                p.wait(v, "sl")
            v.scalar_tensor_tensor(Xs[:, oc, :], bs[0][0][:, :, :],
                                   bos[:, oc:oc + 1], Xs[:, oc, :],
                                   ALU.add, ALU.add)
            v.drain()
            oc_cv2.append(p.inc(v.nop(), "cv"))
        p.wait(t, "cv")
        p.pop("wbuf")

        for c in range(NC):
            rec_ = sy if c % 2 == 0 else g
            rec_.wait_ge(p.sems["cv"], oc_cv2[c])
            dma(rec_, xout[s, c * 128:(c + 1) * 128, :], Xs[:, c, :],
                "so" if c % 2 == 0 else "go")
        stats_pass(Xs)
        p.wait(sy, "ca")
        dma(sy, stats3o[s, :, :],
            stat[:, 0:2, :].rearrange("p a b -> p (a b)"), "so")
        apply_pass(Xs, None, 1, absmax_base=0)
        v.drain()
        v.tensor_reduce(amx[:, 20 + s:21 + s], amx[:, 0:8], AX.X, ALU.max)
        p.inc(v.drain(), "cv")
        p.pop("X")
        p.pop("aoT")
        p.pop("kT")
        p.pop("qT")

    if l2_stop == "full":
        v.tensor_reduce(amx[:, 22:23], amx[:, 20:22], AX.X, ALU.max)
        p.inc(v.drain(), "cv")
        p.wait(sy, "cv")
        roundtrip_col_to_row(amx[:, 22:23])
        p.wait(v, "sr")
        v.tensor_reduce(srow[0:1, 0:1], row128[0:1, :], AX.X, ALU.max)
        p.inc(v.drain(), "cv")
        p.wait(sy, "cv")
        dma(sy, amax_out[0:1, :], srow[0:1, 0:1], "so")
    p.wait(g, "so")
    p.wait(g, "go")
    g.nop()
    p.finish()
    return nc


def host_prepare(inputs, n_cores=8):
    import ml_dtypes
    x = np.asarray(inputs["L0_attn_x"])
    Wq = np.asarray(inputs["L0_attn_Wq"])
    bq = np.asarray(inputs["L0_attn_bq"])
    Wk = np.asarray(inputs["L0_attn_Wk"])
    bk = np.asarray(inputs["L0_attn_bk"])
    Wv = np.asarray(inputs["L0_attn_Wv"])
    bv = np.asarray(inputs["L0_attn_bv"])
    Wo = np.asarray(inputs["L0_attn_Wo"])
    bo = np.asarray(inputs["L0_attn_bo"])
    Wf = np.asarray(inputs["L0_ffn_weight"])
    bf_ = np.asarray(inputs["L0_ffn_bias"])

    headperm = [h for pr in HEAD_PAIRS for h in pr]
    colperm = np.concatenate([np.arange(h * HD, (h + 1) * HD) for h in headperm])
    Wq_p = np.ascontiguousarray(Wq[:, colperm])
    bq_p = np.ascontiguousarray(bq[colperm]).reshape(NC, 128).T.copy()
    myrows = []
    for h in headperm:
        g, kv = h % 4, h // 4
        base = (g * 4 + kv) * HD
        myrows.extend(range(base, base + HD))
    Wo_p = np.ascontiguousarray(Wo[myrows, :])
    bkv_host = np.stack([bk.reshape(2, 128).T, bv.reshape(2, 128).T],
                        axis=1).reshape(128, 4)
    bkv_host = np.concatenate([bk.reshape(2, 128).T, bv.reshape(2, 128).T], axis=1)
    bo_host = bo.reshape(NC, 128).T.copy()
    fb_host = bf_.reshape(NC, 128).T.copy()
    WfT = np.ascontiguousarray(Wf.T)
    eye = np.eye(128, dtype=np.float32)

    per_core = []
    for c in range(n_cores):
        xc = x[2 * c:2 * c + 2]
        xT = np.ascontiguousarray(xc.transpose(0, 2, 1))
        per_core.append({
            "xT": xT, "Wq": Wq_p, "Wk": Wk, "Wv": Wv, "Wo": Wo_p,
            "bq": np.ascontiguousarray(bq_p), "bkv": np.ascontiguousarray(bkv_host),
            "bo": np.ascontiguousarray(bo_host), "WfT": WfT,
            "fbias": np.ascontiguousarray(fb_host), "eye": eye,
        })
    return per_core


def host_finish(results):
    outs = []
    for r in results:
        xT = r["xoutT"]
        outs.append(np.ascontiguousarray(xT.transpose(0, 2, 1)))
    return np.concatenate(outs, axis=0).astype(np.float32)

from concourse.bass_utils import run_bass_kernel_spmd

def kernel(**inputs):
    n_cores = 8
    percore = host_prepare(inputs, n_cores)
    core_ids = list(range(n_cores))

    nc1 = build_launch(1, n_cores)
    res1 = run_bass_kernel_spmd(nc1, [{"xT": pc["xT"]} for pc in percore],
                                core_ids=core_ids)
    amax1 = np.concatenate([res1.results[i]["amax_out"] for i in range(n_cores)],
                           axis=1)

    nc2 = build_launch(2, n_cores)
    in2 = [dict(pc, amaxv=amax1, stats1=res1.results[i]["stats1"])
           for i, pc in enumerate(percore)]
    res2 = run_bass_kernel_spmd(nc2, in2, core_ids=core_ids)
    amax3 = np.concatenate([res2.results[i]["amax_out"] for i in range(n_cores)],
                           axis=1)

    nc3 = build_launch(3, n_cores)
    in3 = [{"xT": res2.results[i]["x2T"], "WfT": pc["WfT"],
            "fbias": pc["fbias"], "amaxv": amax3,
            "stats3": res2.results[i]["stats3"]}
           for i, pc in enumerate(percore)]
    res3 = run_bass_kernel_spmd(nc3, in3, core_ids=core_ids)
    return host_finish([res3.results[i] for i in range(n_cores)])


# version 16
# speedup vs baseline: 1.0494x; 1.0052x over previous
import numpy as np
import concourse.bass as bass
import concourse.mybir as mybir
from contextlib import ExitStack

F32 = mybir.dt.float32
BF16 = mybir.dt.bfloat16
F8 = mybir.dt.float8e4
AF = mybir.ActivationFunctionType
ALU = mybir.AluOpType
AX = mybir.AxisListType

D, N, NC, NW = 1024, 2048, 8, 512
NH, NKV, HD = 16, 4, 64
EPS = 1e-5
HEAD_PAIRS = [(0, 4), (1, 5), (2, 6), (3, 7), (8, 12), (9, 13), (10, 14), (11, 15)]


class RecInstr:
    __slots__ = ("slot",)

    def __init__(self, slot):
        self.slot = slot

    def then_inc(self, sem, k):
        self.slot["inc"] = (sem, k)
        return self


class Rec:
    def __init__(self, name):
        self.name = name
        self.ops = []

    def __getattr__(self, meth):
        if meth.startswith("__"):
            raise AttributeError(meth)

        def call(*a, **k):
            slot = {}
            self.ops.append((meth, a, k, slot))
            return RecInstr(slot)
        return call


def _replay(rec):
    def f(eng):
        for meth, a, k, slot in rec.ops:
            r = getattr(eng, meth)(*a, **k)
            if "inc" in slot:
                r.then_inc(*slot["inc"])
    return f


class Prog:
    def __init__(self, nc):
        self.nc = nc
        self.es = ExitStack()
        self.counts = {}
        self.sems = {}
        self.sy, self.v, self.sc, self.t, self.g = (
            Rec("sync"), Rec("vector"), Rec("scalar"), Rec("tensor"), Rec("gpsimd"))

    def sb(self, name, shape, dt=F32):
        return self.es.enter_context(self.nc.sbuf_tensor(name, shape, dt))

    def push(self, name, shape, dt=F32):
        self._pushn = getattr(self, "_pushn", 0) + 1
        ctx = self.nc.sbuf_tensor(f"{name}_{self._pushn}", shape, dt)
        h = ctx.__enter__()
        self._stack = getattr(self, "_stack", [])
        self._stack.append((name, ctx))
        return h

    def pop(self, name):
        n, ctx = self._stack.pop()
        assert n == name, (n, name)
        ctx.__exit__(None, None, None)

    def ps(self, name, shape):
        return self.es.enter_context(self.nc.psum_tensor(name, shape, F32))

    def sem(self, name):
        s = self.es.enter_context(self.nc.semaphore(name))
        self.sems[name] = s
        self.counts[name] = 0

    def inc(self, instr, name, k=1):
        self.counts[name] += k
        instr.then_inc(self.sems[name], k)
        return self.counts[name]

    def wait(self, rec, name, val=None):
        rec.wait_ge(self.sems[name], self.counts[name] if val is None else val)

    def finish(self):
        with self.nc.Block() as block:
            block.sync(_replay(self.sy))
            block.vector(_replay(self.v))
            block.scalar(_replay(self.sc))
            block.tensor(_replay(self.t))
            block.gpsimd(_replay(self.g))
        self.es.close()


def build_launch(which, n_cores=8, l2_stop="full"):
    nc = bass.Bass("TRN2", target_bir_lowering=False)
    p = Prog(nc)
    sy, v, sc_, t, g = p.sy, p.v, p.sc, p.t, p.g

    xin = nc.dram_tensor("xT", [2, D, N], F32, kind="ExternalInput").ap()
    if which in (2, 3):
        wf = nc.dram_tensor("WfT", [D, D], F32, kind="ExternalInput").ap()
        fbias = nc.dram_tensor("fbias", [128, NC], F32, kind="ExternalInput").ap()
        amaxv = nc.dram_tensor("amaxv", [1, n_cores], F32, kind="ExternalInput").ap()
    if which == 2:
        wq = nc.dram_tensor("Wq", [D, D], F32, kind="ExternalInput").ap()
        wk = nc.dram_tensor("Wk", [D, NKV * HD], F32, kind="ExternalInput").ap()
        wv = nc.dram_tensor("Wv", [D, NKV * HD], F32, kind="ExternalInput").ap()
        wo = nc.dram_tensor("Wo", [D, D], F32, kind="ExternalInput").ap()
        bq = nc.dram_tensor("bq", [128, NC], F32, kind="ExternalInput").ap()
        bkv = nc.dram_tensor("bkv", [128, 4], F32, kind="ExternalInput").ap()
        bo = nc.dram_tensor("bo", [128, NC], F32, kind="ExternalInput").ap()
        eye = nc.dram_tensor("eye", [128, 128], F32, kind="ExternalInput").ap()
    if which in (1, 2):
        amax_out = nc.dram_tensor("amax_out", [1, 1], F32, kind="ExternalOutput").ap()
    if which == 1:
        stats1o = nc.dram_tensor("stats1", [2, 128, 2 * NW], F32,
                                 kind="ExternalOutput").ap()
    if which == 2:
        stats1i = nc.dram_tensor("stats1", [2, 128, 2 * NW], F32,
                                 kind="ExternalInput").ap()
        stats3o = nc.dram_tensor("stats3", [2, 128, 2 * NW], F32,
                                 kind="ExternalOutput").ap()
    if which == 3:
        stats3i = nc.dram_tensor("stats3", [2, 128, 2 * NW], F32,
                                 kind="ExternalInput").ap()
    if which == 2:
        xout = nc.dram_tensor("x2T", [2, D, N], F32, kind="ExternalOutput").ap()
    if which == 3:
        xout = nc.dram_tensor("xoutT", [2, D, N], F32, kind="ExternalOutput").ap()
    scr = nc.dram_tensor("scr", [1, 128], F32, kind="Internal").ap()
    scr3 = nc.dram_tensor("scr3", [3, 128], F32, kind="Internal").ap()
    if which == 2:
        xspill = nc.dram_tensor("xspill", [D, N], F32, kind="Internal").ap()

    onesf = p.sb("onesf", [128, 128], F32)
    onesb = p.sb("onesb", [128, 32], BF16)
    row128 = p.sb("row128", [1, 128], F32)
    row3 = p.sb("row3", [1, 3, 128], F32)
    srow = p.sb("srow", [1, 16], F32)
    sc0 = p.sb("sc0", [128, 8], F32)
    stat = p.sb("stat", [128, 3, NW], F32)
    amx = p.sb("amx", [128, 40], F32)
    cbias = p.sb("cbias", [128, 2], F32)
    scratch = p.sb("scratch", [128, N], F32)
    scratchB = p.sb("scratchB", [128, N], F32)

    nxbuf = 1 if which == 2 else 2
    if which != 2:
        X = [p.sb(f"X{i}", [128, NC, N], F32) for i in range(nxbuf)]

    if which in (2, 3):
        wf8 = p.sb("wf8", [128, NC, NC, 128], F8)
        fb = p.sb("fbs", [128, NC], F32)
        fbh = p.sb("fbh", [128, NC], F32)
        ybh2 = p.sb("ybh", [128, 2, 4, NW], BF16)
    if which == 2:
        vtm = p.sb("vtm", [128, 16, 256], BF16)
        eyeb = p.sb("eyeb", [128, 128], F32)
        rz4 = p.sb("rz4", [128, NW], F32)
        rzb = p.sb("rzb", [128, 2, NW], F32)
        bqs = p.sb("bqs", [128, NC], F32)
        bkvs = p.sb("bkvs", [128, 4], F32)
        bos = p.sb("bos", [128, NC], F32)
    if which == 2:
        wfs = p.push("wfs", [128, NC, NC, 128], F32)
    if which == 3:
        wfsc = p.sb("wfsc", [128, 2, NC, 128], F32)

    pbS = p.ps("pbS", [128, 4, NW])
    pbQ = p.ps("pbQ", [128, 4, NW])
    pbPV = pbQ[:, 0:2]
    pbZR = pbQ[:, 2:4]

    for s_ in ["sl", "sr", "so", "gl", "cv", "ca", "cm", "cq", "ce", "cz", "sp", "cf", "cn", "w0", "w1", "go", "sq", "x1", "x2"]:
        p.sem(s_)

    def dma(rec, out, in_, semname):
        p.counts[semname] += 16
        rec.dma_start(out=out, in_=in_).then_inc(p.sems[semname], 16)

    def roundtrip_col_to_row(col_ap):
        p.wait(sy, "cv")
        dma(sy, scr[0:1, :], col_ap, "sr")
        p.wait(sy, "sr")
        dma(sy, row128[0:1, :], scr[0:1, :], "sr")

    def row_to_allparts(src11, dst_col):
        p.wait(v, "sr")
        v.tensor_scalar(row128[0:1, :], onesf[0:1, 0:128], src11, None, ALU.mult)
        p.inc(v.drain(), "cv")
        p.wait(sy, "cv")
        dma(sy, scr[0:1, :], row128[0:1, :], "sr")
        p.wait(sy, "sr")
        dma(sy, dst_col, scr[0:1, :], "sr")

    def stats_pass(Xs):
        cm0 = p.counts["cm"]
        cv0 = p.counts["cv"]
        bufs = [scratch, scratchB]
        for c in range(NC):
            if c >= 2:
                p.wait(v, "cm", cm0 + c - 1)
            v.tensor_mul(bufs[c % 2][:, 0:N], Xs[:, c, :], Xs[:, c, :])
            p.inc(v.drain(), "cv")
            p.wait(t, "cv", cv0 + c + 1)
            for w in range(4):
                t.matmul(pbS[32 * w:32 * w + 32, 0, :], onesf[:, 0:32],
                         Xs[:, c, w * NW:(w + 1) * NW], start=(c == 0),
                         stop=(c == NC - 1), tile_position=(0, 32 * w),
                         skip_group_check=True)
            for w in range(4):
                i = t.matmul(pbS[32 * w:32 * w + 32, 1, :], onesf[:, 0:32],
                             bufs[c % 2][:, w * NW:(w + 1) * NW], start=(c == 0),
                             stop=(c == NC - 1), tile_position=(0, 32 * w),
                             skip_group_check=True)
            p.inc(i, "cm")
        p.wait(v, "cm")
        v.tensor_scalar(stat[:, 0, :], pbS[:, 0, :], 1.0 / D, None, ALU.mult)
        v.drain()
        v.tensor_mul(scratch[:, 0:NW], stat[:, 0, :], stat[:, 0, :])
        v.drain()
        v.tensor_scalar(stat[:, 2, :], pbS[:, 1, :], 1.0 / D, None, ALU.mult)
        v.drain()
        v.tensor_sub(stat[:, 2, :], stat[:, 2, :], scratch[:, 0:NW])
        p.inc(v.drain(), "cv")
        p.wait(sc_, "cv")
        sc_.activation(stat[:, 2, :], stat[:, 2, :], AF.Ln, bias=cbias[:, 0:1])
        sc_.drain()
        p.inc(sc_.activation(stat[:, 1, :], stat[:, 2, :], AF.Exp, scale=-0.5), "ca")
        p.wait(v, "ca")
        p.wait(t, "ca")

    def apply_pass(Xs, dst_fn, row1, absmax_base=None):
        for w in range(4):
            t.matmul(pbS[:, w, :], onesf[32 * w:32 * w + 1, :],
                     stat[32 * w:32 * w + 1, 0, :], start=True, stop=True,
                     tile_position=(32 * w, 0), skip_group_check=True)
            i = t.matmul(pbQ[:, w, :], onesf[32 * w:32 * w + 1, :],
                         stat[32 * w:32 * w + 1, row1, :], start=True, stop=True,
                         tile_position=(32 * w, 0), skip_group_check=True)
        p.inc(i, "cm")
        p.wait(v, "cm")
        chunk_cv = []
        bufs = [scratch, scratchB]
        for c in range(NC):
            b_ = bufs[c % 2]
            if c >= 2:
                v.drain()
            v.tensor_sub(b_[:, 0:N], Xs[:, c, :],
                         pbS[:, :, :].rearrange("p a b -> p (a b)"))
            v.drain()
            if absmax_base is None:
                i2 = v.tensor_mul(dst_fn(c), b_[:, 0:N],
                                  pbQ[:, :, :].rearrange("p a b -> p (a b)"))
                chunk_cv.append(p.inc(i2, "cv"))
            else:
                v.tensor_mul(b_[:, 0:N], b_[:, 0:N],
                             pbQ[:, :, :].rearrange("p a b -> p (a b)"))
                v.drain()
                k = absmax_base + c
                v.tensor_reduce(amx[:, k:k + 1], b_[:, 0:N], AX.X,
                                ALU.max, apply_absolute_value=True)
        if absmax_base is not None:
            p.inc(v.nop(), "cv")
        p.wait(t, "cv")
        return chunk_cv

    def ffn_matmuls(xq_s, Xs):
        oc_cv = []
        banksA = [(pbS, w) for w in range(4)]
        banksB = [(pbQ, w) for w in range(4)]
        for oc in range(NC):
            bs = banksA if oc % 2 == 0 else banksB
            if oc >= 2:
                p.wait(t, "cv")
            for kc in range(NC):
                for w in range(4):
                    i = t.matmul(bs[w][0][:, bs[w][1], :], wf8[:, kc, oc, :],
                                 xq_s[:, kc, w * NW:(w + 1) * NW],
                                 start=(kc == 0), stop=(kc == NC - 1))
            p.inc(i, "cm")
            p.wait(sc_, "cm")
            if oc >= 2:
                p.wait(sc_, "cv")
            i2 = sc_.activation(ybh2[:, oc % 2, :, :], bs[0][0][:, :, :], AF.Identity,
                                bias=fbh[:, oc:oc + 1], scale=sc0[:, 1:2])
            p.inc(i2, "ca")
            p.wait(v, "ca")
            v.tensor_add(Xs[:, oc, :], Xs[:, oc, :],
                         ybh2[:, oc % 2, :, :].rearrange("p a b -> p (a b)"))
            v.drain()
            oc_cv.append(p.inc(v.nop(), "cv"))
        p.wait(t, "cv")
        return oc_cv

    if which == 2:
        dma(sy, eyeb[:], eye[:, :], "sl")
    if which == 3:
        dma(sy, fb[:, :], fbias[:, :], "sp")
        dma(sy, amx[0:1, 24:24 + n_cores], amaxv[0:1, :], "sp")
    if which != 2:
        xsem = [("sl", "gl"), ("x1", "x2")]
        for i_ in range(nxbuf):
            for c in range(NC):
                rec_ = sy if (i_ * NC + c) % 2 == 0 else g
                dma(rec_, X[i_][:, c, :], xin[i_, c * 128:(c + 1) * 128, :],
                    xsem[i_][0] if rec_ is sy else xsem[i_][1])
    if which == 2:
        dma(sy, fb[:, :], fbias[:, :], "sl")
        for c in range(NC):
            dma(sy, wfs[:, c, :, :].rearrange("p a b -> p (a b)"),
                wf[c * 128:(c + 1) * 128, :], "sl")
        dma(sy, amx[0:1, 24:24 + n_cores], amaxv[0:1, :], "sl")
        dma(sy, bqs[:, :], bq[:, :], "sl")
        dma(sy, bkvs[:, :], bkv[:, :], "sl")
        dma(sy, bos[:, :], bo[:, :], "sl")

    if which == 2:
        p.wait(v, "sl")
    v.memset(onesf[:], 1.0)
    v.memset(onesb[:], 1.0)
    v.memset(cbias[:, 0:1], EPS)
    p.inc(v.drain(), "cv")
    p.wait(t, "cv")
    p.wait(sc_, "cv")
    p.wait(sy, "cv")

    if which in (2, 3):
        if which == 3:
            p.wait(v, "sp")
        v.tensor_scalar(fbh[:, :], fb[:, :], 0.5, None, ALU.mult)
        v.drain()
        if which == 2:
            v.tensor_reduce(amx[:, 17:18],
                            wfs[:, :, :, :].rearrange("p a b c -> p (a b c)"),
                            AX.X, ALU.max, apply_absolute_value=True)
            p.inc(v.drain(), "cv")
        else:
            cv0w = p.counts["cv"]
            for c in range(NC):
                if c >= 2:
                    p.wait(sc_, "cv", cv0w + c - 1)
                sem_c = "w0" if c % 2 == 0 else "w1"
                dma(sc_, wfsc[:, c % 2, :, :].rearrange("p a b -> p (a b)"),
                    wf[c * 128:(c + 1) * 128, :], sem_c)
                p.wait(v, sem_c)
                v.tensor_reduce(amx[:, 8 + c:9 + c],
                                wfsc[:, c % 2, :, :].rearrange("p a b -> p (a b)"),
                                AX.X, ALU.max, apply_absolute_value=True)
                p.inc(v.drain(), "cv")
            v.tensor_reduce(amx[:, 17:18], amx[:, 8:16], AX.X, ALU.max)
            p.inc(v.drain(), "cv")
        p.wait(sy, "cv")
        roundtrip_col_to_row(amx[:, 17:18])
        p.wait(v, "sr")
        v.tensor_reduce(srow[0:1, 0:1], row128[0:1, :], AX.X, ALU.max)
        v.drain()
        v.tensor_scalar(srow[0:1, 1:2], srow[0:1, 0:1], 1e-12, None, ALU.max)
        v.drain()
        v.reciprocal(srow[0:1, 2:3], srow[0:1, 1:2])
        v.drain()
        v.tensor_scalar(srow[0:1, 3:4], srow[0:1, 2:3], 448.0, 448.0, ALU.mult, ALU.min)
        v.drain()
        v.reciprocal(srow[0:1, 4:5], srow[0:1, 3:4])
        v.drain()
        v.tensor_scalar(srow[0:1, 5:6], srow[0:1, 3:4], 0.5, None, ALU.mult)
        v.drain()
        v.tensor_reduce(srow[0:1, 6:7], amx[0:1, 24:24 + n_cores], AX.X, ALU.max)
        v.drain()
        v.tensor_scalar(srow[0:1, 7:8], srow[0:1, 6:7], 1e-12, None, ALU.max)
        v.drain()
        v.reciprocal(srow[0:1, 8:9], srow[0:1, 7:8])
        v.drain()
        v.tensor_scalar(srow[0:1, 9:10], srow[0:1, 8:9], 448.0, 448.0, ALU.mult, ALU.min)
        v.drain()
        v.reciprocal(srow[0:1, 10:11], srow[0:1, 9:10])
        v.drain()
        v.tensor_scalar(srow[0:1, 11:12], srow[0:1, 9:10], 0.5, None, ALU.mult)
        v.drain()
        v.tensor_mul(srow[0:1, 12:13], srow[0:1, 10:11], srow[0:1, 4:5])
        v.drain()
        v.tensor_scalar(srow[0:1, 13:14], srow[0:1, 12:13], 2.0, None, ALU.mult)
        v.drain()
        p.wait(v, "sr")
        v.tensor_scalar(row3[0:1, 0, :], onesf[0:1, 0:128], srow[0:1, 5:6], None, ALU.mult)
        v.tensor_scalar(row3[0:1, 1, :], onesf[0:1, 0:128], srow[0:1, 11:12], None, ALU.mult)
        v.tensor_scalar(row3[0:1, 2, :], onesf[0:1, 0:128], srow[0:1, 13:14], None, ALU.mult)
        p.inc(v.drain(), "cv")
        p.wait(sy, "cv")
        dma(sy, scr3[:, :], row3[0:1, :, :], "sr")
        p.wait(sy, "sr")
        dma(sy, sc0[:, 4:5], scr3[0:1, :], "sr")
        dma(sy, sc0[:, 0:1], scr3[1:2, :], "sr")
        dma(sy, sc0[:, 1:2], scr3[2:3, :], "sr")
        p.wait(v, "sr")
        if which == 2:
            v.tensor_scalar(wf8[:, :, :, :].rearrange("p a b c -> p (a b c)"),
                            wfs[:, :, :, :].rearrange("p a b c -> p (a b c)"),
                            sc0[:, 4:5], None, ALU.mult)
            p.inc(v.drain(), "cq")
            p.wait(t, "cq")
            p.pop("wfs")
        else:
            p.wait(sc_, "cv")
            cq0w = p.counts["cq"]
            for c in range(NC):
                if c >= 2:
                    p.wait(sc_, "cq", cq0w + c - 1)
                sem_c = "w0" if c % 2 == 0 else "w1"
                dma(sc_, wfsc[:, c % 2, :, :].rearrange("p a b -> p (a b)"),
                    wf[c * 128:(c + 1) * 128, :], sem_c)
                p.wait(v, sem_c)
                v.tensor_scalar(wf8[:, c, :, :].rearrange("p a b -> p (a b)"),
                                wfsc[:, c % 2, :, :].rearrange("p a b -> p (a b)"),
                                sc0[:, 4:5], None, ALU.mult)
                p.inc(v.drain(), "cq")
            p.wait(t, "cq")

    if which == 1:
        for s in range(2):
            if s == 1:
                p.wait(v, "so")
                p.wait(sc_, "so")
            p.wait(v, xsem[s][0])
            p.wait(v, xsem[s][1])
            p.wait(t, xsem[s][0])
            p.wait(t, xsem[s][1])
            stats_pass(X[s])
            p.wait(sy, "ca")
            dma(sy, stats1o[s, :, :],
                stat[:, 0:2, :].rearrange("p a b -> p (a b)"), "so")
            apply_pass(X[s], None, 1, absmax_base=0)
            v.drain()
            v.tensor_reduce(amx[:, 20 + s:21 + s], amx[:, 0:8], AX.X, ALU.max)
            p.inc(v.drain(), "cv")
        v.tensor_reduce(amx[:, 22:23], amx[:, 20:22], AX.X, ALU.max)
        p.inc(v.drain(), "cv")
        p.wait(sy, "cv")
        roundtrip_col_to_row(amx[:, 22:23])
        p.wait(v, "sr")
        v.tensor_reduce(srow[0:1, 0:1], row128[0:1, :], AX.X, ALU.max)
        p.inc(v.drain(), "cv")
        p.wait(sy, "cv")
        dma(sy, amax_out[0:1, :], srow[0:1, 0:1], "so")
        p.wait(g, "so")
        g.nop()
        p.finish()
        return nc

    if which == 3:
        for s in range(2):
            xq_s = p.push("xq", [128, NC, N], F8)
            if p.counts["cm"]:
                p.wait(sc_, "cm")
            dma(sc_, stat[:, 0:2, :].rearrange("p a b -> p (a b)"),
                stats3i[s, :, :], "sq")
            p.wait(v, "sq")
            v.tensor_scalar(stat[:, 2, :], stat[:, 1, :], sc0[:, 0:1], None, ALU.mult)
            p.inc(v.drain(), "ca")
            p.wait(t, "ca")
            p.wait(t, "sq")
            p.wait(v, xsem[s][0])
            p.wait(v, xsem[s][1])
            apply_pass(X[s], lambda c, xq_s=xq_s: xq_s[:, c, :], 2)
            oc_cv = ffn_matmuls(xq_s, X[s])
            p.pop("xq")
            for c in range(NC):
                rec_ = sy if c % 2 == 0 else g
                rec_.wait_ge(p.sems["cv"], oc_cv[c])
                dma(rec_, xout[s, c * 128:(c + 1) * 128, :], X[s][:, c, :],
                    "so" if c % 2 == 0 else "go")
        p.wait(g, "so")
        p.wait(g, "go")
        g.nop()
        p.finish()
        return nc

    for s in range(2):
        qT = p.push("qT", [128, NC, N], BF16)
        kT = p.push("kT", [128, 2, N], BF16)
        xnb = p.push("xnb", [128, NC, N], BF16)
        Xs = p.push("X", [128, NC, N], F32)
        if s == 1:
            p.wait(sy, "so")
            p.wait(sy, "go")
            p.wait(sy, "cv")
            p.wait(g, "so")
            p.wait(g, "go")
            p.wait(g, "cv")
        for c in range(NC):
            rec_ = sy if c % 2 == 0 else g
            dma(rec_, Xs[:, c, :], xin[s, c * 128:(c + 1) * 128, :],
                "sl" if rec_ is sy else "gl")
        if p.counts["cm"]:
            p.wait(sy, "cm")
        dma(sy, stat[:, 0:2, :].rearrange("p a b -> p (a b)"),
            stats1i[s, :, :], "sl")
        p.wait(v, "sl")
        p.wait(t, "sl")
        p.wait(v, "gl")
        p.wait(t, "gl")
        if s == 1:
            p.wait(v, "so")
            p.wait(t, "so")
            p.wait(v, "go")
            p.wait(t, "go")
        xq_s = p.push("xq", [128, NC, N], F8)
        v.tensor_scalar(stat[:, 2, :], stat[:, 1, :], sc0[:, 0:1], None, ALU.mult)
        p.inc(v.drain(), "ca")
        p.wait(t, "ca")
        apply_pass(Xs, lambda c: xq_s[:, c, :], 2)
        if l2_stop != "ln1":
            ffn_matmuls(xq_s, Xs)
        p.pop("xq")
        if l2_stop in ("ln1", "ffn1"):
            p.wait(sy, "cv")
            for c in range(NC):
                dma(sy, xout[s, c * 128:(c + 1) * 128, :], Xs[:, c, :], "so")
            p.pop("X")
            p.pop("xnb")
            p.pop("kT")
            p.pop("qT")
            continue
        stats_pass(Xs)
        ch_cv = apply_pass(Xs, lambda c: xnb[:, c, :], 1)
        for c in range(NC):
            rec_ = sy if c % 2 == 0 else sc_
            rec_.wait_ge(p.sems["cv"], ch_cv[c])
            dma(rec_, xspill[c * 128:(c + 1) * 128, :], Xs[:, c, :], "sp")
        p.pop("X")
        wbuf = p.push("wbuf", [128, NC, NC, 128], BF16)
        wkvb = p.push("wkvb", [128, NC, 4, 128], BF16)
        p.wait(g, "sp")
        if s == 1:
            p.wait(g, "cm")
        else:
            p.wait(g, "cq")
        for kc in range(NC):
            for oc in range(NC):
                dma(g, wbuf[:, kc, oc, :],
                    wq[kc * 128:(kc + 1) * 128, oc * 128:(oc + 1) * 128], "gl")
        for kc in range(NC):
            dma(g, wkvb[:, kc, 0:2, :].rearrange("p a b -> p (a b)"),
                wk[kc * 128:(kc + 1) * 128, :], "gl")
            dma(g, wkvb[:, kc, 2:4, :].rearrange("p a b -> p (a b)"),
                wv[kc * 128:(kc + 1) * 128, :], "gl")
        p.wait(t, "gl")

        def proj(n_oc, wsl, dstT, bias_fn, oc_base=0):
            banksA = [(pbS, w) for w in range(4)]
            banksB = [(pbQ, w) for w in range(4)]
            for oc in range(n_oc):
                bs = banksA if oc % 2 == 0 else banksB
                if oc >= 2:
                    p.wait(t, "cv")
                for kc in range(NC):
                    for w in range(4):
                        i = t.matmul(bs[w][0][:, bs[w][1], :], wsl(kc, oc),
                                     xnb[:, kc, w * NW:(w + 1) * NW],
                                     start=(kc == 0), stop=(kc == NC - 1))
                p.inc(i, "cm")
                p.wait(sc_, "cm")
                i2 = sc_.activation(dstT[:, oc_base + oc, :], bs[0][0][:, :, :],
                                    AF.Identity, bias=bias_fn(oc))
                p.inc(i2, "cv")
            p.wait(t, "cv")

        proj(NC, lambda kc, oc: wbuf[:, kc, oc, :], qT, lambda oc: bqs[:, oc:oc + 1])
        proj(2, lambda kc, oc: wkvb[:, kc, oc, :], kT, lambda oc: bkvs[:, oc:oc + 1])
        vT = p.push("vT", [128, 2, N], F32)
        proj(2, lambda kc, oc: wkvb[:, kc, 2 + oc, :], vT,
             lambda oc: bkvs[:, 2 + oc:3 + oc])

        for kvc in range(2):
            for scn in range(16):
                bk_ = scn % 2
                if kvc * 16 + scn >= 2:
                    p.wait(t, "cv")
                i = t.transpose(pbZR[0:128, bk_, 0:128],
                                vT[:, kvc, scn * 128:(scn + 1) * 128], eyeb[:, :])
                p.inc(i, "cm")
                p.wait(v, "cm")
                v.tensor_copy(vtm[:, scn, kvc * 128:(kvc + 1) * 128],
                              pbZR[:, bk_, 0:128])
                v.drain()
                p.inc(v.nop(), "cv")
        p.wait(t, "cv")
        p.pop("vT")

        for c in range(NC):
            if c > 0:
                p.wait(v, "cm")
            v.tensor_mul(scratch[:, 0:N], qT[:, c, :], qT[:, c, :])
            p.inc(v.drain(), "cv")
            p.wait(t, "cv")
            slot = c % 4
            tgt = [(pbS, w) for w in range(4)]
            for w in range(4):
                i = t.matmul(tgt[w][0][32 * slot:32 * slot + 32, tgt[w][1], :],
                             onesf[:, 0:32], scratch[:, w * NW:(w + 1) * NW],
                             start=(c < 4), stop=(c >= 4),
                             tile_position=(0, 32 * slot), skip_group_check=True)
            p.inc(i, "cm")
        p.wait(v, "cm")
        v.tensor_reduce(amx[:, 0:1], pbS[:, :, :].rearrange("p a b -> p (a b)"),
                        AX.X, ALU.max)
        p.inc(v.drain(), "cv")
        p.wait(t, "cv")
        for c in range(2):
            if c > 0:
                p.wait(v, "cm")
            v.tensor_mul(scratch[:, 0:N], kT[:, c, :], kT[:, c, :])
            p.inc(v.drain(), "cv")
            p.wait(t, "cv")
            for w in range(4):
                i = t.matmul(pbPV[32 * (c * 2 + w // 2):32 * (c * 2 + w // 2) + 32,
                                  w % 2, :],
                             onesf[:, 0:32], scratch[:, w * NW:(w + 1) * NW],
                             start=True, stop=True,
                             tile_position=(0, 32 * (c * 2 + w // 2)),
                             skip_group_check=True)
            p.inc(i, "cm")
        p.wait(v, "cm")
        v.tensor_reduce(amx[:, 1:2], pbPV[:, :, :].rearrange("p a b -> p (a b)"),
                        AX.X, ALU.max)
        p.inc(v.drain(), "cv")
        p.wait(sy, "cv")
        roundtrip_col_to_row(amx[:, 0:1])
        p.wait(v, "sr")
        v.tensor_reduce(srow[0:1, 14:15], row128[0:1, :], AX.X, ALU.max)
        p.inc(v.drain(), "cv")
        p.wait(sy, "cv")
        roundtrip_col_to_row(amx[:, 1:2])
        p.wait(v, "sr")
        v.tensor_reduce(srow[0:1, 15:16], row128[0:1, :], AX.X, ALU.max)
        v.drain()
        v.tensor_mul(srow[0:1, 14:15], srow[0:1, 14:15], srow[0:1, 15:16])
        p.inc(v.drain(), "cv")
        p.wait(sc_, "cv")
        sc_.activation(srow[0:1, 14:15], srow[0:1, 14:15], AF.Ln)
        sc_.drain()
        p.inc(sc_.activation(srow[0:1, 14:15], srow[0:1, 14:15], AF.Exp, scale=0.5),
              "ca")
        p.wait(v, "ca")
        v.tensor_scalar(srow[0:1, 14:15], srow[0:1, 14:15], -0.125, None, ALU.mult)
        p.inc(v.drain(), "cv")
        row_to_allparts(srow[0:1, 14:15], sc0[:, 2 + s:3 + s])
        p.wait(sc_, "sr")
        p.wait(t, "sr")

        p.pop("wkvb")
        p.pop("wbuf")
        p.pop("xnb")
        aoT = p.push("aoT", [128, NC, N], BF16)
        expb = p.push("expb", [128, 4, 16, NW], BF16)
        p.wait(sc_, "sp")
        p.wait(v, "sp")
        p.wait(t, "sp")

        for b in range(4):
            qc0, qc1 = 2 * b, 2 * b + 1
            ktc = qc0 // 4
            kvs = [HEAD_PAIRS[qc0][0] // 4, HEAD_PAIRS[qc0][1] // 4,
                   HEAD_PAIRS[qc1][0] // 4, HEAD_PAIRS[qc1][1] // 4]
            for w in range(4):
                nwin = slice(w * NW, (w + 1) * NW)
                ce0 = p.counts["ce"]
                cf0 = p.counts["cf"]

                def qk_group(scn, js, sem, base):
                    s0 = scn * 128
                    if scn >= 1:
                        p.wait(t, sem, base + scn)
                    for j, qc in js:
                        lohalf = (j % 2) * 64
                        i = t.matmul(pbS[0:128, j, :],
                                     kT[lohalf:lohalf + 64, ktc, s0:s0 + 128],
                                     qT[lohalf:lohalf + 64, qc, nwin],
                                     start=True, stop=True,
                                     tile_position=(lohalf, 0),
                                     skip_group_check=True)
                    p.inc(i, "cm")

                def exp_group(scn, j0, sem):
                    p.wait(sc_, "cm")
                    i = sc_.activation(expb[:, j0:j0 + 2, scn, :],
                                       pbS[:, j0:j0 + 2, :], AF.Exp,
                                       bias=sc0[:, 2 + s:3 + s], scale=0.125)
                    p.inc(i, sem)

                def pv_z_group(scn, jp, last):
                    for j, pvb in jp:
                        col = (j % 2) * 64
                        i = t.matmul(pbQ[col:col + 64, pvb, :],
                                     vtm[:, scn, kvs[j] * 64:kvs[j] * 64 + 64],
                                     expb[:, j, scn, :], start=(scn == 0),
                                     stop=(scn == 15),
                                     tile_position=(0, col), skip_group_check=True)
                    for j, _ in jp:
                        i2 = t.matmul(pbQ[32 * j:32 * j + 32, 2, :], onesb[:, :],
                                      expb[:, j, scn, :], start=(scn == 0),
                                      stop=(scn == 15),
                                      tile_position=(0, 32 * j),
                                      skip_group_check=True)
                    if last:
                        p.inc(i, "cm")
                        p.inc(i2, "cz")

                for scn in range(16):
                    qk_group(scn, ((0, qc0), (1, qc0)), "ce", ce0)
                    exp_group(scn, 0, "ce")
                    qk_group(scn, ((2, qc1), (3, qc1)), "cf", cf0)
                    exp_group(scn, 2, "cf")
                    if scn >= 1:
                        pv_z_group(scn - 1, ((0, 0), (1, 0)), False)
                        pv_z_group(scn - 1, ((2, 1), (3, 1)), False)
                p.wait(t, "ce", ce0 + 16)
                pv_z_group(15, ((0, 0), (1, 0)), False)
                p.wait(t, "cf", cf0 + 16)
                pv_z_group(15, ((2, 1), (3, 1)), True)
                p.wait(v, "cz")
                v.reciprocal(rz4[:, :], pbZR[:, 0, :])
                p.inc(v.drain(), "cz")
                p.wait(t, "cz")
                t.matmul(pbZR[0:64, 1, :], onesf[0:1, 0:64], rz4[0:1, :],
                         start=True, stop=True, tile_position=(0, 0),
                         skip_group_check=True)
                t.matmul(pbZR[64:128, 1, :], onesf[32:33, 0:64], rz4[32:33, :],
                         start=True, stop=True, tile_position=(32, 64),
                         skip_group_check=True)
                t.matmul(pbZR[0:64, 0, :], onesf[64:65, 0:64], rz4[64:65, :],
                         start=True, stop=True, tile_position=(64, 0),
                         skip_group_check=True)
                i = t.matmul(pbZR[64:128, 0, :], onesf[96:97, 0:64], rz4[96:97, :],
                             start=True, stop=True, tile_position=(96, 64),
                             skip_group_check=True)
                p.inc(i, "cm")
                p.wait(v, "cm")
                v.tensor_copy(rzb[:, 0, :], pbZR[:, 1, :])
                v.drain()
                v.tensor_copy(rzb[:, 1, :], pbZR[:, 0, :])
                v.drain()
                v.tensor_mul(aoT[:, qc0, nwin], pbPV[:, 0, :], rzb[:, 0, :])
                v.drain()
                v.tensor_mul(aoT[:, qc1, nwin], pbPV[:, 1, :], rzb[:, 1, :])
                v.drain()
                p.inc(v.nop(), "ce")

        p.pop("expb")
        Xs = p.push("X", [128, NC, N], F32)
        p.wait(sy, "ce")
        for c in range(NC):
            dma(sy, Xs[:, c, :], xspill[c * 128:(c + 1) * 128, :], "sl")
        wbuf = p.push("wbuf", [128, NC, NC, 128], BF16)
        p.wait(g, "ce")
        for kc in range(NC):
            for oc in range(NC):
                dma(g, wbuf[:, kc, oc, :],
                    wo[kc * 128:(kc + 1) * 128, oc * 128:(oc + 1) * 128], "gl")
        p.wait(t, "gl")
        banksA = [(pbS, w) for w in range(4)]
        banksB = [(pbQ, w) for w in range(4)]
        oc_cv2 = []
        for oc in range(NC):
            bs = banksA if oc % 2 == 0 else banksB
            if oc >= 2:
                p.wait(t, "cv")
            for kc in range(NC):
                for w in range(4):
                    i = t.matmul(bs[w][0][:, bs[w][1], :], wbuf[:, kc, oc, :],
                                 aoT[:, kc, w * NW:(w + 1) * NW],
                                 start=(kc == 0), stop=(kc == NC - 1))
            p.inc(i, "cm")
            p.wait(v, "cm")
            if oc == 0:
                p.wait(v, "sl")
            v.scalar_tensor_tensor(Xs[:, oc, :], bs[0][0][:, :, :],
                                   bos[:, oc:oc + 1], Xs[:, oc, :],
                                   ALU.add, ALU.add)
            v.drain()
            oc_cv2.append(p.inc(v.nop(), "cv"))
        p.wait(t, "cv")
        p.pop("wbuf")

        for c in range(NC):
            rec_ = sy if c % 2 == 0 else g
            rec_.wait_ge(p.sems["cv"], oc_cv2[c])
            dma(rec_, xout[s, c * 128:(c + 1) * 128, :], Xs[:, c, :],
                "so" if c % 2 == 0 else "go")
        stats_pass(Xs)
        p.wait(sy, "ca")
        dma(sy, stats3o[s, :, :],
            stat[:, 0:2, :].rearrange("p a b -> p (a b)"), "so")
        apply_pass(Xs, None, 1, absmax_base=0)
        v.drain()
        v.tensor_reduce(amx[:, 20 + s:21 + s], amx[:, 0:8], AX.X, ALU.max)
        p.inc(v.drain(), "cv")
        p.pop("X")
        p.pop("aoT")
        p.pop("kT")
        p.pop("qT")

    if l2_stop == "full":
        v.tensor_reduce(amx[:, 22:23], amx[:, 20:22], AX.X, ALU.max)
        p.inc(v.drain(), "cv")
        p.wait(sy, "cv")
        roundtrip_col_to_row(amx[:, 22:23])
        p.wait(v, "sr")
        v.tensor_reduce(srow[0:1, 0:1], row128[0:1, :], AX.X, ALU.max)
        p.inc(v.drain(), "cv")
        p.wait(sy, "cv")
        dma(sy, amax_out[0:1, :], srow[0:1, 0:1], "so")
    p.wait(g, "so")
    p.wait(g, "go")
    g.nop()
    p.finish()
    return nc


def host_prepare(inputs, n_cores=8):
    import ml_dtypes
    x = np.asarray(inputs["L0_attn_x"])
    Wq = np.asarray(inputs["L0_attn_Wq"])
    bq = np.asarray(inputs["L0_attn_bq"])
    Wk = np.asarray(inputs["L0_attn_Wk"])
    bk = np.asarray(inputs["L0_attn_bk"])
    Wv = np.asarray(inputs["L0_attn_Wv"])
    bv = np.asarray(inputs["L0_attn_bv"])
    Wo = np.asarray(inputs["L0_attn_Wo"])
    bo = np.asarray(inputs["L0_attn_bo"])
    Wf = np.asarray(inputs["L0_ffn_weight"])
    bf_ = np.asarray(inputs["L0_ffn_bias"])

    headperm = [h for pr in HEAD_PAIRS for h in pr]
    colperm = np.concatenate([np.arange(h * HD, (h + 1) * HD) for h in headperm])
    Wq_p = np.ascontiguousarray(Wq[:, colperm])
    bq_p = np.ascontiguousarray(bq[colperm]).reshape(NC, 128).T.copy()
    myrows = []
    for h in headperm:
        g, kv = h % 4, h // 4
        base = (g * 4 + kv) * HD
        myrows.extend(range(base, base + HD))
    Wo_p = np.ascontiguousarray(Wo[myrows, :])
    bkv_host = np.stack([bk.reshape(2, 128).T, bv.reshape(2, 128).T],
                        axis=1).reshape(128, 4)
    bkv_host = np.concatenate([bk.reshape(2, 128).T, bv.reshape(2, 128).T], axis=1)
    bo_host = bo.reshape(NC, 128).T.copy()
    fb_host = bf_.reshape(NC, 128).T.copy()
    WfT = np.ascontiguousarray(Wf.T)
    eye = np.eye(128, dtype=np.float32)

    per_core = []
    for c in range(n_cores):
        xc = x[2 * c:2 * c + 2]
        xT = np.ascontiguousarray(xc.transpose(0, 2, 1))
        per_core.append({
            "xT": xT, "Wq": Wq_p, "Wk": Wk, "Wv": Wv, "Wo": Wo_p,
            "bq": np.ascontiguousarray(bq_p), "bkv": np.ascontiguousarray(bkv_host),
            "bo": np.ascontiguousarray(bo_host), "WfT": WfT,
            "fbias": np.ascontiguousarray(fb_host), "eye": eye,
        })
    return per_core


def host_finish(results):
    outs = []
    for r in results:
        xT = r["xoutT"]
        outs.append(np.ascontiguousarray(xT.transpose(0, 2, 1)))
    return np.concatenate(outs, axis=0).astype(np.float32)

from concourse.bass_utils import run_bass_kernel_spmd

def kernel(**inputs):
    n_cores = 8
    percore = host_prepare(inputs, n_cores)
    core_ids = list(range(n_cores))

    nc1 = build_launch(1, n_cores)
    res1 = run_bass_kernel_spmd(nc1, [{"xT": pc["xT"]} for pc in percore],
                                core_ids=core_ids)
    amax1 = np.concatenate([res1.results[i]["amax_out"] for i in range(n_cores)],
                           axis=1)

    nc2 = build_launch(2, n_cores)
    in2 = [dict(pc, amaxv=amax1, stats1=res1.results[i]["stats1"])
           for i, pc in enumerate(percore)]
    res2 = run_bass_kernel_spmd(nc2, in2, core_ids=core_ids)
    amax3 = np.concatenate([res2.results[i]["amax_out"] for i in range(n_cores)],
                           axis=1)

    nc3 = build_launch(3, n_cores)
    in3 = [{"xT": res2.results[i]["x2T"], "WfT": pc["WfT"],
            "fbias": pc["fbias"], "amaxv": amax3,
            "stats3": res2.results[i]["stats3"]}
           for i, pc in enumerate(percore)]
    res3 = run_bass_kernel_spmd(nc3, in3, core_ids=core_ids)
    return host_finish([res3.results[i] for i in range(n_cores)])
